# version 11
# baseline (speedup 1.0000x reference)
import contextlib
import numpy as np
import ml_dtypes
import concourse.bass as bass
import concourse.mybir as mybir
from concourse.bass_utils import run_bass_kernel_spmd

F32 = mybir.dt.float32
BF16 = mybir.dt.bfloat16
AF = mybir.ActivationFunctionType
ALU = mybir.AluOpType
AX = mybir.AxisListType

ENGINES = ("sync", "vector", "scalar", "tensor", "gpsimd")

FULL_CFG = dict(B=4, S=2048, E=1024, I=2816, NCORES=8)


class Tok:
    __slots__ = ("ch", "val", "eng", "dma")

    def __init__(self, ch, val, eng, dma):
        self.ch, self.val, self.eng, self.dma = ch, val, eng, dma


class Prog:

    def __init__(self, nc):
        self.nc = nc
        self.streams = {e: [] for e in ENGINES}
        self.chan = {}
        self.water = {}

    def channel(self, name, sem):
        self.chan[name] = [sem, 0]

    CHAINED = ("vector", "scalar", "gpsimd")

    def emit(self, eng, fn, waits=(), inc=None, inc_k=1, sw=True):
        wl = []
        if sw and eng in self.CHAINED and inc is not None:
            cur = self.chan[inc][1]
            key = (eng, inc)
            if cur > 0 and self.water.get(key, -1) < cur:
                self.water[key] = cur
                wl.append((inc, cur))
        for t in waits:
            if t is None:
                continue
            if (not t.dma) and t.eng == eng:
                continue
            key = (eng, t.ch)
            if self.water.get(key, -1) >= t.val:
                continue
            self.water[key] = t.val
            wl.append((t.ch, t.val))
        tok = None
        if inc is not None:
            c = self.chan[inc]
            c[1] += inc_k
            tok = Tok(inc, c[1], eng, inc_k >= 16)
        self.streams[eng].append((wl, fn, inc, inc_k))
        return tok

    def wait_only(self, eng, waits):
        self.emit(eng, None, waits=waits)

    def run(self, engobj, eng):
        for wl, fn, inc, inc_k in self.streams[eng]:
            for ch, val in wl:
                engobj.wait_ge(self.chan[ch][0], val)
            if fn is None:
                continue
            ins = fn(engobj)
            if inc is not None:
                ins.then_inc(self.chan[inc][0], inc_k)


def bcast16(ap2d, nblk, blk=16):
    return bass.AP(ap2d.tensor, ap2d.offset, [list(ap2d.ap[0]), [1, nblk], [0, blk]])


def build(cfg, debug=False, interleave_w=True):
    B, S, E, I = cfg["B"], cfg["S"], cfg["E"], cfg["I"]
    NE, NS, NI = E // 128, S // 128, I // 128
    NSG = S // 512
    NHF = E // 512
    SCALE = float(E) ** -0.5
    E6 = 6 * E
    assert S % 512 == 0 and E % 512 == 0 and I % 128 == 0

    nc = bass.Bass("TRN2", target_bir_lowering=False, detect_race_conditions=False)

    x_in = nc.dram_tensor("x_in", [B, S, E], F32, kind="ExternalInput")
    wq_in = nc.dram_tensor("wq_in", [E, E], F32, kind="ExternalInput")
    wk_in = nc.dram_tensor("wk_in", [E, E], F32, kind="ExternalInput")
    wv_in = nc.dram_tensor("wv_in", [E, E], F32, kind="ExternalInput")
    wg_in = nc.dram_tensor("wg_in", [I, E], F32, kind="ExternalInput")
    wu_in = nc.dram_tensor("wu_in", [I, E], F32, kind="ExternalInput")
    wd_in = nc.dram_tensor("wd_in", [E, I], F32, kind="ExternalInput")
    c_in = nc.dram_tensor("c_in", [B, E], F32, kind="ExternalInput")
    wmod_in = nc.dram_tensor("wmod_in", [E, E6], F32, kind="ExternalInput")
    out_d = nc.dram_tensor("out", [B, S, E], F32, kind="ExternalOutput")

    _k = dict(kind="ExternalOutput") if debug else {}
    xmid_d = nc.dram_tensor("xmid_d", [B, S, E], F32, **_k)
    modd = nc.dram_tensor("modd", [B, E6], F32, **_k)
    qkT_d = nc.dram_tensor("qkT_d", [2 * NE * NE * 128, 128], BF16, **_k)
    wgT_d = nc.dram_tensor("wgT_d", [E, I], BF16, **_k)
    wuT_d = nc.dram_tensor("wuT_d", [E, I], BF16, **_k)
    wdT_d = nc.dram_tensor("wdT_d", [I, E], BF16, **_k)

    identb_d = nc.inline_tensor(np.eye(128, dtype=ml_dtypes.bfloat16), name="identb_c")
    identf_d = nc.inline_tensor(np.eye(128, dtype=np.float32), name="identf_c")
    _c = np.float32(np.sqrt(1e29))
    _ka = (-_c * (np.arange(128)[:, None] > np.arange(128)[None, :])
           ).astype(ml_dtypes.bfloat16)
    _kb = (_c * (np.arange(128)[:, None] <= np.arange(128)[None, :])
           ).astype(ml_dtypes.bfloat16)
    cmA_d = nc.inline_tensor(_ka, name="cmA_c")
    cmB_d = nc.inline_tensor(_kb, name="cmB_c")
    epsc_d = nc.inline_tensor(np.full((128, 1), 1e-5, dtype=np.float32), name="epsc_c")
    scc_d = nc.inline_tensor(np.full((128, 1), SCALE, dtype=np.float32), name="scc_c")
    nbias_d = nc.inline_tensor(np.full((128, 1), -12.0, dtype=np.float32),
                               name="nbias_c")

    SB_BASE = (nc.sbuf_base + 63) // 64 * 64
    SB_LIMIT = nc.sbuf_base + nc.sbuf_bytes_remaining
    off = [0]

    def sbt(name, shape, dtype, align=64):
        per = int(np.prod(shape[1:])) * mybir.dt.size(dtype)
        off[0] = (off[0] + align - 1) // align * align
        h = nc.alloc_sbuf_tensor_at(name, shape, dtype, offset=SB_BASE + off[0])
        off[0] += per
        return h

    identb = sbt("identb", [128, 128], BF16)
    identf = sbt("identf", [128, 128], F32)
    cmA = sbt("cmA", [128, 128], BF16)
    cmB = sbt("cmB", [128, 128], BF16)
    epsc = sbt("epsc", [128, 1], F32)
    scc = sbt("scc", [128, 1], F32)
    nbias = sbt("nbias", [128, 1], F32)
    xs0 = sbt("xs0", [128, E], F32)
    xs1 = sbt("xs1", [128, E], F32)
    ot0 = sbt("ot0", [128, E], F32)
    ot1 = sbt("ot1", [128, E], F32)
    oa = sbt("oa", [128, E], F32)
    xsq = sbt("xsq", [128, E], F32)
    gab = sbt("gab", [128, E], F32)
    caP = sbt("caP", [128, NE], F32)
    saP = sbt("saP", [128, NE], F32)
    stat = sbt("stat", [128, 24], F32)
    mx1 = sbt("mx1", [128, 4], F32)
    qax = (sbt("qax0", [128, 512], F32), sbt("qax1", [128, 512], F32))
    qbb = (sbt("qbb0", [128, 512], BF16), sbt("qbb1", [128, 512], BF16))
    msg = (sbt("msg0", [128, 512], BF16), sbt("msg1", [128, 512], BF16))
    m1 = sbt("m1", [128, 512], BF16)
    m2 = sbt("m2", [128, 512], BF16)
    m3 = sbt("m3", [128, 512], BF16)
    m4 = sbt("m4", [128, 512], BF16)
    m5 = sbt("m5", [128, 512], BF16)
    m6 = sbt("m6", [128, 512], BF16)
    qbm = (sbt("qbm0", [128, 32], F32), sbt("qbm1", [128, 32], F32))
    qr2 = (sbt("qr20", [128, 32], F32), sbt("qr21", [128, 32], F32))
    qs5 = (sbt("qs50", [128, 32], F32), sbt("qs51", [128, 32], F32))

    pers_end = off[0]
    xr0 = sbt("xr0", [128, E], F32)
    xr1 = sbt("xr1", [128, E], F32)
    xhb = sbt("xhb", [128, E], BF16)
    qT = sbt("qT", [128, NE * S], BF16)
    kT = sbt("kT", [128, NE * S], BF16)
    vsb = sbt("vsb", [128, NS * E], BF16)
    wvT = sbt("wvT", [128, NE * E], BF16)
    Msb = sbt("Msb", [128, NE * E], BF16)
    ssb = sbt("ssb", [128, S], F32)
    psb = sbt("psb", [128, S], BF16)
    ptT = sbt("ptT", [128, 8 * 128], BF16)
    wst0 = sbt("wst0", [128, 512], F32)
    wst1 = sbt("wst1", [128, 512], F32)
    wqo = sbt("wqo", [128, 512], BF16)
    wto0 = sbt("wto0", [128, NE * 128], BF16)
    orw = sbt("orw", [128, E], F32)
    wtoA = nc.alloc_sbuf_tensor_at("wtoA", [128, 512], F32,
                                   offset=nc.lookup_mloc(ssb).addr)
    wtoB = nc.alloc_sbuf_tensor_at("wtoB", [128, 512], F32,
                                   offset=nc.lookup_mloc(ptT).addr)
    a_end = off[0]
    print("A region end:", a_end, "limit:", SB_LIMIT - SB_BASE)
    assert SB_BASE + a_end <= SB_LIMIT, f"SBUF over budget (A): {a_end}"

    off[0] = pers_end
    wgT = sbt("wgT_s", [128, NE * I], BF16)
    wuT = sbt("wuT_s", [128, NE * I], BF16)
    wdT = sbt("wdT_s", [128, NI * E], BF16)
    h2f = sbt("h2f", [128, E], F32)
    h2q = sbt("h2q", [128, E], BF16)
    hqT = sbt("hqT", [128, NI * 128], BF16)
    h2qT = sbt("h2qT", [128, E], BF16)
    hqsb = sbt("hqsb", [128, I], BF16)
    sgf = sbt("sgf", [128, 512], F32)
    hidf = sbt("hidf", [128, 512], F32)
    cbb = sbt("cbb", [128, E], F32)
    sbb = sbt("sbb", [128, E], F32)
    b_end = off[0]
    print("B region end:", b_end)
    assert SB_BASE + b_end <= SB_LIMIT, f"SBUF over budget (B): {b_end}"

    psS = nc.alloc_psum_tensor("psS", [128, 2048], F32)
    psO = nc.alloc_psum_tensor("psO", [128, 1024], F32)
    psT0 = nc.alloc_psum_tensor("psT0", [128, 1024], BF16)
    psT1 = nc.alloc_psum_tensor("psT1", [128, 1024], BF16)

    es = contextlib.ExitStack()
    P = Prog(nc)
    for chn in ("DS", "DG", "T", "V", "S", "G"):
        P.channel(chn, es.enter_context(nc.semaphore("ch_" + chn)))

    def dma(eng, dst, src, waits=()):
        ch = "DS" if eng == "sync" else "DG"
        return P.emit(eng, lambda E_, d=dst, s=src: E_.dma_start(d, s),
                      waits=waits, inc=ch, inc_k=16)

    def op(eng, fn, waits=(), sw=True):
        ch = {"vector": "V", "scalar": "S", "tensor": "T", "gpsimd": "G"}[eng]
        return P.emit(eng, fn, waits=waits, inc=ch, sw=sw)

    st = {}

    QP = dict(i=0, sc={}, sg={}, g={}, v={})

    def quant(xin, qout, W, tin, tax=()):
        nb = W // 16
        pi = QP["i"] & 1
        QP["i"] += 1
        ax, bbt, sg = qax[pi], qbb[pi], msg[pi]
        bm, r2, s5 = qbm[pi], qr2[pi], qs5[pi]
        scfree = QP["sc"].get(pi)
        sgfree = QP["sg"].get(pi)
        gfree = QP["g"].get(pi)
        tins = tin if isinstance(tin, tuple) else (tin,)
        tsa = op("scalar", lambda E_: E_.activation(
            ax[:, 0:W], xin, AF.Abs), waits=tins + (scfree,) + tuple(tax))
        tsg = op("scalar", lambda E_: E_.activation(
            sg[:, 0:W], xin, AF.Sign), waits=tins + (sgfree,) + tuple(tax), sw=False)
        tbm = op("vector", lambda E_: E_.tensor_reduce(
            bm[:, 0:nb].rearrange("p (n o) -> p n o", o=1),
            ax[:, 0:W].rearrange("p (n b) -> p n b", b=16), AX.X, ALU.max),
            waits=(tsa, gfree), sw=False)
        op("vector", lambda E_: E_.tensor_scalar(
            s5[:, 0:nb], bm[:, 0:nb], 6e-12, float(np.float32(1.0 / 12.0)),
            ALU.max, ALU.mult), waits=(tbm,), sw=True)
        tr2 = op("vector", lambda E_: E_.reciprocal(r2[:, 0:nb], s5[:, 0:nb]),
                 sw=True)
        tb = op("vector", lambda E_: E_.tensor_tensor(
            qbb[pi][:, 0:W].rearrange("p (n b) -> p n b", b=16),
            ax[:, 0:W].rearrange("p (n b) -> p n b", b=16),
            bcast16(r2[:, 0:nb], nb), ALU.mult), waits=(tr2, tsa), sw=True)
        QP["sc"][pi] = tb
        vprev = QP["v"].get(1 - pi)
        bbW = bbt[:, 0:W]
        op("vector", lambda E_: E_.tensor_scalar(m1[:, 0:W], bbW, 0.5, None, ALU.is_gt),
           waits=(tb, vprev), sw=True)
        op("vector", lambda E_: E_.tensor_scalar(m2[:, 0:W], bbW, 1.5, None, ALU.is_gt), sw=False)
        op("vector", lambda E_: E_.tensor_scalar(m4[:, 0:W], bbW, 2.5, None, ALU.is_gt), sw=False)
        op("vector", lambda E_: E_.tensor_scalar(m5[:, 0:W], bbW, 3.5, None, ALU.is_gt), sw=False)
        op("vector", lambda E_: E_.tensor_tensor(m3[:, 0:W], m1[:, 0:W], m2[:, 0:W], ALU.add), sw=False)
        op("vector", lambda E_: E_.tensor_tensor(m6[:, 0:W], m4[:, 0:W], m5[:, 0:W], ALU.add), sw=False)
        op("vector", lambda E_: E_.tensor_scalar(m1[:, 0:W], bbW, 5.0, None, ALU.is_gt), sw=False)
        op("vector", lambda E_: E_.tensor_scalar(m2[:, 0:W], bbW, 7.0, None, ALU.is_gt), sw=False)
        op("vector", lambda E_: E_.tensor_scalar(m4[:, 0:W], bbW, 10.0, None, ALU.is_gt), sw=False)
        op("vector", lambda E_: E_.tensor_tensor(m5[:, 0:W], m1[:, 0:W], m2[:, 0:W], ALU.add), sw=False)
        op("vector", lambda E_: E_.tensor_tensor(m2[:, 0:W], m3[:, 0:W], m6[:, 0:W], ALU.add), sw=False)
        op("vector", lambda E_: E_.scalar_tensor_tensor(
            m1[:, 0:W], m5[:, 0:W], 2.0, m2[:, 0:W], ALU.mult, ALU.add), sw=True)
        tq2 = op("vector", lambda E_: E_.scalar_tensor_tensor(
            m3[:, 0:W], m4[:, 0:W], 4.0, m1[:, 0:W], ALU.mult, ALU.add), sw=False)
        tsgm = op("vector", lambda E_: E_.tensor_tensor(
            m4[:, 0:W], m3[:, 0:W], sg[:, 0:W], ALU.mult), waits=(tq2, tsg), sw=False)
        QP["sg"][pi] = tsgm
        tq = op("vector", lambda E_: E_.tensor_tensor(
            qout.rearrange("p (n b) -> p n b", b=16),
            m4[:, 0:W].rearrange("p (n b) -> p n b", b=16),
            bcast16(s5[:, 0:nb], nb), ALU.mult), waits=(tsgm,), sw=True)
        QP["g"][pi] = tq
        QP["v"][pi] = tq
        st["q_prev"] = tq
        return tq, tsg

    tid = dma("sync", identb[:], identb_d[:])
    tidf = dma("sync", identf[:], identf_d[:])
    tmA = dma("sync", cmA[:], cmA_d[:])
    tmB = dma("sync", cmB[:], cmB_d[:])
    teps = dma("sync", epsc[:], epsc_d[:])
    tscc = dma("sync", scc[:], scc_d[:])
    tnb = dma("sync", nbias[:], nbias_d[:])
    tc = dma("sync", xs0[0:B, 0:E], c_in[:, :])

    tcs = op("scalar", lambda E_: E_.activation(xs1[0:B, 0:E], xs0[0:B, 0:E],
                                                AF.Silu), waits=(tc,))
    csT = xhb
    tr_last = None
    for c_ in range(NE):
        tr_last = op("tensor", lambda E_, c_=c_: E_.transpose(
            psO[:, c_ * B:(c_ + 1) * B], xs1[0:B, c_ * 128:(c_ + 1) * 128],
            identf[0:B, 0:B]), waits=(tcs, tidf))
    tev_cs = op("vector", lambda E_: E_.tensor_copy(
        csT[:, 0:NE * B], psO[:, 0:NE * B]), waits=(tr_last,))

    wmst = (wst0, wst1, wtoA, wtoB)
    wmbf = (m1, m2, m5, m6)
    NT6 = E6 // 512
    bufdep = {}
    castdep = {}
    mm_last = None
    tmod_done = None
    for nt in range(NT6):
        mm_slot = nt % 2
        evac_prev = st.get("modev%d" % mm_slot)
        for e in range(NE):
            bi = (nt * NE + e) % 4
            tld = dma("gpsimd" if bi % 2 == 0 else "sync", wmst[bi][:, 0:512],
                      wmod_in[e * 128:(e + 1) * 128, nt * 512:(nt + 1) * 512],
                      waits=(bufdep.get(bi),))
            tcast = op("vector", lambda E_, bi=bi: E_.tensor_copy(
                wmbf[bi][:, 0:512], wmst[bi][:, 0:512]), waits=(tld,))
            castdep[bi] = tcast
            mm_last = op("tensor", lambda E_, e=e, bi=bi, ms=mm_slot: E_.matmul(
                psO[0:B, ms * 512:(ms + 1) * 512], csT[:, e * B:(e + 1) * B],
                wmbf[bi][:, 0:512], start=(e == 0), stop=(e == NE - 1)),
                waits=(tcast, tev_cs, evac_prev))
            bufdep[bi] = mm_last
        lo = nt * 512
        need_clamp = (2 * E <= lo < 3 * E) or (5 * E <= lo < 6 * E)
        tev2 = op("vector", lambda E_, ms=mm_slot: E_.tensor_copy(
            ot0[0:B, 0:512], psO[0:B, ms * 512:(ms + 1) * 512]), waits=(mm_last,))
        if need_clamp:
            op("vector", lambda E_: E_.tensor_scalar(
                ot1[0:B, 0:512], ot0[0:B, 0:512], 0.25, None, ALU.max))
            op("vector", lambda E_: E_.tensor_scalar(
                oa[0:B, 0:512], ot0[0:B, 0:512], -0.25, None, ALU.min))
            op("vector", lambda E_: E_.tensor_scalar(
                xsq[0:B, 0:512], ot0[0:B, 0:512], 0.0, None, ALU.is_ge))
            op("vector", lambda E_: E_.tensor_tensor(
                xs1[0:B, 0:512], ot1[0:B, 0:512], oa[0:B, 0:512], ALU.subtract))
            op("vector", lambda E_: E_.tensor_tensor(
                ot1[0:B, 0:512], xs1[0:B, 0:512], xsq[0:B, 0:512], ALU.mult))
            tev2 = op("vector", lambda E_: E_.tensor_tensor(
                ot0[0:B, 0:512], oa[0:B, 0:512], ot1[0:B, 0:512], ALU.add))
        st["modev%d" % mm_slot] = tev2
        tmod_done = dma("sync", modd[:, nt * 512:(nt + 1) * 512],
                        ot0[0:B, 0:512], waits=(tev2,))
    st["p0cast"] = castdep.get(2)
    st["p0castB"] = castdep.get(3)

    half = (NE + 1) // 2
    wqstg, wkstg = qT, kT
    tstg = None
    for ep in range(NE):
        tstg = dma("gpsimd", wqstg[:, ep * E:(ep + 1) * E],
                   wq_in[ep * 128:(ep + 1) * 128, :])
        tstg = dma("gpsimd", wkstg[:, ep * E:(ep + 1) * E],
                   wk_in[ep * 128:(ep + 1) * 128, :])
    tstg_s = None
    mev = {}
    tM_done = None
    tM_mm = None
    for ec in range(NE):
        for dh in range(NHF):
            slot = ((ec * NHF + dh) % 4) * 512
            mm = None
            for ep in range(NE):
                mm = op("tensor", lambda E_, ec=ec, dh=dh, ep=ep, slot=slot: E_.matmul(
                    psS[:, slot:slot + 512],
                    wqstg[:, ep * E + ec * 128: ep * E + ec * 128 + 128],
                    wkstg[:, ep * E + dh * 512: ep * E + dh * 512 + 512],
                    start=(ep == 0), stop=(ep == NE - 1)),
                    waits=(tstg, tstg_s, mev.get(slot)))
            tM_mm = mm
            tM_done = op("scalar", lambda E_, ec=ec, dh=dh, slot=slot: E_.activation(
                Msb[:, ec * E + dh * 512: ec * E + dh * 512 + 512],
                psS[:, slot:slot + 512], AF.Identity), waits=(mm,), sw=False)
            mev[slot] = tM_done

    bufdep2 = {0: mm_last, 1: mm_last}
    prev_tr = (tid,)
    twv_done = None
    qkv_tr_prev = mm_last
    for n in range(NE):
        bi = n % 2
        stg = wst0 if bi == 0 else wst1
        tca = []
        tprev = None
        for pc in range(E // 512):
            cbf = m3 if pc % 2 == 0 else m4
            tldp = dma("gpsimd", stg[:, 0:512],
                       wv_in[n * 128:(n + 1) * 128, pc * 512:(pc + 1) * 512],
                       waits=(bufdep2.get(bi), tprev, mm_last))
            tcp = op("vector", lambda E_, cbf=cbf, stg=stg: E_.tensor_copy(
                cbf[:, 0:512], stg[:, 0:512]),
                waits=(tldp, mm_last, qkv_tr_prev))
            tprev = tcp
            tca.append(tcp)
        bufdep2[bi] = tca[-1]
        trk = []
        for e in range(NE):
            pc = (e * 128) // 512
            cbf = m3 if pc % 2 == 0 else m4
            ecol = (e * 128) % 512
            bank = psT0 if e < half else psT1
            slot = e % half
            tt = op("tensor", lambda E_, bank=bank, slot=slot, cbf=cbf, ecol=ecol: E_.transpose(
                bank[:, slot * 128:(slot + 1) * 128],
                cbf[:, ecol:ecol + 128], identb[:]),
                waits=(tca[pc], tid, tM_mm) + prev_tr)
            trk.append(tt)
        qkv_tr_prev = trk[-1]

        def _evv(E_, n=n, banki=0):
            bank = psT0 if banki == 0 else psT1
            e0 = banki * half
            cnt = half if banki == 0 else NE - half
            base = wvT[:]
            dst = bass.AP(base.tensor, base.offset + e0 * E + n * 128,
                          [list(base.ap[0]), [E, cnt], [1, 128]])
            src = bank[:, 0:cnt * 128].rearrange("p (c w) -> p c w", w=128)
            return E_.tensor_copy(dst, src)
        ev0 = op("vector", lambda E_, n=n: _evv(E_, n, 0), waits=(trk[half - 1],))
        ev1 = op("vector", lambda E_, n=n: _evv(E_, n, 1), waits=(trk[-1],))
        twv_done = ev1
        prev_tr = (ev0, ev1)
    st["psT_ev"] = prev_tr[-1]
    st["wqo_tr"] = (twv_done, qkv_tr_prev)

    wchunks = []
    for w_d, wT_d, R, C in ((wg_in, wgT_d, I, E), (wu_in, wuT_d, I, E)):
        for r in range(R // 128):
            for cc in range(C // 512):
                wchunks.append((w_d, wT_d, r, cc * 512, 512, R))
    c_list = []
    c0 = 0
    while c0 < I:
        w_ = min(512, I - c0)
        c_list.append((c0, w_))
        c0 += w_
    for r in range(E // 128):
        for (c0, w_) in c_list:
            wchunks.append((wd_in, wdT_d, r, c0, w_, E))

    wqs = dict(free0=(mm_last, bufdep2.get(0)), free1=(mm_last, bufdep2.get(1)),
               stfree={}, bufi=0)

    def emit_wchunk(idx):
        if idx >= len(wchunks):
            return
        w_d, wT_d, r, c0, W, R = wchunks[idx]
        bi = wqs["bufi"]
        wqs["bufi"] ^= 1
        stg = wst0 if bi == 0 else wst1
        fr = wqs["free%d" % bi]
        h = W // 2
        tld_a = dma("gpsimd", stg[:, 0:h],
                    w_d[r * 128:(r + 1) * 128, c0:c0 + h], waits=fr)
        tld_b = dma("sync", stg[:, h:W],
                    w_d[r * 128:(r + 1) * 128, c0 + h:c0 + W], waits=fr)
        tq, tsg = quant(stg[:, 0:W], wqo[:, 0:W], W, (tld_a, tld_b),
                        tax=st.get("wqo_tr", ()))
        wqs["free%d" % bi] = (tq, tsg)
        ntile = W // 128
        trk = []
        for e in range(ntile):
            bank = psT0 if e < 4 else psT1
            slot = e % 4
            tt = op("tensor", lambda E_, bank=bank, slot=slot, e=e: E_.transpose(
                bank[:, slot * 128:(slot + 1) * 128],
                wqo[:, e * 128:(e + 1) * 128], identb[:]),
                waits=(tq, st.get("psT_ev")))
            trk.append(tt)
        st["wqo_tr"] = (trk[-1],)
        wto = wto0
        stfree = wqs["stfree"].get(0)
        n0 = min(4, ntile)
        evs = [op("scalar", lambda E_, n0=n0, wto=wto: E_.activation(
            wto[:, 0:n0 * 128], psT0[:, 0:n0 * 128], AF.Identity),
            waits=(trk[n0 - 1], stfree), sw=False)]
        if ntile > 4:
            evs.append(op("scalar", lambda E_, ntile=ntile, wto=wto: E_.activation(
                wto[:, 4 * 128:ntile * 128], psT1[:, 0:(ntile - 4) * 128],
                AF.Identity), waits=(trk[-1], stfree), sw=False))
        st["psT_ev"] = evs[-1]
        dst = bass.AP(wT_d[:].tensor, c0 * R + r * 128,
                      [[R, 128], [128 * R, ntile], [1, 128]])
        tst = dma("gpsimd", dst,
                  wto[:, 0:ntile * 128].rearrange("p (c w) -> p c w", w=128),
                  waits=tuple(evs))
        wqs["stfree"][0] = tst

    WCH = [0]

    def next_wchunks(k):
        if not interleave_w:
            return
        for _ in range(k):
            if WCH[0] < len(wchunks):
                emit_wchunk(WCH[0])
                WCH[0] += 1

    ast = dict(x={}, xr={}, omst={}, tom=None, tga=None, sc_evac=None,
               texp=None, tmx=None, o_evac=None, ptr_last=None, t3=None,
               t5=None, xhb_tr=None, pv_last=None, trl=None,
               qslot={}, vslot={},
               qe={"q": None, "k": None, "v": None}, ptev={},
               ptT_mm={0: st["p0castB"], 1: st["p0castB"]},
               trlp={}, trmp={}, t5p={}, tm2p={}, t8p={}, cap=None)
    xm_last = None
    for s in range(B):
        t_car = dma("sync", oa[0:1, 0:E], modd[s:s + 1, E:2 * E],
                    waits=(tmod_done, ast["tga"]))
        t_sar = dma("sync", orw[0:1, 0:E], modd[s:s + 1, 0:E],
                    waits=(tmod_done, ast["tom"]))
        t_gab = dma("sync", gab[:],
                    bass.AP(modd[:].tensor, s * E6 + 2 * E, [[0, 128], [1, E]]),
                    waits=(tmod_done, ast["tga"]))
        trc = trs = None
        for c_ in range(NE):
            trc = op("tensor", lambda E_, c_=c_: E_.transpose(
                psO[:, c_:c_ + 1], oa[0:1, c_ * 128:(c_ + 1) * 128],
                identf[0:1, 0:1]), waits=(t_car, tidf, ast["o_evac"]))
            trs = op("tensor", lambda E_, c_=c_: E_.transpose(
                psO[:, 512 + c_:512 + c_ + 1], orw[0:1, c_ * 128:(c_ + 1) * 128],
                identf[0:1, 0:1]), waits=(t_sar,))
        t_cap = op("scalar", lambda E_: E_.activation(
            caP[:, 0:NE], psO[:, 0:NE], AF.Identity, bias=1.0), waits=(trc,))
        t_sap = op("scalar", lambda E_: E_.activation(
            saP[:, 0:NE], psO[:, 512:512 + NE], AF.Identity), waits=(trs,))
        ast["o_evac"] = t_sap
        ast["cap"] = (t_cap, t_sap)

        xst = (xs0, xs1, xr0, xr1)
        for g in range(NSG):
            h_last = None
            for j in range(4):
                ti = g * 4 + j
                bi = ti % 4
                t_ld = dma("sync", xst[bi][:], x_in[s, ti * 128:(ti + 1) * 128, :],
                           waits=(ast["x"].get(bi), tmod_done,
                                  ast["xr"].get(bi - 2) if bi >= 2 else None,
                                  tcs if s == 0 and g == 0 else None))
                so = 12 * (ti % 2)
                op("vector", lambda E_, bi=bi, so=so: E_.tensor_reduce(
                    stat[:, so:so + 1], xst[bi][:], AX.X, ALU.add),
                    waits=(t_ld, ast["t5p"].get(ti % 2)))
                t2 = op("scalar", lambda E_, bi=bi, so=so: E_.activation(
                    xsq[:], xst[bi][:], AF.Square, accum_out=stat[:, so + 1:so + 2]),
                    waits=(t_ld, ast["tm2p"].get(ti % 2), tmod_done,
                           ast["tom"]), sw=False)
                op("vector", lambda E_, so=so: E_.tensor_scalar(
                    stat[:, so + 2:so + 3], stat[:, so:so + 1], 1.0 / E, None,
                    ALU.mult))
                tm2 = op("vector", lambda E_, so=so: E_.tensor_scalar(
                    stat[:, so + 3:so + 4], stat[:, so + 1:so + 2], 1.0 / E, None,
                    ALU.mult), waits=(t2,))
                ast["tm2p"][ti % 2] = tm2
                op("vector", lambda E_, so=so: E_.tensor_tensor(
                    stat[:, so + 4:so + 5], stat[:, so + 2:so + 3],
                    stat[:, so + 2:so + 3], ALU.mult))
                t4 = op("vector", lambda E_, so=so: E_.tensor_tensor(
                    stat[:, so + 5:so + 6], stat[:, so + 3:so + 4],
                    stat[:, so + 4:so + 5], ALU.subtract))
                t5 = op("scalar", lambda E_, so=so: E_.activation(
                    stat[:, so + 6:so + 7], stat[:, so + 5:so + 6], AF.Sqrt,
                    bias=epsc[:, 0:1]), waits=(t4, teps))
                ast["t5p"][ti % 2] = t5
                t6 = op("vector", lambda E_, so=so: E_.reciprocal(
                    stat[:, so + 7:so + 8], stat[:, so + 6:so + 7]),
                    waits=(t5, ast["t8p"].get(ti % 2)))
                t7 = op("vector", lambda E_, so=so: E_.scalar_tensor_tensor(
                    stat[:, so + 8:so + 9], stat[:, so + 2:so + 3], -1.0,
                    stat[:, so + 7:so + 8], ALU.mult, ALU.mult))
                t8 = op("scalar", lambda E_, bi=bi, so=so: E_.activation(
                    xhb[:], xst[bi][:], AF.Identity, bias=stat[:, so + 8:so + 9],
                    scale=stat[:, so + 7:so + 8]),
                    waits=(t7, tmod_done, ast["xhb_tr"],
                           tev_cs if s == 0 and g == 0 else None), sw=False)
                ast["t8p"][ti % 2] = t8
                ast["x"][bi] = t8
                trk = []
                for e in range(NE):
                    bank = psT0 if e < half else psT1
                    slot = e % half
                    tt = op("tensor", lambda E_, bank=bank, slot=slot, e=e: E_.transpose(
                        bank[:, slot * 128:(slot + 1) * 128],
                        xhb[:, e * 128:(e + 1) * 128], identb[:]),
                        waits=(t8, tid, st.get("psT_ev")))
                    trk.append(tt)
                ast["xhb_tr"] = trk[-1]
                for e in range(NE):
                    bank = psT0 if e < half else psT1
                    slot = e % half
                    h_last = op("scalar", lambda E_, bank=bank, slot=slot, e=e, j=j, g=g: E_.activation(
                        kT[:, e * S + g * 512 + j * 128: e * S + g * 512 + (j + 1) * 128],
                        bank[:, slot * 128:(slot + 1) * 128],
                        AF.Identity, bias=saP[:, e:e + 1], scale=caP[:, e:e + 1]),
                        waits=(trk[e], ast["texp"], tM_mm) + ast["cap"],
                        sw=(e == 0 and j == 0))
                st["psT_ev"] = h_last
            for n in range(NE):
                base = (n % 2) * 512
                mm = None
                for e in range(NE):
                    mm = op("tensor", lambda E_, n=n, e=e, base=base, g=g: E_.matmul(
                        psS[:, base:base + 512],
                        Msb[:, e * E + n * 128: e * E + n * 128 + 128],
                        kT[:, e * S + g * 512: e * S + (g + 1) * 512],
                        start=(e == 0), stop=(e == NE - 1)),
                        waits=(h_last, tM_done, ast["qslot"].get(base)))
                ast["qe"]["q"] = op(
                    "scalar", lambda E_, n=n, g=g, base=base: E_.activation(
                        qT[:, n * S + g * 512: n * S + (g + 1) * 512],
                        psS[:, base:base + 512], AF.Identity),
                    waits=(mm, ast["pv_last"], tM_mm), sw=False)
                ast["qslot"][base] = ast["qe"]["q"]
            for j in range(4):
                ti = g * 4 + j
                for hf in range(NHF):
                    mm = None
                    for e in range(NE):
                        mm = op("tensor", lambda E_, j=j, hf=hf, e=e, g=g: E_.matmul(
                            psS[:, 1024 + hf * 512: 1024 + (hf + 1) * 512],
                            kT[:, e * S + g * 512 + j * 128: e * S + g * 512 + (j + 1) * 128],
                            wvT[:, e * E + hf * 512: e * E + (hf + 1) * 512],
                            start=(e == 0), stop=(e == NE - 1)),
                            waits=(h_last, twv_done, ast["vslot"].get(hf)))
                    ast["qe"]["v"] = op(
                        "scalar", lambda E_, ti=ti, hf=hf: E_.activation(
                            vsb[:, ti * E + hf * 512: ti * E + (hf + 1) * 512],
                            psS[:, 1024 + hf * 512: 1024 + (hf + 1) * 512],
                            AF.Identity),
                        waits=(mm, ast["pv_last"]), sw=False)
                    ast["vslot"][hf] = ast["qe"]["v"]
            next_wchunks(1)
        ast["qe"]["k"] = h_last

        ssbs = (ssb, ssb)

        def a_stage1(i):
            W = (i + 1) * 128
            NK4 = (W + 511) // 512
            dk4 = (i * 128) // 512
            doff = (i * 128) % 512
            sc_mm = None
            for e in range(NE):
                for k4 in range(NK4):
                    w_ = min(512, W - k4 * 512)
                    sc_mm = op("tensor", lambda E_, e=e, k4=k4, w_=w_, i=i: E_.matmul(
                        psS[:, k4 * 512:k4 * 512 + w_],
                        qT[:, e * S + i * 128:e * S + (i + 1) * 128],
                        kT[:, e * S + k4 * 512:e * S + k4 * 512 + w_],
                        start=(e == 0),
                        stop=(e == NE - 1 and k4 != dk4)),
                        waits=(ast["qe"]["q"], ast["qe"]["k"], ast["sc_evac"]))
            sc_mm = op("tensor", lambda E_, dk4=dk4, doff=doff: E_.matmul(
                psS[:, dk4 * 512 + doff: dk4 * 512 + doff + 128],
                cmA[:], cmB[:], start=False, stop=True),
                waits=(tmA, tmB))
            ev = None
            for k4 in range(NK4):
                w_ = min(512, W - k4 * 512)
                ev = op("scalar", lambda E_, k4=k4, w_=w_, i=i: E_.activation(
                    ssbs[i % 2][:, k4 * 512:k4 * 512 + w_],
                    psS[:, k4 * 512:k4 * 512 + w_],
                    AF.Identity, scale=scc[:, 0:1]),
                    waits=(sc_mm, tscc, st["p0cast"]), sw=False)
            ast["sc_evac"] = ev

        def a_stage2_front(i):
            pi2 = i % 2
            texp = op("scalar", lambda E_, W=(i + 1) * 128, i=i, pi2=pi2: E_.activation(
                psb[:, 0:W], ssbs[i % 2][:, 0:W], AF.Exp, bias=nbias[:, 0:1],
                accum_out=mx1[:, pi2 * 2:pi2 * 2 + 1]),
                waits=(tnb, ast["ptr_last"], ast["trlp"].get(pi2)), sw=False)
            ast["texp"] = texp
            trl = op("vector", lambda E_, pi2=pi2: E_.reciprocal(
                mx1[:, pi2 * 2 + 1:pi2 * 2 + 2], mx1[:, pi2 * 2:pi2 * 2 + 1]),
                waits=(texp, ast["trmp"].get(pi2)))
            ast["trl"] = trl
            ast["trlp"][pi2] = trl

        def a_stage2(i, s):
            W = (i + 1) * 128
            bi = i % 2
            xrb = xr0 if bi == 0 else xr1
            t_ldr = dma("sync", xrb[:], x_in[s, i * 128:(i + 1) * 128, :],
                        waits=(ast["xr"].get(bi), ast["x"].get(bi + 2),
                               t_car, t_sar, t_cap if bi == 0 else t_sap))
            texp = ast["texp"]
            trl = ast["trl"]
            pv_mm = None
            NKT = i + 1
            for g4 in range(0, NKT, 4):
                cnt = min(4, NKT - g4)
                pgrp = (g4 // 4) % 2
                bank = psT0 if pgrp == 0 else psT1
                trk = []
                for kk in range(cnt):
                    kt = g4 + kk
                    tt = op("tensor", lambda E_, bank=bank, kk=kk, kt=kt: E_.transpose(
                        bank[:, kk * 128:(kk + 1) * 128],
                        psb[:, kt * 128:(kt + 1) * 128], identb[:]),
                        waits=(texp, st.get("psT_ev"), ast["ptev"].get(pgrp)))
                    trk.append(tt)
                ast["ptr_last"] = trk[-1]
                tevp = op("scalar", lambda E_, bank=bank, cnt=cnt, pgrp=pgrp: E_.activation(
                    ptT[:, pgrp * 512: pgrp * 512 + cnt * 128], bank[:, 0:cnt * 128],
                    AF.Identity), waits=(trk[-1], ast["ptT_mm"].get(pgrp)), sw=False)
                st["psT_ev"] = tevp
                ast["ptev"][pgrp] = tevp
                for kk in range(cnt):
                    kt = g4 + kk
                    for hf in range(NHF):
                        pv_mm = op("tensor", lambda E_, kt=kt, kk=kk, hf=hf, pgrp=pgrp, i=i: E_.matmul(
                            psO[:, hf * 512:(hf + 1) * 512],
                            ptT[:, pgrp * 512 + kk * 128: pgrp * 512 + (kk + 1) * 128],
                            vsb[:, kt * E + hf * 512: kt * E + (hf + 1) * 512],
                            start=(kt == 0), stop=(kt == i)),
                            waits=(tevp, ast["qe"]["v"], ast["o_evac"]))
                ast["ptT_mm"][pgrp] = pv_mm
            ast["pv_last"] = pv_mm
            oev = None
            for hf in range(NHF):
                oev = op("scalar", lambda E_, hf=hf: E_.activation(
                    oa[:, hf * 512:(hf + 1) * 512], psO[:, hf * 512:(hf + 1) * 512],
                    AF.Identity), waits=(pv_mm, ast["tga"]), sw=False)
            ast["o_evac"] = oev
            tga = op("gpsimd", lambda E_: E_.tensor_tensor(
                xsq[:], oa[:], gab[:], ALU.mult), waits=(oev, t_gab))
            ast["tga"] = tga
            pi2 = i % 2
            _m = mx1[:]
            rsc = bass.AP(_m.tensor, _m.offset + pi2 * 2 + 1,
                          [list(_m.ap[0]), [0, E]])
            trm = op("gpsimd", lambda E_, rsc=rsc: E_.tensor_tensor(
                orw[:], xsq[:], rsc, ALU.mult), waits=(trl,))
            ast["trmp"][pi2] = trm
            obuf = ot0 if bi == 0 else ot1
            tom = op("gpsimd", lambda E_, obuf=obuf, xrb=xrb: E_.tensor_tensor(
                obuf[:], orw[:], xrb[:], ALU.add),
                waits=(trm, t_ldr, ast["omst"].get(bi)))
            ast["tom"] = tom
            ast["xr"][bi] = tom
            xm = dma("sync", xmid_d[s, i * 128:(i + 1) * 128, :], obuf[:],
                     waits=(tom,))
            ast["omst"][bi] = xm
            next_wchunks(2)
            return xm

        a_stage1(0)
        for i in range(NS):
            a_stage2_front(i)
            if i + 1 < NS:
                a_stage1(i + 1)
            xm_last = a_stage2(i, s)

    while WCH[0] < len(wchunks):
        emit_wchunk(WCH[0])
        WCH[0] += 1

    barrier = tuple(x for x in (xm_last, wqs["stfree"].get(0),
                                wqs["stfree"].get(1)) if x is not None)
    t_wB = None
    for e in range(NE):
        t_wB = dma("gpsimd", wgT[:, e * I:(e + 1) * I],
                   wgT_d[e * 128:(e + 1) * 128, :], waits=barrier)
    for e in range(NE):
        t_wB = dma("gpsimd", wuT[:, e * I:(e + 1) * I],
                   wuT_d[e * 128:(e + 1) * 128, :], waits=barrier)
    for ic in range(NI):
        t_wB = dma("gpsimd", wdT[:, ic * E:(ic + 1) * E],
                   wdT_d[ic * 128:(ic + 1) * 128, :], waits=barrier)

    bst = dict(x={}, ost={}, tout=None, tgb=barrier[0],
               t5p={}, tm2p={}, t8p={},
               h2q_tr=(), hq_tr=(), gate_mm=None, mmd=None, thd=None,
               tsg=None, t9=None, sl={0: None, 1: None})
    itiles = c_list

    def b_back(s, i, bi, xb_, mmd, t_gbb):
        tgb = op("vector", lambda E_: E_.tensor_tensor(
            oa[:], psO[:, 0:E], gab[:], ALU.mult),
            waits=(mmd, t_gbb, bst["tout"]), sw=False)
        bst["tgb"] = tgb
        obuf = ot0 if bi == 0 else ot1
        tout = op("gpsimd", lambda E_, obuf=obuf, xb_=xb_: E_.tensor_tensor(
            obuf[:], oa[:], xb_[:], ALU.add),
            waits=(tgb, bst["ost"].get(bi)))
        bst["tout"] = tout
        bst["x"][bi] = tout
        tst = dma("gpsimd", out_d[s, i * 128:(i + 1) * 128, :], obuf[:],
                  waits=(tout,))
        bst["ost"][bi] = tst

    pend = None
    for s in range(B):
        if pend is not None:
            b_back(*pend)
            pend = None
        t_cbr = dma("sync", h2f[:],
                    bass.AP(modd[:].tensor, s * E6 + 4 * E, [[0, 128], [1, E]]),
                    waits=barrier + (bst["t9"], bst["tsg"]))
        t_cb1 = op("vector", lambda E_: E_.tensor_scalar(
            cbb[:], h2f[:], 1.0, None, ALU.add), waits=(t_cbr, bst["t9"]))
        t_sbb = dma("sync", sbb[:],
                    bass.AP(modd[:].tensor, s * E6 + 3 * E, [[0, 128], [1, E]]),
                    waits=barrier + (bst["t9"],))
        t_gbb = dma("sync", gab[:],
                    bass.AP(modd[:].tensor, s * E6 + 5 * E, [[0, 128], [1, E]]),
                    waits=barrier + (ast["tga"], bst["tgb"]))
        for i in range(NS):
            bi = i % 2
            xb_ = xs0 if bi == 0 else xs1
            t_ld = dma("sync", xb_[:], xmid_d[s, i * 128:(i + 1) * 128, :],
                       waits=(bst["x"].get(bi), xm_last))
            so = 12 * (i % 2)
            op("vector", lambda E_, xb_=xb_, so=so: E_.tensor_reduce(
                stat[:, so:so + 1], xb_[:], AX.X, ALU.add),
                waits=(t_ld, bst["t5p"].get(bi)))
            t2 = op("scalar", lambda E_, xb_=xb_, so=so: E_.activation(
                xsq[:], xb_[:], AF.Square, accum_out=stat[:, so + 1:so + 2]),
                waits=(t_ld, bst["tm2p"].get(bi), ast["tga"], bst["t9"]), sw=False)
            op("vector", lambda E_, so=so: E_.tensor_scalar(
                stat[:, so + 2:so + 3], stat[:, so:so + 1], 1.0 / E, None,
                ALU.mult))
            tm2 = op("vector", lambda E_, so=so: E_.tensor_scalar(
                stat[:, so + 3:so + 4], stat[:, so + 1:so + 2], 1.0 / E, None,
                ALU.mult), waits=(t2,))
            bst["tm2p"][bi] = tm2
            op("vector", lambda E_, so=so: E_.tensor_tensor(
                stat[:, so + 4:so + 5], stat[:, so + 2:so + 3],
                stat[:, so + 2:so + 3], ALU.mult))
            t4 = op("vector", lambda E_, so=so: E_.tensor_tensor(
                stat[:, so + 5:so + 6], stat[:, so + 3:so + 4],
                stat[:, so + 4:so + 5], ALU.subtract))
            t5 = op("scalar", lambda E_, so=so: E_.activation(
                stat[:, so + 6:so + 7], stat[:, so + 5:so + 6], AF.Sqrt,
                bias=epsc[:, 0:1]), waits=(t4, teps))
            bst["t5p"][bi] = t5
            t6 = op("vector", lambda E_, so=so: E_.reciprocal(
                stat[:, so + 7:so + 8], stat[:, so + 6:so + 7]),
                waits=(t5, bst["t8p"].get(bi)))
            t7 = op("vector", lambda E_, so=so: E_.scalar_tensor_tensor(
                stat[:, so + 8:so + 9], stat[:, so + 2:so + 3], -1.0,
                stat[:, so + 7:so + 8], ALU.mult, ALU.mult))
            t8 = op("scalar", lambda E_, xb_=xb_, so=so: E_.activation(
                h2f[:], xb_[:], AF.Identity, bias=stat[:, so + 8:so + 9],
                scale=stat[:, so + 7:so + 8]),
                waits=(t7, t_cb1, bst["tsg"]), sw=False)
            bst["t8p"][bi] = t8
            t9a = op("gpsimd", lambda E_: E_.tensor_tensor(
                xsq[:], h2f[:], cbb[:], ALU.mult), waits=(t8, t_cb1))
            t9 = op("gpsimd", lambda E_: E_.tensor_tensor(
                h2f[:], xsq[:], sbb[:], ALU.add), waits=(t_sbb,))
            bst["t9"] = t9
            tqs = None
            for hc in range(E // 512):
                tqs, tsgh2 = quant(h2f[:, hc * 512:(hc + 1) * 512],
                                   h2q[:, hc * 512:(hc + 1) * 512], 512, t9,
                                   tax=bst["h2q_tr"])
                bst["tsg"] = tsgh2
            tq = tqs
            trk = []
            for e in range(NE):
                bank = psT0 if e < half else psT1
                slot = e % half
                tt = op("tensor", lambda E_, bank=bank, slot=slot, e=e: E_.transpose(
                    bank[:, slot * 128:(slot + 1) * 128],
                    h2q[:, e * 128:(e + 1) * 128], identb[:]),
                    waits=(tq, st.get("psT_ev"), tid))
                trk.append(tt)
            bst["h2q_tr"] = (trk[-1],)
            tev0 = op("scalar", lambda E_: E_.activation(
                h2qT[:, 0:half * 128], psT0[:, 0:half * 128], AF.Identity),
                waits=(trk[half - 1], bst["gate_mm"]), sw=False)
            tev1 = op("scalar", lambda E_: E_.activation(
                h2qT[:, half * 128:NE * 128], psT1[:, 0:(NE - half) * 128],
                AF.Identity), waits=(trk[-1], bst["gate_mm"]), sw=False)
            st["psT_ev"] = tev1
            th2qT = tev1
            sltoks = []
            for it, (c0, w_) in enumerate(itiles):
                sl = it % 2
                gslot = 0 if sl == 0 else 1024
                uslot = 512 if sl == 0 else 1536
                mmg = None
                for e in range(NE):
                    mmg = op("tensor", lambda E_, e=e, c0=c0, w_=w_, gslot=gslot: E_.matmul(
                        psS[:, gslot:gslot + w_], h2qT[:, e * 128:(e + 1) * 128],
                        wgT[:, e * I + c0:e * I + c0 + w_],
                        start=(e == 0), stop=(e == NE - 1)),
                        waits=(th2qT, t_wB, bst["sl"][sl]))
                for e in range(NE):
                    mmg = op("tensor", lambda E_, e=e, c0=c0, w_=w_, uslot=uslot: E_.matmul(
                        psS[:, uslot:uslot + w_], h2qT[:, e * 128:(e + 1) * 128],
                        wuT[:, e * I + c0:e * I + c0 + w_],
                        start=(e == 0), stop=(e == NE - 1)))
                bst["gate_mm"] = mmg
                tsl = op("scalar", lambda E_, gslot=gslot, w_=w_: E_.activation(
                    sgf[:, 0:w_], psS[:, gslot:gslot + w_], AF.Silu),
                    waits=(mmg, bst["thd"]), sw=False)
                thd = op("vector", lambda E_, uslot=uslot, w_=w_: E_.tensor_tensor(
                    hidf[:, 0:w_], sgf[:, 0:w_], psS[:, uslot:uslot + w_], ALU.mult),
                    waits=(tsl, bst["sl"][sl], bst["tsg"]), sw=False)
                bst["thd"] = thd
                hq_dst = bass.AP(hqsb[:].tensor, hqsb[:].offset + c0,
                                 [list(hqsb[:].ap[0]), [1, w_]])
                tqh, tsg_h = quant(hidf[:, 0:w_], hq_dst, w_, thd,
                                   tax=bst["hq_tr"])
                bst["sl"][sl] = tqh
                bst["tsg"] = tsg_h
                sltoks.append(tqh)
            trk2 = None
            for icb in range(0, NI, 4):
                cnt = min(4, NI - icb)
                bank = psT0 if (icb // 4) % 2 == 0 else psT1
                grp = []
                for kk in range(cnt):
                    trk2 = op("tensor", lambda E_, bank=bank, kk=kk, icb=icb: E_.transpose(
                        bank[:, kk * 128:(kk + 1) * 128],
                        hqsb[:, (icb + kk) * 128:(icb + kk + 1) * 128], identb[:]),
                        waits=(sltoks[icb // 4], st.get("psT_ev")))
                    grp.append(trk2)
                tevh = op("scalar", lambda E_, bank=bank, cnt=cnt, icb=icb: E_.activation(
                    hqT[:, icb * 128:(icb + cnt) * 128], bank[:, 0:cnt * 128],
                    AF.Identity), waits=(grp[-1], bst["mmd"]), sw=False)
                st["psT_ev"] = tevh
            bst["hq_tr"] = (trk2,)
            thqT = st["psT_ev"]
            if pend is not None:
                b_back(*pend)
            mmd = None
            for ic in range(NI):
                for hf in range(NHF):
                    mmd = op("tensor", lambda E_, ic=ic, hf=hf: E_.matmul(
                        psO[:, hf * 512:(hf + 1) * 512],
                        hqT[:, ic * 128:(ic + 1) * 128],
                        wdT[:, ic * E + hf * 512: ic * E + (hf + 1) * 512],
                        start=(ic == 0), stop=(ic == NI - 1)),
                        waits=(thqT, t_wB, bst["tgb"]))
            bst["mmd"] = mmd
            pend = (s, i, bi, xb_, mmd, t_gbb)
    b_back(*pend)

    final_ds = Tok("DS", P.chan["DS"][1], "sync", True)
    final_dg = Tok("DG", P.chan["DG"][1], "gpsimd", True)
    P.wait_only("sync", (final_ds,))
    P.wait_only("gpsimd", (final_dg,))

    with nc.Block() as block:
        @block.sync
        def _(sync):
            P.run(sync, "sync")

        @block.vector
        def _(vector):
            P.run(vector, "vector")

        @block.scalar
        def _(scalar):
            P.run(scalar, "scalar")

        @block.tensor
        def _(tensor):
            P.run(tensor, "tensor")

        @block.gpsimd
        def _(gpsimd):
            P.run(gpsimd, "gpsimd")

    es.close()
    return nc


_NC_CACHE = {}


def kernel(L0_attn_x, L0_attn_WQ, L0_attn_WK, L0_attn_WV,
           L0_ffn_w_gate, L0_ffn_w_up, L0_ffn_w_down, L0_c, L0_wmod):
    cfg = FULL_CFG
    n = cfg["NCORES"]
    bpc = L0_attn_x.shape[0] // n
    if "full" not in _NC_CACHE:
        _NC_CACHE["full"] = build(cfg)
    nc = _NC_CACHE["full"]
    x = np.ascontiguousarray(L0_attn_x)
    c = np.ascontiguousarray(L0_c)
    shared = {
        "wq_in": np.ascontiguousarray(L0_attn_WQ),
        "wk_in": np.ascontiguousarray(L0_attn_WK),
        "wv_in": np.ascontiguousarray(L0_attn_WV),
        "wg_in": np.ascontiguousarray(L0_ffn_w_gate),
        "wu_in": np.ascontiguousarray(L0_ffn_w_up),
        "wd_in": np.ascontiguousarray(L0_ffn_w_down),
        "wmod_in": np.ascontiguousarray(L0_wmod),
    }
    in_maps = []
    for ci in range(n):
        m = dict(shared)
        m["x_in"] = np.ascontiguousarray(x[ci * bpc:(ci + 1) * bpc])
        m["c_in"] = np.ascontiguousarray(c[ci * bpc:(ci + 1) * bpc])
        in_maps.append(m)
    res = run_bass_kernel_spmd(nc, in_maps, core_ids=list(range(n)))
    return np.concatenate([np.asarray(r["out"]) for r in res.results], axis=0)


# version 12
# speedup vs baseline: 1.0068x; 1.0068x over previous
import contextlib
import numpy as np
import ml_dtypes
import concourse.bass as bass
import concourse.mybir as mybir
from concourse.bass_utils import run_bass_kernel_spmd

F32 = mybir.dt.float32
BF16 = mybir.dt.bfloat16
AF = mybir.ActivationFunctionType
ALU = mybir.AluOpType
AX = mybir.AxisListType

ENGINES = ("sync", "vector", "scalar", "tensor", "gpsimd")

FULL_CFG = dict(B=4, S=2048, E=1024, I=2816, NCORES=8)


class Tok:
    __slots__ = ("ch", "val", "eng", "dma")

    def __init__(self, ch, val, eng, dma):
        self.ch, self.val, self.eng, self.dma = ch, val, eng, dma


class Prog:

    def __init__(self, nc):
        self.nc = nc
        self.streams = {e: [] for e in ENGINES}
        self.chan = {}
        self.water = {}

    def channel(self, name, sem):
        self.chan[name] = [sem, 0]

    CHAINED = ("vector", "scalar", "gpsimd")

    def emit(self, eng, fn, waits=(), inc=None, inc_k=1, sw=True):
        wl = []
        if sw and eng in self.CHAINED and inc is not None:
            cur = self.chan[inc][1]
            key = (eng, inc)
            if cur > 0 and self.water.get(key, -1) < cur:
                self.water[key] = cur
                wl.append((inc, cur))
        for t in waits:
            if t is None:
                continue
            if (not t.dma) and t.eng == eng:
                continue
            key = (eng, t.ch)
            if self.water.get(key, -1) >= t.val:
                continue
            self.water[key] = t.val
            wl.append((t.ch, t.val))
        tok = None
        if inc is not None:
            c = self.chan[inc]
            c[1] += inc_k
            tok = Tok(inc, c[1], eng, inc_k >= 16)
        self.streams[eng].append((wl, fn, inc, inc_k))
        return tok

    def wait_only(self, eng, waits):
        self.emit(eng, None, waits=waits)

    def run(self, engobj, eng):
        for wl, fn, inc, inc_k in self.streams[eng]:
            for ch, val in wl:
                engobj.wait_ge(self.chan[ch][0], val)
            if fn is None:
                continue
            ins = fn(engobj)
            if inc is not None:
                ins.then_inc(self.chan[inc][0], inc_k)


def bcast16(ap2d, nblk, blk=16):
    return bass.AP(ap2d.tensor, ap2d.offset, [list(ap2d.ap[0]), [1, nblk], [0, blk]])


def build(cfg, debug=False, interleave_w=True):
    B, S, E, I = cfg["B"], cfg["S"], cfg["E"], cfg["I"]
    NE, NS, NI = E // 128, S // 128, I // 128
    NSG = S // 512
    NHF = E // 512
    SCALE = float(E) ** -0.5
    E6 = 6 * E
    assert S % 512 == 0 and E % 512 == 0 and I % 128 == 0

    nc = bass.Bass("TRN2", target_bir_lowering=False, detect_race_conditions=False)

    x_in = nc.dram_tensor("x_in", [B, S, E], F32, kind="ExternalInput")
    wq_in = nc.dram_tensor("wq_in", [E, E], F32, kind="ExternalInput")
    wk_in = nc.dram_tensor("wk_in", [E, E], F32, kind="ExternalInput")
    wv_in = nc.dram_tensor("wv_in", [E, E], F32, kind="ExternalInput")
    wg_in = nc.dram_tensor("wg_in", [I, E], F32, kind="ExternalInput")
    wu_in = nc.dram_tensor("wu_in", [I, E], F32, kind="ExternalInput")
    wd_in = nc.dram_tensor("wd_in", [E, I], F32, kind="ExternalInput")
    c_in = nc.dram_tensor("c_in", [B, E], F32, kind="ExternalInput")
    wmod_in = nc.dram_tensor("wmod_in", [E, E6], F32, kind="ExternalInput")
    out_d = nc.dram_tensor("out", [B, S, E], F32, kind="ExternalOutput")

    _k = dict(kind="ExternalOutput") if debug else {}
    xmid_d = nc.dram_tensor("xmid_d", [B, S, E], F32, **_k)
    modd = nc.dram_tensor("modd", [B, E6], F32, **_k)
    qkT_d = nc.dram_tensor("qkT_d", [2 * NE * NE * 128, 128], BF16, **_k)
    wgT_d = nc.dram_tensor("wgT_d", [E, I], BF16, **_k)
    wuT_d = nc.dram_tensor("wuT_d", [E, I], BF16, **_k)
    wdT_d = nc.dram_tensor("wdT_d", [I, E], BF16, **_k)

    identb_d = nc.inline_tensor(np.eye(128, dtype=ml_dtypes.bfloat16), name="identb_c")
    identf_d = nc.inline_tensor(np.eye(128, dtype=np.float32), name="identf_c")
    _c = np.float32(np.sqrt(1e29))
    _ka = (-_c * (np.arange(128)[:, None] > np.arange(128)[None, :])
           ).astype(ml_dtypes.bfloat16)
    _kb = (_c * (np.arange(128)[:, None] <= np.arange(128)[None, :])
           ).astype(ml_dtypes.bfloat16)
    cmA_d = nc.inline_tensor(_ka, name="cmA_c")
    cmB_d = nc.inline_tensor(_kb, name="cmB_c")
    epsc_d = nc.inline_tensor(np.full((128, 1), 1e-5, dtype=np.float32), name="epsc_c")
    scc_d = nc.inline_tensor(np.full((128, 1), SCALE, dtype=np.float32), name="scc_c")
    nbias_d = nc.inline_tensor(np.full((128, 1), -12.0, dtype=np.float32),
                               name="nbias_c")

    SB_BASE = (nc.sbuf_base + 63) // 64 * 64
    SB_LIMIT = nc.sbuf_base + nc.sbuf_bytes_remaining
    off = [0]

    def sbt(name, shape, dtype, align=64):
        per = int(np.prod(shape[1:])) * mybir.dt.size(dtype)
        off[0] = (off[0] + align - 1) // align * align
        h = nc.alloc_sbuf_tensor_at(name, shape, dtype, offset=SB_BASE + off[0])
        off[0] += per
        return h

    identb = sbt("identb", [128, 128], BF16)
    identf = sbt("identf", [128, 128], F32)
    cmA = sbt("cmA", [128, 128], BF16)
    cmB = sbt("cmB", [128, 128], BF16)
    epsc = sbt("epsc", [128, 1], F32)
    scc = sbt("scc", [128, 1], F32)
    nbias = sbt("nbias", [128, 1], F32)
    xs0 = sbt("xs0", [128, E], F32)
    xs1 = sbt("xs1", [128, E], F32)
    ot0 = sbt("ot0", [128, E], F32)
    ot1 = sbt("ot1", [128, E], F32)
    oa = sbt("oa", [128, E], F32)
    xsq = sbt("xsq", [128, E], F32)
    gab = sbt("gab", [128, E], F32)
    caP = sbt("caP", [128, NE], F32)
    saP = sbt("saP", [128, NE], F32)
    stat = sbt("stat", [128, 24], F32)
    mx1 = sbt("mx1", [128, 4], F32)
    qax = (sbt("qax0", [128, 512], F32), sbt("qax1", [128, 512], F32))
    qbb = (sbt("qbb0", [128, 512], BF16), sbt("qbb1", [128, 512], BF16))
    msg = (sbt("msg0", [128, 512], BF16), sbt("msg1", [128, 512], BF16))
    m1 = sbt("m1", [128, 512], BF16)
    m2 = sbt("m2", [128, 512], BF16)
    m3 = sbt("m3", [128, 512], BF16)
    m4 = sbt("m4", [128, 512], BF16)
    m5 = sbt("m5", [128, 512], BF16)
    m6 = sbt("m6", [128, 512], BF16)
    qbm = (sbt("qbm0", [128, 32], F32), sbt("qbm1", [128, 32], F32))
    qr2 = (sbt("qr20", [128, 32], F32), sbt("qr21", [128, 32], F32))
    qs5 = (sbt("qs50", [128, 32], F32), sbt("qs51", [128, 32], F32))

    pers_end = off[0]
    xr0 = sbt("xr0", [128, E], F32)
    xr1 = sbt("xr1", [128, E], F32)
    xhb = sbt("xhb", [128, E], BF16)
    qT = sbt("qT", [128, NE * S], BF16)
    kT = sbt("kT", [128, NE * S], BF16)
    vsb = sbt("vsb", [128, NS * E], BF16)
    wvT = sbt("wvT", [128, NE * E], BF16)
    Msb = sbt("Msb", [128, NE * E], BF16)
    ssb = sbt("ssb", [128, S], F32)
    psb = sbt("psb", [128, S], BF16)
    ptT = sbt("ptT", [128, 8 * 128], BF16)
    wst0 = sbt("wst0", [128, 512], F32)
    wst1 = sbt("wst1", [128, 512], F32)
    wqo = sbt("wqo", [128, 512], BF16)
    wto0 = sbt("wto0", [128, NE * 128], BF16)
    orw = sbt("orw", [128, E], F32)
    wtoA = nc.alloc_sbuf_tensor_at("wtoA", [128, 512], F32,
                                   offset=nc.lookup_mloc(ssb).addr)
    wtoB = nc.alloc_sbuf_tensor_at("wtoB", [128, 512], F32,
                                   offset=nc.lookup_mloc(ptT).addr)
    a_end = off[0]
    print("A region end:", a_end, "limit:", SB_LIMIT - SB_BASE)
    assert SB_BASE + a_end <= SB_LIMIT, f"SBUF over budget (A): {a_end}"

    off[0] = pers_end
    wgT = sbt("wgT_s", [128, NE * I], BF16)
    wuT = sbt("wuT_s", [128, NE * I], BF16)
    wdT = sbt("wdT_s", [128, NI * E], BF16)
    h2f = sbt("h2f", [128, E], F32)
    h2q = sbt("h2q", [128, E], BF16)
    hqT = sbt("hqT", [128, NI * 128], BF16)
    h2qT = sbt("h2qT", [128, E], BF16)
    hqsb = sbt("hqsb", [128, I], BF16)
    sgf = sbt("sgf", [128, 512], F32)
    hidf = sbt("hidf", [128, 512], F32)
    cbb = sbt("cbb", [128, E], F32)
    sbb = sbt("sbb", [128, E], F32)
    b_end = off[0]
    print("B region end:", b_end)
    assert SB_BASE + b_end <= SB_LIMIT, f"SBUF over budget (B): {b_end}"

    psS = nc.alloc_psum_tensor("psS", [128, 2048], F32)
    psO = nc.alloc_psum_tensor("psO", [128, 1024], F32)
    psT0 = nc.alloc_psum_tensor("psT0", [128, 1024], BF16)
    psT1 = nc.alloc_psum_tensor("psT1", [128, 1024], BF16)

    es = contextlib.ExitStack()
    P = Prog(nc)
    for chn in ("DS", "DG", "T", "V", "S", "G"):
        P.channel(chn, es.enter_context(nc.semaphore("ch_" + chn)))

    def dma(eng, dst, src, waits=()):
        ch = "DS" if eng == "sync" else "DG"
        return P.emit(eng, lambda E_, d=dst, s=src: E_.dma_start(d, s),
                      waits=waits, inc=ch, inc_k=16)

    def op(eng, fn, waits=(), sw=True):
        ch = {"vector": "V", "scalar": "S", "tensor": "T", "gpsimd": "G"}[eng]
        return P.emit(eng, fn, waits=waits, inc=ch, sw=sw)

    st = {}

    QP = dict(i=0, sc={}, sg={}, g={}, v={})

    def quant(xin, qout, W, tin, tax=()):
        nb = W // 16
        pi = QP["i"] & 1
        QP["i"] += 1
        ax, bbt, sg = qax[pi], qbb[pi], msg[pi]
        bm, r2, s5 = qbm[pi], qr2[pi], qs5[pi]
        scfree = QP["sc"].get(pi)
        sgfree = QP["sg"].get(pi)
        gfree = QP["g"].get(pi)
        tins = tin if isinstance(tin, tuple) else (tin,)
        tsa = op("scalar", lambda E_: E_.activation(
            ax[:, 0:W], xin, AF.Abs), waits=tins + (scfree,) + tuple(tax))
        tsg = op("scalar", lambda E_: E_.activation(
            sg[:, 0:W], xin, AF.Sign), waits=tins + (sgfree,) + tuple(tax), sw=False)
        tbm = op("vector", lambda E_: E_.tensor_reduce(
            bm[:, 0:nb].rearrange("p (n o) -> p n o", o=1),
            ax[:, 0:W].rearrange("p (n b) -> p n b", b=16), AX.X, ALU.max),
            waits=(tsa, gfree), sw=False)
        op("vector", lambda E_: E_.tensor_scalar(
            s5[:, 0:nb], bm[:, 0:nb], 6e-12, float(np.float32(1.0 / 12.0)),
            ALU.max, ALU.mult), waits=(tbm,), sw=True)
        tr2 = op("vector", lambda E_: E_.reciprocal(r2[:, 0:nb], s5[:, 0:nb]),
                 sw=True)
        tb = op("vector", lambda E_: E_.tensor_tensor(
            qbb[pi][:, 0:W].rearrange("p (n b) -> p n b", b=16),
            ax[:, 0:W].rearrange("p (n b) -> p n b", b=16),
            bcast16(r2[:, 0:nb], nb), ALU.mult), waits=(tr2, tsa), sw=True)
        QP["sc"][pi] = tb
        vprev = QP["v"].get(1 - pi)
        bbW = bbt[:, 0:W]
        op("vector", lambda E_: E_.tensor_scalar(m1[:, 0:W], bbW, 0.5, None, ALU.is_gt),
           waits=(tb, vprev), sw=True)
        op("vector", lambda E_: E_.tensor_scalar(m2[:, 0:W], bbW, 1.5, None, ALU.is_gt), sw=False)
        op("vector", lambda E_: E_.tensor_scalar(m4[:, 0:W], bbW, 2.5, None, ALU.is_gt), sw=False)
        op("vector", lambda E_: E_.tensor_scalar(m5[:, 0:W], bbW, 3.5, None, ALU.is_gt), sw=False)
        op("vector", lambda E_: E_.tensor_tensor(m3[:, 0:W], m1[:, 0:W], m2[:, 0:W], ALU.add), sw=False)
        op("vector", lambda E_: E_.tensor_tensor(m6[:, 0:W], m4[:, 0:W], m5[:, 0:W], ALU.add), sw=False)
        op("vector", lambda E_: E_.tensor_scalar(m1[:, 0:W], bbW, 5.0, None, ALU.is_gt), sw=False)
        op("vector", lambda E_: E_.tensor_scalar(m2[:, 0:W], bbW, 7.0, None, ALU.is_gt), sw=False)
        op("vector", lambda E_: E_.tensor_scalar(m4[:, 0:W], bbW, 10.0, None, ALU.is_gt), sw=False)
        op("vector", lambda E_: E_.tensor_tensor(m5[:, 0:W], m1[:, 0:W], m2[:, 0:W], ALU.add), sw=False)
        op("vector", lambda E_: E_.tensor_tensor(m2[:, 0:W], m3[:, 0:W], m6[:, 0:W], ALU.add), sw=False)
        op("vector", lambda E_: E_.scalar_tensor_tensor(
            m1[:, 0:W], m5[:, 0:W], 2.0, m2[:, 0:W], ALU.mult, ALU.add), sw=True)
        tq2 = op("vector", lambda E_: E_.scalar_tensor_tensor(
            m3[:, 0:W], m4[:, 0:W], 4.0, m1[:, 0:W], ALU.mult, ALU.add), sw=False)
        tsgm = op("vector", lambda E_: E_.tensor_tensor(
            m4[:, 0:W], m3[:, 0:W], sg[:, 0:W], ALU.mult), waits=(tq2, tsg), sw=False)
        QP["sg"][pi] = tsgm
        tq = op("vector", lambda E_: E_.tensor_tensor(
            qout.rearrange("p (n b) -> p n b", b=16),
            m4[:, 0:W].rearrange("p (n b) -> p n b", b=16),
            bcast16(s5[:, 0:nb], nb), ALU.mult), waits=(tsgm,), sw=True)
        QP["g"][pi] = tq
        QP["v"][pi] = tq
        st["q_prev"] = tq
        return tq, tsg

    tid = dma("sync", identb[:], identb_d[:])
    tidf = dma("sync", identf[:], identf_d[:])
    tmA = dma("sync", cmA[:], cmA_d[:])
    tmB = dma("sync", cmB[:], cmB_d[:])
    teps = dma("sync", epsc[:], epsc_d[:])
    tscc = dma("sync", scc[:], scc_d[:])
    tnb = dma("sync", nbias[:], nbias_d[:])
    tc = dma("sync", xs0[0:B, 0:E], c_in[:, :])

    tcs = op("scalar", lambda E_: E_.activation(xs1[0:B, 0:E], xs0[0:B, 0:E],
                                                AF.Silu), waits=(tc,))
    csT = xhb
    tr_last = None
    for c_ in range(NE):
        tr_last = op("tensor", lambda E_, c_=c_: E_.transpose(
            psO[:, c_ * B:(c_ + 1) * B], xs1[0:B, c_ * 128:(c_ + 1) * 128],
            identf[0:B, 0:B]), waits=(tcs, tidf))
    tev_cs = op("vector", lambda E_: E_.tensor_copy(
        csT[:, 0:NE * B], psO[:, 0:NE * B]), waits=(tr_last,))

    wmst = (wst0, wst1, wtoA, wtoB)
    wmbf = (m1, m2, m5, m6)
    NT6 = E6 // 512
    bufdep = {}
    castdep = {}
    mm_last = None
    tmod_done = None
    for nt in range(NT6):
        mm_slot = nt % 2
        evac_prev = st.get("modev%d" % mm_slot)
        for e in range(NE):
            bi = (nt * NE + e) % 4
            tld = dma("gpsimd" if bi % 2 == 0 else "sync", wmst[bi][:, 0:512],
                      wmod_in[e * 128:(e + 1) * 128, nt * 512:(nt + 1) * 512],
                      waits=(bufdep.get(bi),))
            tcast = op("vector", lambda E_, bi=bi: E_.tensor_copy(
                wmbf[bi][:, 0:512], wmst[bi][:, 0:512]), waits=(tld,))
            castdep[bi] = tcast
            mm_last = op("tensor", lambda E_, e=e, bi=bi, ms=mm_slot: E_.matmul(
                psO[0:B, ms * 512:(ms + 1) * 512], csT[:, e * B:(e + 1) * B],
                wmbf[bi][:, 0:512], start=(e == 0), stop=(e == NE - 1)),
                waits=(tcast, tev_cs, evac_prev))
            bufdep[bi] = mm_last
        lo = nt * 512
        need_clamp = (2 * E <= lo < 3 * E) or (5 * E <= lo < 6 * E)
        tev2 = op("vector", lambda E_, ms=mm_slot: E_.tensor_copy(
            ot0[0:B, 0:512], psO[0:B, ms * 512:(ms + 1) * 512]), waits=(mm_last,))
        if need_clamp:
            op("vector", lambda E_: E_.tensor_scalar(
                ot1[0:B, 0:512], ot0[0:B, 0:512], 0.25, None, ALU.max))
            op("vector", lambda E_: E_.tensor_scalar(
                oa[0:B, 0:512], ot0[0:B, 0:512], -0.25, None, ALU.min))
            op("vector", lambda E_: E_.tensor_scalar(
                xsq[0:B, 0:512], ot0[0:B, 0:512], 0.0, None, ALU.is_ge))
            op("vector", lambda E_: E_.tensor_tensor(
                xs1[0:B, 0:512], ot1[0:B, 0:512], oa[0:B, 0:512], ALU.subtract))
            op("vector", lambda E_: E_.tensor_tensor(
                ot1[0:B, 0:512], xs1[0:B, 0:512], xsq[0:B, 0:512], ALU.mult))
            tev2 = op("vector", lambda E_: E_.tensor_tensor(
                ot0[0:B, 0:512], oa[0:B, 0:512], ot1[0:B, 0:512], ALU.add))
        st["modev%d" % mm_slot] = tev2
        tmod_done = dma("sync", modd[:, nt * 512:(nt + 1) * 512],
                        ot0[0:B, 0:512], waits=(tev2,))
    st["p0cast"] = castdep.get(2)
    st["p0castB"] = castdep.get(3)

    half = (NE + 1) // 2
    wqstg, wkstg = qT, kT
    tstg = None
    for ep in range(NE):
        tstg = dma("gpsimd", wqstg[:, ep * E:(ep + 1) * E],
                   wq_in[ep * 128:(ep + 1) * 128, :])
        tstg = dma("gpsimd", wkstg[:, ep * E:(ep + 1) * E],
                   wk_in[ep * 128:(ep + 1) * 128, :])
    tstg_s = None
    mev = {}
    tM_done = None
    tM_mm = None
    for ec in range(NE):
        for dh in range(NHF):
            slot = ((ec * NHF + dh) % 4) * 512
            mm = None
            for ep in range(NE):
                mm = op("tensor", lambda E_, ec=ec, dh=dh, ep=ep, slot=slot: E_.matmul(
                    psS[:, slot:slot + 512],
                    wqstg[:, ep * E + ec * 128: ep * E + ec * 128 + 128],
                    wkstg[:, ep * E + dh * 512: ep * E + dh * 512 + 512],
                    start=(ep == 0), stop=(ep == NE - 1)),
                    waits=(tstg, tstg_s, mev.get(slot)))
            tM_mm = mm
            tM_done = op("scalar", lambda E_, ec=ec, dh=dh, slot=slot: E_.activation(
                Msb[:, ec * E + dh * 512: ec * E + dh * 512 + 512],
                psS[:, slot:slot + 512], AF.Identity), waits=(mm,), sw=False)
            mev[slot] = tM_done

    bufdep2 = {0: mm_last, 1: mm_last}
    prev_tr = (tid,)
    twv_done = None
    qkv_tr_prev = mm_last
    for n in range(NE):
        bi = n % 2
        stg = wst0 if bi == 0 else wst1
        tca = []
        tprev = None
        for pc in range(E // 512):
            cbf = m3 if pc % 2 == 0 else m4
            tldp = dma("gpsimd", stg[:, 0:512],
                       wv_in[n * 128:(n + 1) * 128, pc * 512:(pc + 1) * 512],
                       waits=(bufdep2.get(bi), tprev, mm_last))
            tcp = op("vector", lambda E_, cbf=cbf, stg=stg: E_.tensor_copy(
                cbf[:, 0:512], stg[:, 0:512]),
                waits=(tldp, mm_last, qkv_tr_prev))
            tprev = tcp
            tca.append(tcp)
        bufdep2[bi] = tca[-1]
        trk = []
        for e in range(NE):
            pc = (e * 128) // 512
            cbf = m3 if pc % 2 == 0 else m4
            ecol = (e * 128) % 512
            bank = psT0 if e < half else psT1
            slot = e % half
            tt = op("tensor", lambda E_, bank=bank, slot=slot, cbf=cbf, ecol=ecol: E_.transpose(
                bank[:, slot * 128:(slot + 1) * 128],
                cbf[:, ecol:ecol + 128], identb[:]),
                waits=(tca[pc], tid, tM_mm) + prev_tr)
            trk.append(tt)
        qkv_tr_prev = trk[-1]

        def _evv(E_, n=n, banki=0):
            bank = psT0 if banki == 0 else psT1
            e0 = banki * half
            cnt = half if banki == 0 else NE - half
            base = wvT[:]
            dst = bass.AP(base.tensor, base.offset + e0 * E + n * 128,
                          [list(base.ap[0]), [E, cnt], [1, 128]])
            src = bank[:, 0:cnt * 128].rearrange("p (c w) -> p c w", w=128)
            return E_.tensor_copy(dst, src)
        ev0 = op("vector", lambda E_, n=n: _evv(E_, n, 0), waits=(trk[half - 1],))
        ev1 = op("vector", lambda E_, n=n: _evv(E_, n, 1), waits=(trk[-1],))
        twv_done = ev1
        prev_tr = (ev0, ev1)
    st["psT_ev"] = prev_tr[-1]
    st["wqo_tr"] = (twv_done, qkv_tr_prev)

    wchunks = []
    for w_d, wT_d, R, C in ((wg_in, wgT_d, I, E), (wu_in, wuT_d, I, E)):
        for r in range(R // 128):
            for cc in range(C // 512):
                wchunks.append((w_d, wT_d, r, cc * 512, 512, R))
    c_list = []
    c0 = 0
    while c0 < I:
        w_ = min(512, I - c0)
        c_list.append((c0, w_))
        c0 += w_
    for r in range(E // 128):
        for (c0, w_) in c_list:
            wchunks.append((wd_in, wdT_d, r, c0, w_, E))

    wqs = dict(free0=(mm_last, bufdep2.get(0)), free1=(mm_last, bufdep2.get(1)),
               stfree={}, bufi=0)

    def emit_wchunk(idx):
        if idx >= len(wchunks):
            return
        w_d, wT_d, r, c0, W, R = wchunks[idx]
        bi = wqs["bufi"]
        wqs["bufi"] ^= 1
        stg = wst0 if bi == 0 else wst1
        fr = wqs["free%d" % bi]
        h = W // 2
        tld_a = dma("gpsimd", stg[:, 0:h],
                    w_d[r * 128:(r + 1) * 128, c0:c0 + h], waits=fr)
        tld_b = dma("sync", stg[:, h:W],
                    w_d[r * 128:(r + 1) * 128, c0 + h:c0 + W], waits=fr)
        tq, tsg = quant(stg[:, 0:W], wqo[:, 0:W], W, (tld_a, tld_b),
                        tax=st.get("wqo_tr", ()))
        wqs["free%d" % bi] = (tq, tsg)
        ntile = W // 128
        trk = []
        for e in range(ntile):
            bank = psT0 if e < 4 else psT1
            slot = e % 4
            tt = op("tensor", lambda E_, bank=bank, slot=slot, e=e: E_.transpose(
                bank[:, slot * 128:(slot + 1) * 128],
                wqo[:, e * 128:(e + 1) * 128], identb[:]),
                waits=(tq, st.get("psT_ev")))
            trk.append(tt)
        st["wqo_tr"] = (trk[-1],)
        wto = wto0
        stfree = wqs["stfree"].get(0)
        n0 = min(4, ntile)
        evs = [op("scalar", lambda E_, n0=n0, wto=wto: E_.activation(
            wto[:, 0:n0 * 128], psT0[:, 0:n0 * 128], AF.Identity),
            waits=(trk[n0 - 1], stfree), sw=False)]
        if ntile > 4:
            evs.append(op("scalar", lambda E_, ntile=ntile, wto=wto: E_.activation(
                wto[:, 4 * 128:ntile * 128], psT1[:, 0:(ntile - 4) * 128],
                AF.Identity), waits=(trk[-1], stfree), sw=False))
        st["psT_ev"] = evs[-1]
        dst = bass.AP(wT_d[:].tensor, c0 * R + r * 128,
                      [[R, 128], [128 * R, ntile], [1, 128]])
        tst = dma("gpsimd", dst,
                  wto[:, 0:ntile * 128].rearrange("p (c w) -> p c w", w=128),
                  waits=tuple(evs))
        wqs["stfree"][0] = tst

    WCH = [0]

    def next_wchunks(k):
        if not interleave_w:
            return
        for _ in range(k):
            if WCH[0] < len(wchunks):
                emit_wchunk(WCH[0])
                WCH[0] += 1

    ast = dict(x={}, xr={}, omst={}, tom=None, tga=None, sc_evac=None,
               texp=None, tmx=None, o_evac=None, ptr_last=None, t3=None,
               t5=None, xhb_tr=None, pv_last=None, trl=None,
               qslot={}, vslot={},
               qe={"q": None, "k": None, "v": None}, ptev={},
               ptT_mm={0: st["p0castB"], 1: st["p0castB"]},
               trlp={}, trmp={}, t5p={}, tm2p={}, t8p={}, cap=None)
    xm_last = None
    for s in range(B):
        t_car = dma("sync", oa[0:1, 0:E], modd[s:s + 1, E:2 * E],
                    waits=(tmod_done, ast["tga"]))
        t_sar = dma("sync", orw[0:1, 0:E], modd[s:s + 1, 0:E],
                    waits=(tmod_done, ast["tom"]))
        t_gab = dma("sync", gab[:],
                    bass.AP(modd[:].tensor, s * E6 + 2 * E, [[0, 128], [1, E]]),
                    waits=(tmod_done, ast["tga"]))
        trc = trs = None
        for c_ in range(NE):
            trc = op("tensor", lambda E_, c_=c_: E_.transpose(
                psO[:, c_:c_ + 1], oa[0:1, c_ * 128:(c_ + 1) * 128],
                identf[0:1, 0:1]), waits=(t_car, tidf, ast["o_evac"]))
            trs = op("tensor", lambda E_, c_=c_: E_.transpose(
                psO[:, 512 + c_:512 + c_ + 1], orw[0:1, c_ * 128:(c_ + 1) * 128],
                identf[0:1, 0:1]), waits=(t_sar,))
        t_cap = op("scalar", lambda E_: E_.activation(
            caP[:, 0:NE], psO[:, 0:NE], AF.Identity, bias=1.0), waits=(trc,))
        t_sap = op("scalar", lambda E_: E_.activation(
            saP[:, 0:NE], psO[:, 512:512 + NE], AF.Identity), waits=(trs,))
        ast["o_evac"] = t_sap
        ast["cap"] = (t_cap, t_sap)

        xst = (xs0, xs1)
        for g in range(NSG):
            h_last = None
            for j in range(4):
                ti = g * 4 + j
                bi = ti % 2
                t_ld = dma("sync", xst[bi][:], x_in[s, ti * 128:(ti + 1) * 128, :],
                           waits=(ast["x"].get(bi), tmod_done,
                                  ast["xr"].get(bi - 2) if bi >= 2 else None,
                                  tcs if s == 0 and g == 0 else None))
                so = 12 * (ti % 2)
                op("vector", lambda E_, bi=bi, so=so: E_.tensor_reduce(
                    stat[:, so:so + 1], xst[bi][:], AX.X, ALU.add),
                    waits=(t_ld, ast["t5p"].get(ti % 2)))
                t2 = op("scalar", lambda E_, bi=bi, so=so: E_.activation(
                    xsq[:], xst[bi][:], AF.Square, accum_out=stat[:, so + 1:so + 2]),
                    waits=(t_ld, ast["tm2p"].get(ti % 2), tmod_done,
                           ast["tom"]), sw=False)
                op("vector", lambda E_, so=so: E_.tensor_scalar(
                    stat[:, so + 2:so + 3], stat[:, so:so + 1], 1.0 / E, None,
                    ALU.mult))
                tm2 = op("vector", lambda E_, so=so: E_.tensor_scalar(
                    stat[:, so + 3:so + 4], stat[:, so + 1:so + 2], 1.0 / E, None,
                    ALU.mult), waits=(t2,))
                ast["tm2p"][ti % 2] = tm2
                op("vector", lambda E_, so=so: E_.tensor_tensor(
                    stat[:, so + 4:so + 5], stat[:, so + 2:so + 3],
                    stat[:, so + 2:so + 3], ALU.mult))
                t4 = op("vector", lambda E_, so=so: E_.tensor_tensor(
                    stat[:, so + 5:so + 6], stat[:, so + 3:so + 4],
                    stat[:, so + 4:so + 5], ALU.subtract))
                t5 = op("scalar", lambda E_, so=so: E_.activation(
                    stat[:, so + 6:so + 7], stat[:, so + 5:so + 6], AF.Sqrt,
                    bias=epsc[:, 0:1]), waits=(t4, teps))
                ast["t5p"][ti % 2] = t5
                t6 = op("vector", lambda E_, so=so: E_.reciprocal(
                    stat[:, so + 7:so + 8], stat[:, so + 6:so + 7]),
                    waits=(t5, ast["t8p"].get(ti % 2)))
                t7 = op("vector", lambda E_, so=so: E_.scalar_tensor_tensor(
                    stat[:, so + 8:so + 9], stat[:, so + 2:so + 3], -1.0,
                    stat[:, so + 7:so + 8], ALU.mult, ALU.mult))
                t8 = op("scalar", lambda E_, bi=bi, so=so: E_.activation(
                    xhb[:], xst[bi][:], AF.Identity, bias=stat[:, so + 8:so + 9],
                    scale=stat[:, so + 7:so + 8]),
                    waits=(t7, tmod_done, ast["xhb_tr"],
                           tev_cs if s == 0 and g == 0 else None), sw=False)
                ast["t8p"][ti % 2] = t8
                ast["x"][bi] = t8
                trk = []
                for e in range(NE):
                    bank = psT0 if e < half else psT1
                    slot = e % half
                    tt = op("tensor", lambda E_, bank=bank, slot=slot, e=e: E_.transpose(
                        bank[:, slot * 128:(slot + 1) * 128],
                        xhb[:, e * 128:(e + 1) * 128], identb[:]),
                        waits=(t8, tid, st.get("psT_ev")))
                    trk.append(tt)
                ast["xhb_tr"] = trk[-1]
                for e in range(NE):
                    bank = psT0 if e < half else psT1
                    slot = e % half
                    h_last = op("scalar", lambda E_, bank=bank, slot=slot, e=e, j=j, g=g: E_.activation(
                        kT[:, e * S + g * 512 + j * 128: e * S + g * 512 + (j + 1) * 128],
                        bank[:, slot * 128:(slot + 1) * 128],
                        AF.Identity, bias=saP[:, e:e + 1], scale=caP[:, e:e + 1]),
                        waits=(trk[e], ast["texp"], tM_mm) + ast["cap"],
                        sw=(e == 0 and j == 0))
                st["psT_ev"] = h_last
            for n in range(NE):
                base = (n % 2) * 512
                mm = None
                for e in range(NE):
                    mm = op("tensor", lambda E_, n=n, e=e, base=base, g=g: E_.matmul(
                        psS[:, base:base + 512],
                        Msb[:, e * E + n * 128: e * E + n * 128 + 128],
                        kT[:, e * S + g * 512: e * S + (g + 1) * 512],
                        start=(e == 0), stop=(e == NE - 1)),
                        waits=(h_last, tM_done, ast["qslot"].get(base)))
                ast["qe"]["q"] = op(
                    "scalar", lambda E_, n=n, g=g, base=base: E_.activation(
                        qT[:, n * S + g * 512: n * S + (g + 1) * 512],
                        psS[:, base:base + 512], AF.Identity),
                    waits=(mm, ast["pv_last"], tM_mm), sw=False)
                ast["qslot"][base] = ast["qe"]["q"]
            for j in range(4):
                ti = g * 4 + j
                for hf in range(NHF):
                    mm = None
                    for e in range(NE):
                        mm = op("tensor", lambda E_, j=j, hf=hf, e=e, g=g: E_.matmul(
                            psS[:, 1024 + hf * 512: 1024 + (hf + 1) * 512],
                            kT[:, e * S + g * 512 + j * 128: e * S + g * 512 + (j + 1) * 128],
                            wvT[:, e * E + hf * 512: e * E + (hf + 1) * 512],
                            start=(e == 0), stop=(e == NE - 1)),
                            waits=(h_last, twv_done, ast["vslot"].get(hf)))
                    ast["qe"]["v"] = op(
                        "scalar", lambda E_, ti=ti, hf=hf: E_.activation(
                            vsb[:, ti * E + hf * 512: ti * E + (hf + 1) * 512],
                            psS[:, 1024 + hf * 512: 1024 + (hf + 1) * 512],
                            AF.Identity),
                        waits=(mm, ast["pv_last"]), sw=False)
                    ast["vslot"][hf] = ast["qe"]["v"]
            next_wchunks(1)
        ast["qe"]["k"] = h_last

        ssbs = (ssb, ssb)

        def a_stage1(i):
            W = (i + 1) * 128
            NK4 = (W + 511) // 512
            dk4 = (i * 128) // 512
            doff = (i * 128) % 512
            sc_mm = None
            for e in range(NE):
                for k4 in range(NK4):
                    w_ = min(512, W - k4 * 512)
                    sc_mm = op("tensor", lambda E_, e=e, k4=k4, w_=w_, i=i: E_.matmul(
                        psS[:, k4 * 512:k4 * 512 + w_],
                        qT[:, e * S + i * 128:e * S + (i + 1) * 128],
                        kT[:, e * S + k4 * 512:e * S + k4 * 512 + w_],
                        start=(e == 0),
                        stop=(e == NE - 1 and k4 != dk4)),
                        waits=(ast["qe"]["q"], ast["qe"]["k"], ast["sc_evac"]))
            sc_mm = op("tensor", lambda E_, dk4=dk4, doff=doff: E_.matmul(
                psS[:, dk4 * 512 + doff: dk4 * 512 + doff + 128],
                cmA[:], cmB[:], start=False, stop=True),
                waits=(tmA, tmB))
            ev = None
            for k4 in range(NK4):
                w_ = min(512, W - k4 * 512)
                ev = op("scalar", lambda E_, k4=k4, w_=w_, i=i: E_.activation(
                    ssbs[i % 2][:, k4 * 512:k4 * 512 + w_],
                    psS[:, k4 * 512:k4 * 512 + w_],
                    AF.Identity, scale=scc[:, 0:1]),
                    waits=(sc_mm, tscc, st["p0cast"]), sw=False)
            ast["sc_evac"] = ev

        def a_stage2_front(i):
            pi2 = i % 2
            texp = op("scalar", lambda E_, W=(i + 1) * 128, i=i, pi2=pi2: E_.activation(
                psb[:, 0:W], ssbs[i % 2][:, 0:W], AF.Exp, bias=nbias[:, 0:1],
                accum_out=mx1[:, pi2 * 2:pi2 * 2 + 1]),
                waits=(tnb, ast["ptr_last"], ast["trlp"].get(pi2)), sw=False)
            ast["texp"] = texp
            trl = op("vector", lambda E_, pi2=pi2: E_.reciprocal(
                mx1[:, pi2 * 2 + 1:pi2 * 2 + 2], mx1[:, pi2 * 2:pi2 * 2 + 1]),
                waits=(texp, ast["trmp"].get(pi2)))
            ast["trl"] = trl
            ast["trlp"][pi2] = trl

        def a_stage2(i, s):
            W = (i + 1) * 128
            bi = i % 2
            xrb = xr0 if bi == 0 else xr1
            t_ldr = dma("sync", xrb[:], x_in[s, i * 128:(i + 1) * 128, :],
                        waits=(ast["xr"].get(bi), ast["x"].get(bi + 2),
                               t_car, t_sar, t_cap if bi == 0 else t_sap))
            texp = ast["texp"]
            trl = ast["trl"]
            pv_mm = None
            NKT = i + 1
            for g4 in range(0, NKT, 4):
                cnt = min(4, NKT - g4)
                pgrp = (g4 // 4) % 2
                bank = psT0 if pgrp == 0 else psT1
                trk = []
                for kk in range(cnt):
                    kt = g4 + kk
                    tt = op("tensor", lambda E_, bank=bank, kk=kk, kt=kt: E_.transpose(
                        bank[:, kk * 128:(kk + 1) * 128],
                        psb[:, kt * 128:(kt + 1) * 128], identb[:]),
                        waits=(texp, st.get("psT_ev"), ast["ptev"].get(pgrp)))
                    trk.append(tt)
                ast["ptr_last"] = trk[-1]
                tevp = op("scalar", lambda E_, bank=bank, cnt=cnt, pgrp=pgrp: E_.activation(
                    ptT[:, pgrp * 512: pgrp * 512 + cnt * 128], bank[:, 0:cnt * 128],
                    AF.Identity), waits=(trk[-1], ast["ptT_mm"].get(pgrp)), sw=False)
                st["psT_ev"] = tevp
                ast["ptev"][pgrp] = tevp
                for kk in range(cnt):
                    kt = g4 + kk
                    for hf in range(NHF):
                        pv_mm = op("tensor", lambda E_, kt=kt, kk=kk, hf=hf, pgrp=pgrp, i=i: E_.matmul(
                            psO[:, hf * 512:(hf + 1) * 512],
                            ptT[:, pgrp * 512 + kk * 128: pgrp * 512 + (kk + 1) * 128],
                            vsb[:, kt * E + hf * 512: kt * E + (hf + 1) * 512],
                            start=(kt == 0), stop=(kt == i)),
                            waits=(tevp, ast["qe"]["v"], ast["o_evac"]))
                ast["ptT_mm"][pgrp] = pv_mm
            ast["pv_last"] = pv_mm
            oev = None
            for hf in range(NHF):
                oev = op("scalar", lambda E_, hf=hf: E_.activation(
                    oa[:, hf * 512:(hf + 1) * 512], psO[:, hf * 512:(hf + 1) * 512],
                    AF.Identity), waits=(pv_mm, ast["tga"]), sw=False)
            ast["o_evac"] = oev
            tga = op("gpsimd", lambda E_: E_.tensor_tensor(
                xsq[:], oa[:], gab[:], ALU.mult), waits=(oev, t_gab))
            ast["tga"] = tga
            pi2 = i % 2
            _m = mx1[:]
            rsc = bass.AP(_m.tensor, _m.offset + pi2 * 2 + 1,
                          [list(_m.ap[0]), [0, E]])
            trm = op("gpsimd", lambda E_, rsc=rsc: E_.tensor_tensor(
                orw[:], xsq[:], rsc, ALU.mult), waits=(trl,))
            ast["trmp"][pi2] = trm
            obuf = ot0 if bi == 0 else ot1
            tom = op("gpsimd", lambda E_, obuf=obuf, xrb=xrb: E_.tensor_tensor(
                obuf[:], orw[:], xrb[:], ALU.add),
                waits=(trm, t_ldr, ast["omst"].get(bi)))
            ast["tom"] = tom
            ast["xr"][bi] = tom
            xm = dma("sync", xmid_d[s, i * 128:(i + 1) * 128, :], obuf[:],
                     waits=(tom,))
            ast["omst"][bi] = xm
            next_wchunks(2)
            return xm

        a_stage1(0)
        for i in range(NS):
            a_stage2_front(i)
            if i + 1 < NS:
                a_stage1(i + 1)
            xm_last = a_stage2(i, s)

    while WCH[0] < len(wchunks):
        emit_wchunk(WCH[0])
        WCH[0] += 1

    barrier = tuple(x for x in (xm_last, wqs["stfree"].get(0),
                                wqs["stfree"].get(1)) if x is not None)
    t_wB = None
    for e in range(NE):
        t_wB = dma("gpsimd", wgT[:, e * I:(e + 1) * I],
                   wgT_d[e * 128:(e + 1) * 128, :], waits=barrier)
    for e in range(NE):
        t_wB = dma("gpsimd", wuT[:, e * I:(e + 1) * I],
                   wuT_d[e * 128:(e + 1) * 128, :], waits=barrier)
    for ic in range(NI):
        t_wB = dma("gpsimd", wdT[:, ic * E:(ic + 1) * E],
                   wdT_d[ic * 128:(ic + 1) * 128, :], waits=barrier)

    bst = dict(x={}, ost={}, tout=None, tgb=barrier[0],
               t5p={}, tm2p={}, t8p={},
               h2q_tr=(), hq_tr=(), gate_mm=None, mmd=None, thd=None,
               tsg=None, t9=None, sl={0: None, 1: None})
    itiles = c_list

    def b_back(s, i, bi, xb_, mmd, t_gbb):
        tgb = op("vector", lambda E_: E_.tensor_tensor(
            oa[:], psO[:, 0:E], gab[:], ALU.mult),
            waits=(mmd, t_gbb, bst["tout"]), sw=False)
        bst["tgb"] = tgb
        obuf = ot0 if bi == 0 else ot1
        tout = op("gpsimd", lambda E_, obuf=obuf, xb_=xb_: E_.tensor_tensor(
            obuf[:], oa[:], xb_[:], ALU.add),
            waits=(tgb, bst["ost"].get(bi)))
        bst["tout"] = tout
        bst["x"][bi] = tout
        tst = dma("gpsimd", out_d[s, i * 128:(i + 1) * 128, :], obuf[:],
                  waits=(tout,))
        bst["ost"][bi] = tst

    pend = None
    for s in range(B):
        if pend is not None:
            b_back(*pend)
            pend = None
        t_cbr = dma("sync", h2f[:],
                    bass.AP(modd[:].tensor, s * E6 + 4 * E, [[0, 128], [1, E]]),
                    waits=barrier + (bst["t9"], bst["tsg"]))
        t_cb1 = op("vector", lambda E_: E_.tensor_scalar(
            cbb[:], h2f[:], 1.0, None, ALU.add), waits=(t_cbr, bst["t9"]))
        t_sbb = dma("sync", sbb[:],
                    bass.AP(modd[:].tensor, s * E6 + 3 * E, [[0, 128], [1, E]]),
                    waits=barrier + (bst["t9"],))
        t_gbb = dma("sync", gab[:],
                    bass.AP(modd[:].tensor, s * E6 + 5 * E, [[0, 128], [1, E]]),
                    waits=barrier + (ast["tga"], bst["tgb"]))
        for i in range(NS):
            bi = i % 2
            xb_ = xs0 if bi == 0 else xs1
            t_ld = dma("sync", xb_[:], xmid_d[s, i * 128:(i + 1) * 128, :],
                       waits=(bst["x"].get(bi), xm_last))
            so = 12 * (i % 2)
            op("vector", lambda E_, xb_=xb_, so=so: E_.tensor_reduce(
                stat[:, so:so + 1], xb_[:], AX.X, ALU.add),
                waits=(t_ld, bst["t5p"].get(bi)))
            t2 = op("scalar", lambda E_, xb_=xb_, so=so: E_.activation(
                xsq[:], xb_[:], AF.Square, accum_out=stat[:, so + 1:so + 2]),
                waits=(t_ld, bst["tm2p"].get(bi), ast["tga"], bst["t9"]), sw=False)
            op("vector", lambda E_, so=so: E_.tensor_scalar(
                stat[:, so + 2:so + 3], stat[:, so:so + 1], 1.0 / E, None,
                ALU.mult))
            tm2 = op("vector", lambda E_, so=so: E_.tensor_scalar(
                stat[:, so + 3:so + 4], stat[:, so + 1:so + 2], 1.0 / E, None,
                ALU.mult), waits=(t2,))
            bst["tm2p"][bi] = tm2
            op("vector", lambda E_, so=so: E_.tensor_tensor(
                stat[:, so + 4:so + 5], stat[:, so + 2:so + 3],
                stat[:, so + 2:so + 3], ALU.mult))
            t4 = op("vector", lambda E_, so=so: E_.tensor_tensor(
                stat[:, so + 5:so + 6], stat[:, so + 3:so + 4],
                stat[:, so + 4:so + 5], ALU.subtract))
            t5 = op("scalar", lambda E_, so=so: E_.activation(
                stat[:, so + 6:so + 7], stat[:, so + 5:so + 6], AF.Sqrt,
                bias=epsc[:, 0:1]), waits=(t4, teps))
            bst["t5p"][bi] = t5
            t6 = op("vector", lambda E_, so=so: E_.reciprocal(
                stat[:, so + 7:so + 8], stat[:, so + 6:so + 7]),
                waits=(t5, bst["t8p"].get(bi)))
            t7 = op("vector", lambda E_, so=so: E_.scalar_tensor_tensor(
                stat[:, so + 8:so + 9], stat[:, so + 2:so + 3], -1.0,
                stat[:, so + 7:so + 8], ALU.mult, ALU.mult))
            t8 = op("scalar", lambda E_, xb_=xb_, so=so: E_.activation(
                h2f[:], xb_[:], AF.Identity, bias=stat[:, so + 8:so + 9],
                scale=stat[:, so + 7:so + 8]),
                waits=(t7, t_cb1, bst["tsg"]), sw=False)
            bst["t8p"][bi] = t8
            t9a = op("gpsimd", lambda E_: E_.tensor_tensor(
                xsq[:], h2f[:], cbb[:], ALU.mult), waits=(t8, t_cb1))
            t9 = op("gpsimd", lambda E_: E_.tensor_tensor(
                h2f[:], xsq[:], sbb[:], ALU.add), waits=(t_sbb,))
            bst["t9"] = t9
            tqs = None
            for hc in range(E // 512):
                tqs, tsgh2 = quant(h2f[:, hc * 512:(hc + 1) * 512],
                                   h2q[:, hc * 512:(hc + 1) * 512], 512, t9,
                                   tax=bst["h2q_tr"])
                bst["tsg"] = tsgh2
            tq = tqs
            trk = []
            for e in range(NE):
                bank = psT0 if e < half else psT1
                slot = e % half
                tt = op("tensor", lambda E_, bank=bank, slot=slot, e=e: E_.transpose(
                    bank[:, slot * 128:(slot + 1) * 128],
                    h2q[:, e * 128:(e + 1) * 128], identb[:]),
                    waits=(tq, st.get("psT_ev"), tid))
                trk.append(tt)
            bst["h2q_tr"] = (trk[-1],)
            tev0 = op("scalar", lambda E_: E_.activation(
                h2qT[:, 0:half * 128], psT0[:, 0:half * 128], AF.Identity),
                waits=(trk[half - 1], bst["gate_mm"]), sw=False)
            tev1 = op("scalar", lambda E_: E_.activation(
                h2qT[:, half * 128:NE * 128], psT1[:, 0:(NE - half) * 128],
                AF.Identity), waits=(trk[-1], bst["gate_mm"]), sw=False)
            st["psT_ev"] = tev1
            th2qT = tev1
            sltoks = []
            for it, (c0, w_) in enumerate(itiles):
                sl = it % 2
                gslot = 0 if sl == 0 else 1024
                uslot = 512 if sl == 0 else 1536
                mmg = None
                for e in range(NE):
                    mmg = op("tensor", lambda E_, e=e, c0=c0, w_=w_, gslot=gslot: E_.matmul(
                        psS[:, gslot:gslot + w_], h2qT[:, e * 128:(e + 1) * 128],
                        wgT[:, e * I + c0:e * I + c0 + w_],
                        start=(e == 0), stop=(e == NE - 1)),
                        waits=(th2qT, t_wB, bst["sl"][sl]))
                for e in range(NE):
                    mmg = op("tensor", lambda E_, e=e, c0=c0, w_=w_, uslot=uslot: E_.matmul(
                        psS[:, uslot:uslot + w_], h2qT[:, e * 128:(e + 1) * 128],
                        wuT[:, e * I + c0:e * I + c0 + w_],
                        start=(e == 0), stop=(e == NE - 1)))
                bst["gate_mm"] = mmg
                tsl = op("scalar", lambda E_, gslot=gslot, w_=w_: E_.activation(
                    sgf[:, 0:w_], psS[:, gslot:gslot + w_], AF.Silu),
                    waits=(mmg, bst["thd"]), sw=False)
                thd = op("vector", lambda E_, uslot=uslot, w_=w_: E_.tensor_tensor(
                    hidf[:, 0:w_], sgf[:, 0:w_], psS[:, uslot:uslot + w_], ALU.mult),
                    waits=(tsl, bst["sl"][sl], bst["tsg"]), sw=False)
                bst["thd"] = thd
                hq_dst = bass.AP(hqsb[:].tensor, hqsb[:].offset + c0,
                                 [list(hqsb[:].ap[0]), [1, w_]])
                tqh, tsg_h = quant(hidf[:, 0:w_], hq_dst, w_, thd,
                                   tax=bst["hq_tr"])
                bst["sl"][sl] = tqh
                bst["tsg"] = tsg_h
                sltoks.append(tqh)
            trk2 = None
            for icb in range(0, NI, 4):
                cnt = min(4, NI - icb)
                bank = psT0 if (icb // 4) % 2 == 0 else psT1
                grp = []
                for kk in range(cnt):
                    trk2 = op("tensor", lambda E_, bank=bank, kk=kk, icb=icb: E_.transpose(
                        bank[:, kk * 128:(kk + 1) * 128],
                        hqsb[:, (icb + kk) * 128:(icb + kk + 1) * 128], identb[:]),
                        waits=(sltoks[icb // 4], st.get("psT_ev")))
                    grp.append(trk2)
                tevh = op("scalar", lambda E_, bank=bank, cnt=cnt, icb=icb: E_.activation(
                    hqT[:, icb * 128:(icb + cnt) * 128], bank[:, 0:cnt * 128],
                    AF.Identity), waits=(grp[-1], bst["mmd"]), sw=False)
                st["psT_ev"] = tevh
            bst["hq_tr"] = (trk2,)
            thqT = st["psT_ev"]
            if pend is not None:
                b_back(*pend)
            mmd = None
            for ic in range(NI):
                for hf in range(NHF):
                    mmd = op("tensor", lambda E_, ic=ic, hf=hf: E_.matmul(
                        psO[:, hf * 512:(hf + 1) * 512],
                        hqT[:, ic * 128:(ic + 1) * 128],
                        wdT[:, ic * E + hf * 512: ic * E + (hf + 1) * 512],
                        start=(ic == 0), stop=(ic == NI - 1)),
                        waits=(thqT, t_wB, bst["tgb"]))
            bst["mmd"] = mmd
            pend = (s, i, bi, xb_, mmd, t_gbb)
    b_back(*pend)

    final_ds = Tok("DS", P.chan["DS"][1], "sync", True)
    final_dg = Tok("DG", P.chan["DG"][1], "gpsimd", True)
    P.wait_only("sync", (final_ds,))
    P.wait_only("gpsimd", (final_dg,))

    with nc.Block() as block:
        @block.sync
        def _(sync):
            P.run(sync, "sync")

        @block.vector
        def _(vector):
            P.run(vector, "vector")

        @block.scalar
        def _(scalar):
            P.run(scalar, "scalar")

        @block.tensor
        def _(tensor):
            P.run(tensor, "tensor")

        @block.gpsimd
        def _(gpsimd):
            P.run(gpsimd, "gpsimd")

    es.close()
    return nc


_NC_CACHE = {}


def kernel(L0_attn_x, L0_attn_WQ, L0_attn_WK, L0_attn_WV,
           L0_ffn_w_gate, L0_ffn_w_up, L0_ffn_w_down, L0_c, L0_wmod):
    cfg = FULL_CFG
    n = cfg["NCORES"]
    bpc = L0_attn_x.shape[0] // n
    if "full" not in _NC_CACHE:
        _NC_CACHE["full"] = build(cfg)
    nc = _NC_CACHE["full"]
    x = np.ascontiguousarray(L0_attn_x)
    c = np.ascontiguousarray(L0_c)
    shared = {
        "wq_in": np.ascontiguousarray(L0_attn_WQ),
        "wk_in": np.ascontiguousarray(L0_attn_WK),
        "wv_in": np.ascontiguousarray(L0_attn_WV),
        "wg_in": np.ascontiguousarray(L0_ffn_w_gate),
        "wu_in": np.ascontiguousarray(L0_ffn_w_up),
        "wd_in": np.ascontiguousarray(L0_ffn_w_down),
        "wmod_in": np.ascontiguousarray(L0_wmod),
    }
    in_maps = []
    for ci in range(n):
        m = dict(shared)
        m["x_in"] = np.ascontiguousarray(x[ci * bpc:(ci + 1) * bpc])
        m["c_in"] = np.ascontiguousarray(c[ci * bpc:(ci + 1) * bpc])
        in_maps.append(m)
    res = run_bass_kernel_spmd(nc, in_maps, core_ids=list(range(n)))
    return np.concatenate([np.asarray(r["out"]) for r in res.results], axis=0)


# version 13
# speedup vs baseline: 1.0110x; 1.0041x over previous
import contextlib
import numpy as np
import ml_dtypes
import concourse.bass as bass
import concourse.mybir as mybir
from concourse.bass_utils import run_bass_kernel_spmd

F32 = mybir.dt.float32
BF16 = mybir.dt.bfloat16
AF = mybir.ActivationFunctionType
ALU = mybir.AluOpType
AX = mybir.AxisListType

ENGINES = ("sync", "vector", "scalar", "tensor", "gpsimd")

FULL_CFG = dict(B=4, S=2048, E=1024, I=2816, NCORES=8)


class Tok:
    __slots__ = ("ch", "val", "eng", "dma")

    def __init__(self, ch, val, eng, dma):
        self.ch, self.val, self.eng, self.dma = ch, val, eng, dma


class Prog:

    def __init__(self, nc):
        self.nc = nc
        self.streams = {e: [] for e in ENGINES}
        self.chan = {}
        self.water = {}

    def channel(self, name, sem):
        self.chan[name] = [sem, 0]

    CHAINED = ("vector", "scalar", "gpsimd")

    def emit(self, eng, fn, waits=(), inc=None, inc_k=1, sw=True):
        wl = []
        if sw and eng in self.CHAINED and inc is not None:
            cur = self.chan[inc][1]
            key = (eng, inc)
            if cur > 0 and self.water.get(key, -1) < cur:
                self.water[key] = cur
                wl.append((inc, cur))
        for t in waits:
            if t is None:
                continue
            if (not t.dma) and t.eng == eng:
                continue
            key = (eng, t.ch)
            if self.water.get(key, -1) >= t.val:
                continue
            self.water[key] = t.val
            wl.append((t.ch, t.val))
        tok = None
        if inc is not None:
            c = self.chan[inc]
            c[1] += inc_k
            tok = Tok(inc, c[1], eng, inc_k >= 16)
        self.streams[eng].append((wl, fn, inc, inc_k))
        return tok

    def wait_only(self, eng, waits):
        self.emit(eng, None, waits=waits)

    def run(self, engobj, eng):
        for wl, fn, inc, inc_k in self.streams[eng]:
            for ch, val in wl:
                engobj.wait_ge(self.chan[ch][0], val)
            if fn is None:
                continue
            ins = fn(engobj)
            if inc is not None:
                ins.then_inc(self.chan[inc][0], inc_k)


def bcast16(ap2d, nblk, blk=16):
    return bass.AP(ap2d.tensor, ap2d.offset, [list(ap2d.ap[0]), [1, nblk], [0, blk]])


def build(cfg, debug=False, interleave_w=True):
    B, S, E, I = cfg["B"], cfg["S"], cfg["E"], cfg["I"]
    NE, NS, NI = E // 128, S // 128, I // 128
    NSG = S // 512
    NHF = E // 512
    SCALE = float(E) ** -0.5
    E6 = 6 * E
    assert S % 512 == 0 and E % 512 == 0 and I % 128 == 0

    nc = bass.Bass("TRN2", target_bir_lowering=False, detect_race_conditions=False)

    x_in = nc.dram_tensor("x_in", [B, S, E], F32, kind="ExternalInput")
    wq_in = nc.dram_tensor("wq_in", [E, E], F32, kind="ExternalInput")
    wk_in = nc.dram_tensor("wk_in", [E, E], F32, kind="ExternalInput")
    wv_in = nc.dram_tensor("wv_in", [E, E], F32, kind="ExternalInput")
    wg_in = nc.dram_tensor("wg_in", [I, E], F32, kind="ExternalInput")
    wu_in = nc.dram_tensor("wu_in", [I, E], F32, kind="ExternalInput")
    wd_in = nc.dram_tensor("wd_in", [E, I], F32, kind="ExternalInput")
    c_in = nc.dram_tensor("c_in", [B, E], F32, kind="ExternalInput")
    wmod_in = nc.dram_tensor("wmod_in", [E, E6], F32, kind="ExternalInput")
    out_d = nc.dram_tensor("out", [B, S, E], F32, kind="ExternalOutput")

    _k = dict(kind="ExternalOutput") if debug else {}
    xmid_d = nc.dram_tensor("xmid_d", [B, S, E], F32, **_k)
    modd = nc.dram_tensor("modd", [B, E6], F32, **_k)
    qkT_d = nc.dram_tensor("qkT_d", [2 * NE * NE * 128, 128], BF16, **_k)
    wgT_d = nc.dram_tensor("wgT_d", [E, I], BF16, **_k)
    wuT_d = nc.dram_tensor("wuT_d", [E, I], BF16, **_k)
    wdT_d = nc.dram_tensor("wdT_d", [I, E], BF16, **_k)

    identb_d = nc.inline_tensor(np.eye(128, dtype=ml_dtypes.bfloat16), name="identb_c")
    identf_d = nc.inline_tensor(np.eye(128, dtype=np.float32), name="identf_c")
    _c = np.float32(np.sqrt(1e29))
    _ka = (-_c * (np.arange(128)[:, None] > np.arange(128)[None, :])
           ).astype(ml_dtypes.bfloat16)
    _kb = (_c * (np.arange(128)[:, None] <= np.arange(128)[None, :])
           ).astype(ml_dtypes.bfloat16)
    cmA_d = nc.inline_tensor(_ka, name="cmA_c")
    cmB_d = nc.inline_tensor(_kb, name="cmB_c")
    epsc_d = nc.inline_tensor(np.full((128, 1), 1e-5, dtype=np.float32), name="epsc_c")
    scc_d = nc.inline_tensor(np.full((128, 1), SCALE, dtype=np.float32), name="scc_c")
    nbias_d = nc.inline_tensor(np.full((128, 1), -12.0, dtype=np.float32),
                               name="nbias_c")

    SB_BASE = (nc.sbuf_base + 63) // 64 * 64
    SB_LIMIT = nc.sbuf_base + nc.sbuf_bytes_remaining
    off = [0]

    def sbt(name, shape, dtype, align=64):
        per = int(np.prod(shape[1:])) * mybir.dt.size(dtype)
        off[0] = (off[0] + align - 1) // align * align
        h = nc.alloc_sbuf_tensor_at(name, shape, dtype, offset=SB_BASE + off[0])
        off[0] += per
        return h

    identb = sbt("identb", [128, 128], BF16)
    identf = sbt("identf", [128, 128], F32)
    cmA = sbt("cmA", [128, 128], BF16)
    cmB = sbt("cmB", [128, 128], BF16)
    epsc = sbt("epsc", [128, 1], F32)
    scc = sbt("scc", [128, 1], F32)
    nbias = sbt("nbias", [128, 1], F32)
    xs0 = sbt("xs0", [128, E], F32)
    xs1 = sbt("xs1", [128, E], F32)
    ot0 = sbt("ot0", [128, E], F32)
    ot1 = sbt("ot1", [128, E], F32)
    oa = sbt("oa", [128, E], F32)
    xsq = sbt("xsq", [128, E], F32)
    gab = sbt("gab", [128, E], F32)
    caP = sbt("caP", [128, NE], F32)
    saP = sbt("saP", [128, NE], F32)
    stat = sbt("stat", [128, 24], F32)
    mx1 = sbt("mx1", [128, 4], F32)
    qax = (sbt("qax0", [128, 512], F32), sbt("qax1", [128, 512], F32))
    qbb = (sbt("qbb0", [128, 512], BF16), sbt("qbb1", [128, 512], BF16))
    msg = (sbt("msg0", [128, 512], BF16), sbt("msg1", [128, 512], BF16))
    m1 = sbt("m1", [128, 512], BF16)
    m2 = sbt("m2", [128, 512], BF16)
    m3 = sbt("m3", [128, 512], BF16)
    m4 = sbt("m4", [128, 512], BF16)
    m5 = sbt("m5", [128, 512], BF16)
    m6 = sbt("m6", [128, 512], BF16)
    qbm = (sbt("qbm0", [128, 32], F32), sbt("qbm1", [128, 32], F32))
    qr2 = (sbt("qr20", [128, 32], F32), sbt("qr21", [128, 32], F32))
    qs5 = (sbt("qs50", [128, 32], F32), sbt("qs51", [128, 32], F32))

    pers_end = off[0]
    xr0 = sbt("xr0", [128, E], F32)
    xr1 = sbt("xr1", [128, E], F32)
    xhb = sbt("xhb", [128, E], BF16)
    qT = sbt("qT", [128, NE * S], BF16)
    kT = sbt("kT", [128, NE * S], BF16)
    vsb = sbt("vsb", [128, NS * E], BF16)
    wvT = sbt("wvT", [128, NE * E], BF16)
    Msb = sbt("Msb", [128, NE * E], BF16)
    ssb = sbt("ssb", [128, S], F32)
    psb = sbt("psb", [128, S], BF16)
    ptT = sbt("ptT", [128, 8 * 128], BF16)
    wst0 = sbt("wst0", [128, 512], F32)
    wst1 = sbt("wst1", [128, 512], F32)
    wqo = sbt("wqo", [128, 512], BF16)
    wto0 = sbt("wto0", [128, NE * 128], BF16)
    orw = sbt("orw", [128, E], F32)
    wtoA = nc.alloc_sbuf_tensor_at("wtoA", [128, 512], F32,
                                   offset=nc.lookup_mloc(ssb).addr)
    wtoB = nc.alloc_sbuf_tensor_at("wtoB", [128, 512], F32,
                                   offset=nc.lookup_mloc(ptT).addr)
    a_end = off[0]
    print("A region end:", a_end, "limit:", SB_LIMIT - SB_BASE)
    assert SB_BASE + a_end <= SB_LIMIT, f"SBUF over budget (A): {a_end}"

    off[0] = pers_end
    wgT = sbt("wgT_s", [128, NE * I], BF16)
    wuT = sbt("wuT_s", [128, NE * I], BF16)
    wdT = sbt("wdT_s", [128, NI * E], BF16)
    h2f = sbt("h2f", [128, E], F32)
    h2q = sbt("h2q", [128, E], BF16)
    hqT = sbt("hqT", [128, NI * 128], BF16)
    h2qT = sbt("h2qT", [128, E], BF16)
    hqsb = sbt("hqsb", [128, I], BF16)
    sgf = sbt("sgf", [128, 512], F32)
    hidf = sbt("hidf", [128, 512], F32)
    cbb = sbt("cbb", [128, E], F32)
    sbb = sbt("sbb", [128, E], F32)
    b_end = off[0]
    print("B region end:", b_end)
    assert SB_BASE + b_end <= SB_LIMIT, f"SBUF over budget (B): {b_end}"

    psS = nc.alloc_psum_tensor("psS", [128, 2048], F32)
    psO = nc.alloc_psum_tensor("psO", [128, 1024], F32)
    psT0 = nc.alloc_psum_tensor("psT0", [128, 1024], BF16)
    psT1 = nc.alloc_psum_tensor("psT1", [128, 1024], BF16)

    es = contextlib.ExitStack()
    P = Prog(nc)
    for chn in ("DS", "DG", "T", "V", "S", "G"):
        P.channel(chn, es.enter_context(nc.semaphore("ch_" + chn)))

    def dma(eng, dst, src, waits=()):
        ch = "DS" if eng == "sync" else "DG"
        return P.emit(eng, lambda E_, d=dst, s=src: E_.dma_start(d, s),
                      waits=waits, inc=ch, inc_k=16)

    def op(eng, fn, waits=(), sw=True):
        ch = {"vector": "V", "scalar": "S", "tensor": "T", "gpsimd": "G"}[eng]
        return P.emit(eng, fn, waits=waits, inc=ch, sw=sw)

    st = {}

    QP = dict(i=0, sc={}, sg={}, g={}, v={})

    def quant(xin, qout, W, tin, tax=()):
        nb = W // 16
        pi = QP["i"] & 1
        QP["i"] += 1
        ax, bbt, sg = qax[pi], qbb[pi], msg[pi]
        bm, r2, s5 = qbm[pi], qr2[pi], qs5[pi]
        scfree = QP["sc"].get(pi)
        sgfree = QP["sg"].get(pi)
        gfree = QP["g"].get(pi)
        tins = tin if isinstance(tin, tuple) else (tin,)
        tsa = op("scalar", lambda E_: E_.activation(
            ax[:, 0:W], xin, AF.Abs), waits=tins + (scfree,) + tuple(tax))
        tsg = op("scalar", lambda E_: E_.activation(
            sg[:, 0:W], xin, AF.Sign), waits=tins + (sgfree,) + tuple(tax), sw=False)
        tbm = op("vector", lambda E_: E_.tensor_reduce(
            bm[:, 0:nb].rearrange("p (n o) -> p n o", o=1),
            ax[:, 0:W].rearrange("p (n b) -> p n b", b=16), AX.X, ALU.max),
            waits=(tsa, gfree), sw=False)
        op("vector", lambda E_: E_.tensor_scalar(
            s5[:, 0:nb], bm[:, 0:nb], 6e-12, float(np.float32(1.0 / 12.0)),
            ALU.max, ALU.mult), waits=(tbm,), sw=True)
        tr2 = op("vector", lambda E_: E_.reciprocal(r2[:, 0:nb], s5[:, 0:nb]),
                 sw=True)
        tb = op("vector", lambda E_: E_.tensor_tensor(
            qbb[pi][:, 0:W].rearrange("p (n b) -> p n b", b=16),
            ax[:, 0:W].rearrange("p (n b) -> p n b", b=16),
            bcast16(r2[:, 0:nb], nb), ALU.mult), waits=(tr2, tsa), sw=True)
        QP["sc"][pi] = tb
        vprev = QP["v"].get(1 - pi)
        bbW = bbt[:, 0:W]
        op("vector", lambda E_: E_.tensor_scalar(m1[:, 0:W], bbW, 0.5, None, ALU.is_gt),
           waits=(tb, vprev), sw=True)
        op("vector", lambda E_: E_.tensor_scalar(m2[:, 0:W], bbW, 1.5, None, ALU.is_gt), sw=False)
        op("vector", lambda E_: E_.tensor_scalar(m4[:, 0:W], bbW, 2.5, None, ALU.is_gt), sw=False)
        op("vector", lambda E_: E_.tensor_scalar(m5[:, 0:W], bbW, 3.5, None, ALU.is_gt), sw=False)
        op("vector", lambda E_: E_.tensor_tensor(m3[:, 0:W], m1[:, 0:W], m2[:, 0:W], ALU.add), sw=False)
        op("vector", lambda E_: E_.tensor_tensor(m6[:, 0:W], m4[:, 0:W], m5[:, 0:W], ALU.add), sw=True)
        op("vector", lambda E_: E_.tensor_scalar(m1[:, 0:W], bbW, 5.0, None, ALU.is_gt), sw=False)
        op("vector", lambda E_: E_.tensor_scalar(m2[:, 0:W], bbW, 7.0, None, ALU.is_gt), sw=False)
        op("vector", lambda E_: E_.tensor_scalar(m4[:, 0:W], bbW, 10.0, None, ALU.is_gt), sw=False)
        op("vector", lambda E_: E_.tensor_tensor(m5[:, 0:W], m1[:, 0:W], m2[:, 0:W], ALU.add), sw=True)
        op("vector", lambda E_: E_.tensor_tensor(m2[:, 0:W], m3[:, 0:W], m6[:, 0:W], ALU.add), sw=False)
        op("vector", lambda E_: E_.scalar_tensor_tensor(
            m1[:, 0:W], m5[:, 0:W], 2.0, m2[:, 0:W], ALU.mult, ALU.add), sw=True)
        tq2 = op("vector", lambda E_: E_.scalar_tensor_tensor(
            m3[:, 0:W], m4[:, 0:W], 4.0, m1[:, 0:W], ALU.mult, ALU.add), sw=False)
        tsgm = op("vector", lambda E_: E_.tensor_tensor(
            m4[:, 0:W], m3[:, 0:W], sg[:, 0:W], ALU.mult), waits=(tq2, tsg), sw=False)
        QP["sg"][pi] = tsgm
        tq = op("vector", lambda E_: E_.tensor_tensor(
            qout.rearrange("p (n b) -> p n b", b=16),
            m4[:, 0:W].rearrange("p (n b) -> p n b", b=16),
            bcast16(s5[:, 0:nb], nb), ALU.mult), waits=(tsgm,), sw=True)
        QP["g"][pi] = tq
        QP["v"][pi] = tq
        st["q_prev"] = tq
        return tq, tsg

    tid = dma("sync", identb[:], identb_d[:])
    tidf = dma("sync", identf[:], identf_d[:])
    tmA = dma("sync", cmA[:], cmA_d[:])
    tmB = dma("sync", cmB[:], cmB_d[:])
    teps = dma("sync", epsc[:], epsc_d[:])
    tscc = dma("sync", scc[:], scc_d[:])
    tnb = dma("sync", nbias[:], nbias_d[:])
    tc = dma("sync", xs0[0:B, 0:E], c_in[:, :])

    tcs = op("scalar", lambda E_: E_.activation(xs1[0:B, 0:E], xs0[0:B, 0:E],
                                                AF.Silu), waits=(tc,))
    csT = xhb
    tr_last = None
    for c_ in range(NE):
        tr_last = op("tensor", lambda E_, c_=c_: E_.transpose(
            psO[:, c_ * B:(c_ + 1) * B], xs1[0:B, c_ * 128:(c_ + 1) * 128],
            identf[0:B, 0:B]), waits=(tcs, tidf))
    tev_cs = op("vector", lambda E_: E_.tensor_copy(
        csT[:, 0:NE * B], psO[:, 0:NE * B]), waits=(tr_last,))

    wmst = (wst0, wst1, wtoA, wtoB)
    wmbf = (m1, m2, m5, m6)
    NT6 = E6 // 512
    bufdep = {}
    castdep = {}
    mm_last = None
    tmod_done = None
    for nt in range(NT6):
        mm_slot = nt % 2
        evac_prev = st.get("modev%d" % mm_slot)
        for e in range(NE):
            bi = (nt * NE + e) % 4
            tld = dma("gpsimd" if bi % 2 == 0 else "sync", wmst[bi][:, 0:512],
                      wmod_in[e * 128:(e + 1) * 128, nt * 512:(nt + 1) * 512],
                      waits=(bufdep.get(bi),))
            tcast = op("vector", lambda E_, bi=bi: E_.tensor_copy(
                wmbf[bi][:, 0:512], wmst[bi][:, 0:512]), waits=(tld,))
            castdep[bi] = tcast
            mm_last = op("tensor", lambda E_, e=e, bi=bi, ms=mm_slot: E_.matmul(
                psO[0:B, ms * 512:(ms + 1) * 512], csT[:, e * B:(e + 1) * B],
                wmbf[bi][:, 0:512], start=(e == 0), stop=(e == NE - 1)),
                waits=(tcast, tev_cs, evac_prev))
            bufdep[bi] = mm_last
        lo = nt * 512
        need_clamp = (2 * E <= lo < 3 * E) or (5 * E <= lo < 6 * E)
        tev2 = op("vector", lambda E_, ms=mm_slot: E_.tensor_copy(
            ot0[0:B, 0:512], psO[0:B, ms * 512:(ms + 1) * 512]), waits=(mm_last,))
        if need_clamp:
            op("vector", lambda E_: E_.tensor_scalar(
                ot1[0:B, 0:512], ot0[0:B, 0:512], 0.25, None, ALU.max))
            op("vector", lambda E_: E_.tensor_scalar(
                oa[0:B, 0:512], ot0[0:B, 0:512], -0.25, None, ALU.min))
            op("vector", lambda E_: E_.tensor_scalar(
                xsq[0:B, 0:512], ot0[0:B, 0:512], 0.0, None, ALU.is_ge))
            op("vector", lambda E_: E_.tensor_tensor(
                xs1[0:B, 0:512], ot1[0:B, 0:512], oa[0:B, 0:512], ALU.subtract))
            op("vector", lambda E_: E_.tensor_tensor(
                ot1[0:B, 0:512], xs1[0:B, 0:512], xsq[0:B, 0:512], ALU.mult))
            tev2 = op("vector", lambda E_: E_.tensor_tensor(
                ot0[0:B, 0:512], oa[0:B, 0:512], ot1[0:B, 0:512], ALU.add))
        st["modev%d" % mm_slot] = tev2
        tmod_done = dma("sync", modd[:, nt * 512:(nt + 1) * 512],
                        ot0[0:B, 0:512], waits=(tev2,))
    st["p0cast"] = castdep.get(2)
    st["p0castB"] = castdep.get(3)

    half = (NE + 1) // 2
    wqstg, wkstg = qT, kT
    tstg = None
    for ep in range(NE):
        tstg = dma("gpsimd", wqstg[:, ep * E:(ep + 1) * E],
                   wq_in[ep * 128:(ep + 1) * 128, :])
        tstg = dma("gpsimd", wkstg[:, ep * E:(ep + 1) * E],
                   wk_in[ep * 128:(ep + 1) * 128, :])
    tstg_s = None
    mev = {}
    tM_done = None
    tM_mm = None
    for ec in range(NE):
        for dh in range(NHF):
            slot = ((ec * NHF + dh) % 4) * 512
            mm = None
            for ep in range(NE):
                mm = op("tensor", lambda E_, ec=ec, dh=dh, ep=ep, slot=slot: E_.matmul(
                    psS[:, slot:slot + 512],
                    wqstg[:, ep * E + ec * 128: ep * E + ec * 128 + 128],
                    wkstg[:, ep * E + dh * 512: ep * E + dh * 512 + 512],
                    start=(ep == 0), stop=(ep == NE - 1)),
                    waits=(tstg, tstg_s, mev.get(slot)))
            tM_mm = mm
            tM_done = op("scalar", lambda E_, ec=ec, dh=dh, slot=slot: E_.activation(
                Msb[:, ec * E + dh * 512: ec * E + dh * 512 + 512],
                psS[:, slot:slot + 512], AF.Identity), waits=(mm,), sw=False)
            mev[slot] = tM_done

    bufdep2 = {0: mm_last, 1: mm_last}
    prev_tr = (tid,)
    twv_done = None
    qkv_tr_prev = mm_last
    for n in range(NE):
        bi = n % 2
        stg = wst0 if bi == 0 else wst1
        tca = []
        tprev = None
        for pc in range(E // 512):
            cbf = m3 if pc % 2 == 0 else m4
            tldp = dma("gpsimd", stg[:, 0:512],
                       wv_in[n * 128:(n + 1) * 128, pc * 512:(pc + 1) * 512],
                       waits=(bufdep2.get(bi), tprev, mm_last))
            tcp = op("vector", lambda E_, cbf=cbf, stg=stg: E_.tensor_copy(
                cbf[:, 0:512], stg[:, 0:512]),
                waits=(tldp, mm_last, qkv_tr_prev))
            tprev = tcp
            tca.append(tcp)
        bufdep2[bi] = tca[-1]
        trk = []
        for e in range(NE):
            pc = (e * 128) // 512
            cbf = m3 if pc % 2 == 0 else m4
            ecol = (e * 128) % 512
            bank = psT0 if e < half else psT1
            slot = e % half
            tt = op("tensor", lambda E_, bank=bank, slot=slot, cbf=cbf, ecol=ecol: E_.transpose(
                bank[:, slot * 128:(slot + 1) * 128],
                cbf[:, ecol:ecol + 128], identb[:]),
                waits=(tca[pc], tid, tM_mm) + prev_tr)
            trk.append(tt)
        qkv_tr_prev = trk[-1]

        def _evv(E_, n=n, banki=0):
            bank = psT0 if banki == 0 else psT1
            e0 = banki * half
            cnt = half if banki == 0 else NE - half
            base = wvT[:]
            dst = bass.AP(base.tensor, base.offset + e0 * E + n * 128,
                          [list(base.ap[0]), [E, cnt], [1, 128]])
            src = bank[:, 0:cnt * 128].rearrange("p (c w) -> p c w", w=128)
            return E_.tensor_copy(dst, src)
        ev0 = op("vector", lambda E_, n=n: _evv(E_, n, 0), waits=(trk[half - 1],))
        ev1 = op("vector", lambda E_, n=n: _evv(E_, n, 1), waits=(trk[-1],))
        twv_done = ev1
        prev_tr = (ev0, ev1)
    st["psT_ev"] = prev_tr[-1]
    st["wqo_tr"] = (twv_done, qkv_tr_prev)

    wchunks = []
    for w_d, wT_d, R, C in ((wg_in, wgT_d, I, E), (wu_in, wuT_d, I, E)):
        for r in range(R // 128):
            for cc in range(C // 512):
                wchunks.append((w_d, wT_d, r, cc * 512, 512, R))
    c_list = []
    c0 = 0
    while c0 < I:
        w_ = min(512, I - c0)
        c_list.append((c0, w_))
        c0 += w_
    for r in range(E // 128):
        for (c0, w_) in c_list:
            wchunks.append((wd_in, wdT_d, r, c0, w_, E))

    wqs = dict(free0=(mm_last, bufdep2.get(0)), free1=(mm_last, bufdep2.get(1)),
               stfree={}, bufi=0)

    def emit_wchunk(idx):
        if idx >= len(wchunks):
            return
        w_d, wT_d, r, c0, W, R = wchunks[idx]
        bi = wqs["bufi"]
        wqs["bufi"] ^= 1
        stg = wst0 if bi == 0 else wst1
        fr = wqs["free%d" % bi]
        tld = dma("gpsimd" if bi == 0 else "sync",
                  stg[:, 0:W], w_d[r * 128:(r + 1) * 128, c0:c0 + W],
                  waits=fr)
        tq, tsg = quant(stg[:, 0:W], wqo[:, 0:W], W, tld,
                        tax=st.get("wqo_tr", ()))
        wqs["free%d" % bi] = (tq, tsg)
        ntile = W // 128
        trk = []
        for e in range(ntile):
            bank = psT0 if e < 4 else psT1
            slot = e % 4
            tt = op("tensor", lambda E_, bank=bank, slot=slot, e=e: E_.transpose(
                bank[:, slot * 128:(slot + 1) * 128],
                wqo[:, e * 128:(e + 1) * 128], identb[:]),
                waits=(tq, st.get("psT_ev")))
            trk.append(tt)
        st["wqo_tr"] = (trk[-1],)
        wto = wto0
        stfree = wqs["stfree"].get(0)
        n0 = min(4, ntile)
        evs = [op("scalar", lambda E_, n0=n0, wto=wto: E_.activation(
            wto[:, 0:n0 * 128], psT0[:, 0:n0 * 128], AF.Identity),
            waits=(trk[n0 - 1], stfree), sw=False)]
        if ntile > 4:
            evs.append(op("scalar", lambda E_, ntile=ntile, wto=wto: E_.activation(
                wto[:, 4 * 128:ntile * 128], psT1[:, 0:(ntile - 4) * 128],
                AF.Identity), waits=(trk[-1], stfree), sw=False))
        st["psT_ev"] = evs[-1]
        dst = bass.AP(wT_d[:].tensor, c0 * R + r * 128,
                      [[R, 128], [128 * R, ntile], [1, 128]])
        tst = dma("gpsimd", dst,
                  wto[:, 0:ntile * 128].rearrange("p (c w) -> p c w", w=128),
                  waits=tuple(evs))
        wqs["stfree"][0] = tst

    WCH = [0]

    def next_wchunks(k):
        if not interleave_w:
            return
        for _ in range(k):
            if WCH[0] < len(wchunks):
                emit_wchunk(WCH[0])
                WCH[0] += 1

    ast = dict(x={}, xr={}, omst={}, tom=None, tga=None, sc_evac=None,
               texp=None, tmx=None, o_evac=None, ptr_last=None, t3=None,
               t5=None, xhb_tr=None, pv_last=None, trl=None,
               qslot={}, vslot={},
               qe={"q": None, "k": None, "v": None}, ptev={},
               ptT_mm={0: st["p0castB"], 1: st["p0castB"]},
               trlp={}, trmp={}, t5p={}, tm2p={}, t8p={}, cap=None)
    xm_last = None
    for s in range(B):
        t_car = dma("sync", oa[0:1, 0:E], modd[s:s + 1, E:2 * E],
                    waits=(tmod_done, ast["tga"]))
        t_sar = dma("sync", orw[0:1, 0:E], modd[s:s + 1, 0:E],
                    waits=(tmod_done, ast["tom"]))
        t_gab = dma("sync", gab[:],
                    bass.AP(modd[:].tensor, s * E6 + 2 * E, [[0, 128], [1, E]]),
                    waits=(tmod_done, ast["tga"]))
        trc = trs = None
        for c_ in range(NE):
            trc = op("tensor", lambda E_, c_=c_: E_.transpose(
                psO[:, c_:c_ + 1], oa[0:1, c_ * 128:(c_ + 1) * 128],
                identf[0:1, 0:1]), waits=(t_car, tidf, ast["o_evac"]))
            trs = op("tensor", lambda E_, c_=c_: E_.transpose(
                psO[:, 512 + c_:512 + c_ + 1], orw[0:1, c_ * 128:(c_ + 1) * 128],
                identf[0:1, 0:1]), waits=(t_sar,))
        t_cap = op("scalar", lambda E_: E_.activation(
            caP[:, 0:NE], psO[:, 0:NE], AF.Identity, bias=1.0), waits=(trc,))
        t_sap = op("scalar", lambda E_: E_.activation(
            saP[:, 0:NE], psO[:, 512:512 + NE], AF.Identity), waits=(trs,))
        ast["o_evac"] = t_sap
        ast["cap"] = (t_cap, t_sap)

        xst = (xs0, xs1, xr0, xr1)
        for g in range(NSG):
            h_last = None
            for j in range(4):
                ti = g * 4 + j
                bi = ti % 4
                t_ld = dma("sync", xst[bi][:], x_in[s, ti * 128:(ti + 1) * 128, :],
                           waits=(ast["x"].get(bi), tmod_done,
                                  ast["xr"].get(bi - 2) if bi >= 2 else None,
                                  tcs if s == 0 and g == 0 else None))
                so = 12 * (ti % 2)
                op("vector", lambda E_, bi=bi, so=so: E_.tensor_reduce(
                    stat[:, so:so + 1], xst[bi][:], AX.X, ALU.add),
                    waits=(t_ld, ast["t5p"].get(ti % 2)))
                t2 = op("scalar", lambda E_, bi=bi, so=so: E_.activation(
                    xsq[:], xst[bi][:], AF.Square, accum_out=stat[:, so + 1:so + 2]),
                    waits=(t_ld, ast["tm2p"].get(ti % 2), tmod_done,
                           ast["tom"]), sw=False)
                op("vector", lambda E_, so=so: E_.tensor_scalar(
                    stat[:, so + 2:so + 3], stat[:, so:so + 1], 1.0 / E, None,
                    ALU.mult))
                tm2 = op("vector", lambda E_, so=so: E_.tensor_scalar(
                    stat[:, so + 3:so + 4], stat[:, so + 1:so + 2], 1.0 / E, None,
                    ALU.mult), waits=(t2,))
                ast["tm2p"][ti % 2] = tm2
                op("vector", lambda E_, so=so: E_.tensor_tensor(
                    stat[:, so + 4:so + 5], stat[:, so + 2:so + 3],
                    stat[:, so + 2:so + 3], ALU.mult))
                t4 = op("vector", lambda E_, so=so: E_.tensor_tensor(
                    stat[:, so + 5:so + 6], stat[:, so + 3:so + 4],
                    stat[:, so + 4:so + 5], ALU.subtract))
                t5 = op("scalar", lambda E_, so=so: E_.activation(
                    stat[:, so + 6:so + 7], stat[:, so + 5:so + 6], AF.Sqrt,
                    bias=epsc[:, 0:1]), waits=(t4, teps))
                ast["t5p"][ti % 2] = t5
                t6 = op("vector", lambda E_, so=so: E_.reciprocal(
                    stat[:, so + 7:so + 8], stat[:, so + 6:so + 7]),
                    waits=(t5, ast["t8p"].get(ti % 2)))
                t7 = op("vector", lambda E_, so=so: E_.scalar_tensor_tensor(
                    stat[:, so + 8:so + 9], stat[:, so + 2:so + 3], -1.0,
                    stat[:, so + 7:so + 8], ALU.mult, ALU.mult))
                t8 = op("scalar", lambda E_, bi=bi, so=so: E_.activation(
                    xhb[:], xst[bi][:], AF.Identity, bias=stat[:, so + 8:so + 9],
                    scale=stat[:, so + 7:so + 8]),
                    waits=(t7, tmod_done, ast["xhb_tr"],
                           tev_cs if s == 0 and g == 0 else None), sw=False)
                ast["t8p"][ti % 2] = t8
                ast["x"][bi] = t8
                trk = []
                for e in range(NE):
                    bank = psT0 if e < half else psT1
                    slot = e % half
                    tt = op("tensor", lambda E_, bank=bank, slot=slot, e=e: E_.transpose(
                        bank[:, slot * 128:(slot + 1) * 128],
                        xhb[:, e * 128:(e + 1) * 128], identb[:]),
                        waits=(t8, tid, st.get("psT_ev")))
                    trk.append(tt)
                ast["xhb_tr"] = trk[-1]
                for e in range(NE):
                    bank = psT0 if e < half else psT1
                    slot = e % half
                    h_last = op("scalar", lambda E_, bank=bank, slot=slot, e=e, j=j, g=g: E_.activation(
                        kT[:, e * S + g * 512 + j * 128: e * S + g * 512 + (j + 1) * 128],
                        bank[:, slot * 128:(slot + 1) * 128],
                        AF.Identity, bias=saP[:, e:e + 1], scale=caP[:, e:e + 1]),
                        waits=(trk[e], ast["texp"], tM_mm) + ast["cap"],
                        sw=(e == 0 and j == 0))
                st["psT_ev"] = h_last
            for n in range(NE):
                base = (n % 2) * 512
                mm = None
                for e in range(NE):
                    mm = op("tensor", lambda E_, n=n, e=e, base=base, g=g: E_.matmul(
                        psS[:, base:base + 512],
                        Msb[:, e * E + n * 128: e * E + n * 128 + 128],
                        kT[:, e * S + g * 512: e * S + (g + 1) * 512],
                        start=(e == 0), stop=(e == NE - 1)),
                        waits=(h_last, tM_done, ast["qslot"].get(base)))
                ast["qe"]["q"] = op(
                    "scalar", lambda E_, n=n, g=g, base=base: E_.activation(
                        qT[:, n * S + g * 512: n * S + (g + 1) * 512],
                        psS[:, base:base + 512], AF.Identity),
                    waits=(mm, ast["pv_last"], tM_mm), sw=False)
                ast["qslot"][base] = ast["qe"]["q"]
            for j in range(4):
                ti = g * 4 + j
                for hf in range(NHF):
                    mm = None
                    for e in range(NE):
                        mm = op("tensor", lambda E_, j=j, hf=hf, e=e, g=g: E_.matmul(
                            psS[:, 1024 + hf * 512: 1024 + (hf + 1) * 512],
                            kT[:, e * S + g * 512 + j * 128: e * S + g * 512 + (j + 1) * 128],
                            wvT[:, e * E + hf * 512: e * E + (hf + 1) * 512],
                            start=(e == 0), stop=(e == NE - 1)),
                            waits=(h_last, twv_done, ast["vslot"].get(hf)))
                    ast["qe"]["v"] = op(
                        "scalar", lambda E_, ti=ti, hf=hf: E_.activation(
                            vsb[:, ti * E + hf * 512: ti * E + (hf + 1) * 512],
                            psS[:, 1024 + hf * 512: 1024 + (hf + 1) * 512],
                            AF.Identity),
                        waits=(mm, ast["pv_last"]), sw=False)
                    ast["vslot"][hf] = ast["qe"]["v"]
            next_wchunks(1)
        ast["qe"]["k"] = h_last

        ssbs = (ssb, ssb)

        def a_stage1(i):
            W = (i + 1) * 128
            NK4 = (W + 511) // 512
            dk4 = (i * 128) // 512
            doff = (i * 128) % 512
            sc_mm = None
            for e in range(NE):
                for k4 in range(NK4):
                    w_ = min(512, W - k4 * 512)
                    sc_mm = op("tensor", lambda E_, e=e, k4=k4, w_=w_, i=i: E_.matmul(
                        psS[:, k4 * 512:k4 * 512 + w_],
                        qT[:, e * S + i * 128:e * S + (i + 1) * 128],
                        kT[:, e * S + k4 * 512:e * S + k4 * 512 + w_],
                        start=(e == 0),
                        stop=(e == NE - 1 and k4 != dk4)),
                        waits=(ast["qe"]["q"], ast["qe"]["k"], ast["sc_evac"]))
            sc_mm = op("tensor", lambda E_, dk4=dk4, doff=doff: E_.matmul(
                psS[:, dk4 * 512 + doff: dk4 * 512 + doff + 128],
                cmA[:], cmB[:], start=False, stop=True),
                waits=(tmA, tmB))
            ev = None
            for k4 in range(NK4):
                w_ = min(512, W - k4 * 512)
                ev = op("scalar", lambda E_, k4=k4, w_=w_, i=i: E_.activation(
                    ssbs[i % 2][:, k4 * 512:k4 * 512 + w_],
                    psS[:, k4 * 512:k4 * 512 + w_],
                    AF.Identity, scale=scc[:, 0:1]),
                    waits=(sc_mm, tscc, st["p0cast"]), sw=False)
            ast["sc_evac"] = ev

        def a_stage2_front(i):
            pi2 = i % 2
            texp = op("scalar", lambda E_, W=(i + 1) * 128, i=i, pi2=pi2: E_.activation(
                psb[:, 0:W], ssbs[i % 2][:, 0:W], AF.Exp, bias=nbias[:, 0:1],
                accum_out=mx1[:, pi2 * 2:pi2 * 2 + 1]),
                waits=(tnb, ast["ptr_last"], ast["trlp"].get(pi2)), sw=False)
            ast["texp"] = texp
            trl = op("vector", lambda E_, pi2=pi2: E_.reciprocal(
                mx1[:, pi2 * 2 + 1:pi2 * 2 + 2], mx1[:, pi2 * 2:pi2 * 2 + 1]),
                waits=(texp, ast["trmp"].get(pi2)))
            ast["trl"] = trl
            ast["trlp"][pi2] = trl

        def a_stage2(i, s):
            W = (i + 1) * 128
            bi = i % 2
            xrb = xr0 if bi == 0 else xr1
            t_ldr = dma("sync", xrb[:], x_in[s, i * 128:(i + 1) * 128, :],
                        waits=(ast["xr"].get(bi), ast["x"].get(bi + 2),
                               t_car, t_sar, t_cap if bi == 0 else t_sap))
            texp = ast["texp"]
            trl = ast["trl"]
            pv_mm = None
            NKT = i + 1
            for g4 in range(0, NKT, 4):
                cnt = min(4, NKT - g4)
                pgrp = (g4 // 4) % 2
                bank = psT0 if pgrp == 0 else psT1
                trk = []
                for kk in range(cnt):
                    kt = g4 + kk
                    tt = op("tensor", lambda E_, bank=bank, kk=kk, kt=kt: E_.transpose(
                        bank[:, kk * 128:(kk + 1) * 128],
                        psb[:, kt * 128:(kt + 1) * 128], identb[:]),
                        waits=(texp, st.get("psT_ev"), ast["ptev"].get(pgrp)))
                    trk.append(tt)
                ast["ptr_last"] = trk[-1]
                tevp = op("scalar", lambda E_, bank=bank, cnt=cnt, pgrp=pgrp: E_.activation(
                    ptT[:, pgrp * 512: pgrp * 512 + cnt * 128], bank[:, 0:cnt * 128],
                    AF.Identity), waits=(trk[-1], ast["ptT_mm"].get(pgrp)), sw=False)
                st["psT_ev"] = tevp
                ast["ptev"][pgrp] = tevp
                for kk in range(cnt):
                    kt = g4 + kk
                    for hf in range(NHF):
                        pv_mm = op("tensor", lambda E_, kt=kt, kk=kk, hf=hf, pgrp=pgrp, i=i: E_.matmul(
                            psO[:, hf * 512:(hf + 1) * 512],
                            ptT[:, pgrp * 512 + kk * 128: pgrp * 512 + (kk + 1) * 128],
                            vsb[:, kt * E + hf * 512: kt * E + (hf + 1) * 512],
                            start=(kt == 0), stop=(kt == i)),
                            waits=(tevp, ast["qe"]["v"], ast["o_evac"]))
                ast["ptT_mm"][pgrp] = pv_mm
            ast["pv_last"] = pv_mm
            oev = None
            for hf in range(NHF):
                oev = op("scalar", lambda E_, hf=hf: E_.activation(
                    oa[:, hf * 512:(hf + 1) * 512], psO[:, hf * 512:(hf + 1) * 512],
                    AF.Identity), waits=(pv_mm, ast["tga"]), sw=False)
            ast["o_evac"] = oev
            tga = op("gpsimd", lambda E_: E_.tensor_tensor(
                xsq[:], oa[:], gab[:], ALU.mult), waits=(oev, t_gab))
            ast["tga"] = tga
            pi2 = i % 2
            _m = mx1[:]
            rsc = bass.AP(_m.tensor, _m.offset + pi2 * 2 + 1,
                          [list(_m.ap[0]), [0, E]])
            trm = op("gpsimd", lambda E_, rsc=rsc: E_.tensor_tensor(
                orw[:], xsq[:], rsc, ALU.mult), waits=(trl,))
            ast["trmp"][pi2] = trm
            obuf = ot0 if bi == 0 else ot1
            tom = op("gpsimd", lambda E_, obuf=obuf, xrb=xrb: E_.tensor_tensor(
                obuf[:], orw[:], xrb[:], ALU.add),
                waits=(trm, t_ldr, ast["omst"].get(bi)))
            ast["tom"] = tom
            ast["xr"][bi] = tom
            xm = dma("sync", xmid_d[s, i * 128:(i + 1) * 128, :], obuf[:],
                     waits=(tom,))
            ast["omst"][bi] = xm
            next_wchunks(2)
            return xm

        a_stage1(0)
        for i in range(NS):
            a_stage2_front(i)
            if i + 1 < NS:
                a_stage1(i + 1)
            xm_last = a_stage2(i, s)

    while WCH[0] < len(wchunks):
        emit_wchunk(WCH[0])
        WCH[0] += 1

    barrier = tuple(x for x in (xm_last, wqs["stfree"].get(0),
                                wqs["stfree"].get(1)) if x is not None)
    t_wB = None
    for e in range(NE):
        t_wB = dma("gpsimd", wgT[:, e * I:(e + 1) * I],
                   wgT_d[e * 128:(e + 1) * 128, :], waits=barrier)
    for e in range(NE):
        t_wB = dma("gpsimd", wuT[:, e * I:(e + 1) * I],
                   wuT_d[e * 128:(e + 1) * 128, :], waits=barrier)
    for ic in range(NI):
        t_wB = dma("gpsimd", wdT[:, ic * E:(ic + 1) * E],
                   wdT_d[ic * 128:(ic + 1) * 128, :], waits=barrier)

    bst = dict(x={}, ost={}, tout=None, tgb=barrier[0],
               t5p={}, tm2p={}, t8p={},
               h2q_tr=(), hq_tr=(), gate_mm=None, mmd=None, thd=None,
               tsg=None, t9=None, sl={0: None, 1: None})
    itiles = c_list

    def b_back(s, i, bi, xb_, mmd, t_gbb):
        tgb = op("vector", lambda E_: E_.tensor_tensor(
            oa[:], psO[:, 0:E], gab[:], ALU.mult),
            waits=(mmd, t_gbb, bst["tout"]), sw=False)
        bst["tgb"] = tgb
        obuf = ot0 if bi == 0 else ot1
        tout = op("gpsimd", lambda E_, obuf=obuf, xb_=xb_: E_.tensor_tensor(
            obuf[:], oa[:], xb_[:], ALU.add),
            waits=(tgb, bst["ost"].get(bi)))
        bst["tout"] = tout
        bst["x"][bi] = tout
        tst = dma("gpsimd", out_d[s, i * 128:(i + 1) * 128, :], obuf[:],
                  waits=(tout,))
        bst["ost"][bi] = tst

    pend = None
    for s in range(B):
        if pend is not None:
            b_back(*pend)
            pend = None
        t_cbr = dma("sync", h2f[:],
                    bass.AP(modd[:].tensor, s * E6 + 4 * E, [[0, 128], [1, E]]),
                    waits=barrier + (bst["t9"], bst["tsg"]))
        t_cb1 = op("vector", lambda E_: E_.tensor_scalar(
            cbb[:], h2f[:], 1.0, None, ALU.add), waits=(t_cbr, bst["t9"]))
        t_sbb = dma("sync", sbb[:],
                    bass.AP(modd[:].tensor, s * E6 + 3 * E, [[0, 128], [1, E]]),
                    waits=barrier + (bst["t9"],))
        t_gbb = dma("sync", gab[:],
                    bass.AP(modd[:].tensor, s * E6 + 5 * E, [[0, 128], [1, E]]),
                    waits=barrier + (ast["tga"], bst["tgb"]))
        for i in range(NS):
            bi = i % 2
            xb_ = xs0 if bi == 0 else xs1
            t_ld = dma("sync", xb_[:], xmid_d[s, i * 128:(i + 1) * 128, :],
                       waits=(bst["x"].get(bi), xm_last))
            so = 12 * (i % 2)
            op("vector", lambda E_, xb_=xb_, so=so: E_.tensor_reduce(
                stat[:, so:so + 1], xb_[:], AX.X, ALU.add),
                waits=(t_ld, bst["t5p"].get(bi)))
            t2 = op("scalar", lambda E_, xb_=xb_, so=so: E_.activation(
                xsq[:], xb_[:], AF.Square, accum_out=stat[:, so + 1:so + 2]),
                waits=(t_ld, bst["tm2p"].get(bi), ast["tga"], bst["t9"]), sw=False)
            op("vector", lambda E_, so=so: E_.tensor_scalar(
                stat[:, so + 2:so + 3], stat[:, so:so + 1], 1.0 / E, None,
                ALU.mult))
            tm2 = op("vector", lambda E_, so=so: E_.tensor_scalar(
                stat[:, so + 3:so + 4], stat[:, so + 1:so + 2], 1.0 / E, None,
                ALU.mult), waits=(t2,))
            bst["tm2p"][bi] = tm2
            op("vector", lambda E_, so=so: E_.tensor_tensor(
                stat[:, so + 4:so + 5], stat[:, so + 2:so + 3],
                stat[:, so + 2:so + 3], ALU.mult))
            t4 = op("vector", lambda E_, so=so: E_.tensor_tensor(
                stat[:, so + 5:so + 6], stat[:, so + 3:so + 4],
                stat[:, so + 4:so + 5], ALU.subtract))
            t5 = op("scalar", lambda E_, so=so: E_.activation(
                stat[:, so + 6:so + 7], stat[:, so + 5:so + 6], AF.Sqrt,
                bias=epsc[:, 0:1]), waits=(t4, teps))
            bst["t5p"][bi] = t5
            t6 = op("vector", lambda E_, so=so: E_.reciprocal(
                stat[:, so + 7:so + 8], stat[:, so + 6:so + 7]),
                waits=(t5, bst["t8p"].get(bi)))
            t7 = op("vector", lambda E_, so=so: E_.scalar_tensor_tensor(
                stat[:, so + 8:so + 9], stat[:, so + 2:so + 3], -1.0,
                stat[:, so + 7:so + 8], ALU.mult, ALU.mult))
            t8 = op("scalar", lambda E_, xb_=xb_, so=so: E_.activation(
                h2f[:], xb_[:], AF.Identity, bias=stat[:, so + 8:so + 9],
                scale=stat[:, so + 7:so + 8]),
                waits=(t7, t_cb1, bst["tsg"]), sw=False)
            bst["t8p"][bi] = t8
            t9a = op("gpsimd", lambda E_: E_.tensor_tensor(
                xsq[:], h2f[:], cbb[:], ALU.mult), waits=(t8, t_cb1))
            t9 = op("gpsimd", lambda E_: E_.tensor_tensor(
                h2f[:], xsq[:], sbb[:], ALU.add), waits=(t_sbb,))
            bst["t9"] = t9
            tqs = None
            for hc in range(E // 512):
                tqs, tsgh2 = quant(h2f[:, hc * 512:(hc + 1) * 512],
                                   h2q[:, hc * 512:(hc + 1) * 512], 512, t9,
                                   tax=bst["h2q_tr"])
                bst["tsg"] = tsgh2
            tq = tqs
            trk = []
            for e in range(NE):
                bank = psT0 if e < half else psT1
                slot = e % half
                tt = op("tensor", lambda E_, bank=bank, slot=slot, e=e: E_.transpose(
                    bank[:, slot * 128:(slot + 1) * 128],
                    h2q[:, e * 128:(e + 1) * 128], identb[:]),
                    waits=(tq, st.get("psT_ev"), tid))
                trk.append(tt)
            bst["h2q_tr"] = (trk[-1],)
            tev0 = op("scalar", lambda E_: E_.activation(
                h2qT[:, 0:half * 128], psT0[:, 0:half * 128], AF.Identity),
                waits=(trk[half - 1], bst["gate_mm"]), sw=False)
            tev1 = op("scalar", lambda E_: E_.activation(
                h2qT[:, half * 128:NE * 128], psT1[:, 0:(NE - half) * 128],
                AF.Identity), waits=(trk[-1], bst["gate_mm"]), sw=False)
            st["psT_ev"] = tev1
            th2qT = tev1
            sltoks = []
            for it, (c0, w_) in enumerate(itiles):
                sl = it % 2
                gslot = 0 if sl == 0 else 1024
                uslot = 512 if sl == 0 else 1536
                mmg = None
                for e in range(NE):
                    mmg = op("tensor", lambda E_, e=e, c0=c0, w_=w_, gslot=gslot: E_.matmul(
                        psS[:, gslot:gslot + w_], h2qT[:, e * 128:(e + 1) * 128],
                        wgT[:, e * I + c0:e * I + c0 + w_],
                        start=(e == 0), stop=(e == NE - 1)),
                        waits=(th2qT, t_wB, bst["sl"][sl]))
                for e in range(NE):
                    mmg = op("tensor", lambda E_, e=e, c0=c0, w_=w_, uslot=uslot: E_.matmul(
                        psS[:, uslot:uslot + w_], h2qT[:, e * 128:(e + 1) * 128],
                        wuT[:, e * I + c0:e * I + c0 + w_],
                        start=(e == 0), stop=(e == NE - 1)))
                bst["gate_mm"] = mmg
                tsl = op("scalar", lambda E_, gslot=gslot, w_=w_: E_.activation(
                    sgf[:, 0:w_], psS[:, gslot:gslot + w_], AF.Silu),
                    waits=(mmg, bst["thd"]), sw=False)
                thd = op("vector", lambda E_, uslot=uslot, w_=w_: E_.tensor_tensor(
                    hidf[:, 0:w_], sgf[:, 0:w_], psS[:, uslot:uslot + w_], ALU.mult),
                    waits=(tsl, bst["sl"][sl], bst["tsg"]), sw=False)
                bst["thd"] = thd
                hq_dst = bass.AP(hqsb[:].tensor, hqsb[:].offset + c0,
                                 [list(hqsb[:].ap[0]), [1, w_]])
                tqh, tsg_h = quant(hidf[:, 0:w_], hq_dst, w_, thd,
                                   tax=bst["hq_tr"])
                bst["sl"][sl] = tqh
                bst["tsg"] = tsg_h
                sltoks.append(tqh)
            trk2 = None
            for icb in range(0, NI, 4):
                cnt = min(4, NI - icb)
                bank = psT0 if (icb // 4) % 2 == 0 else psT1
                grp = []
                for kk in range(cnt):
                    trk2 = op("tensor", lambda E_, bank=bank, kk=kk, icb=icb: E_.transpose(
                        bank[:, kk * 128:(kk + 1) * 128],
                        hqsb[:, (icb + kk) * 128:(icb + kk + 1) * 128], identb[:]),
                        waits=(sltoks[icb // 4], st.get("psT_ev")))
                    grp.append(trk2)
                tevh = op("scalar", lambda E_, bank=bank, cnt=cnt, icb=icb: E_.activation(
                    hqT[:, icb * 128:(icb + cnt) * 128], bank[:, 0:cnt * 128],
                    AF.Identity), waits=(grp[-1], bst["mmd"]), sw=False)
                st["psT_ev"] = tevh
            bst["hq_tr"] = (trk2,)
            thqT = st["psT_ev"]
            if pend is not None:
                b_back(*pend)
            mmd = None
            for ic in range(NI):
                for hf in range(NHF):
                    mmd = op("tensor", lambda E_, ic=ic, hf=hf: E_.matmul(
                        psO[:, hf * 512:(hf + 1) * 512],
                        hqT[:, ic * 128:(ic + 1) * 128],
                        wdT[:, ic * E + hf * 512: ic * E + (hf + 1) * 512],
                        start=(ic == 0), stop=(ic == NI - 1)),
                        waits=(thqT, t_wB, bst["tgb"]))
            bst["mmd"] = mmd
            pend = (s, i, bi, xb_, mmd, t_gbb)
    b_back(*pend)

    final_ds = Tok("DS", P.chan["DS"][1], "sync", True)
    final_dg = Tok("DG", P.chan["DG"][1], "gpsimd", True)
    P.wait_only("sync", (final_ds,))
    P.wait_only("gpsimd", (final_dg,))

    with nc.Block() as block:
        @block.sync
        def _(sync):
            P.run(sync, "sync")

        @block.vector
        def _(vector):
            P.run(vector, "vector")

        @block.scalar
        def _(scalar):
            P.run(scalar, "scalar")

        @block.tensor
        def _(tensor):
            P.run(tensor, "tensor")

        @block.gpsimd
        def _(gpsimd):
            P.run(gpsimd, "gpsimd")

    es.close()
    return nc


_NC_CACHE = {}


def kernel(L0_attn_x, L0_attn_WQ, L0_attn_WK, L0_attn_WV,
           L0_ffn_w_gate, L0_ffn_w_up, L0_ffn_w_down, L0_c, L0_wmod):
    cfg = FULL_CFG
    n = cfg["NCORES"]
    bpc = L0_attn_x.shape[0] // n
    if "full" not in _NC_CACHE:
        _NC_CACHE["full"] = build(cfg)
    nc = _NC_CACHE["full"]
    x = np.ascontiguousarray(L0_attn_x)
    c = np.ascontiguousarray(L0_c)
    shared = {
        "wq_in": np.ascontiguousarray(L0_attn_WQ),
        "wk_in": np.ascontiguousarray(L0_attn_WK),
        "wv_in": np.ascontiguousarray(L0_attn_WV),
        "wg_in": np.ascontiguousarray(L0_ffn_w_gate),
        "wu_in": np.ascontiguousarray(L0_ffn_w_up),
        "wd_in": np.ascontiguousarray(L0_ffn_w_down),
        "wmod_in": np.ascontiguousarray(L0_wmod),
    }
    in_maps = []
    for ci in range(n):
        m = dict(shared)
        m["x_in"] = np.ascontiguousarray(x[ci * bpc:(ci + 1) * bpc])
        m["c_in"] = np.ascontiguousarray(c[ci * bpc:(ci + 1) * bpc])
        in_maps.append(m)
    res = run_bass_kernel_spmd(nc, in_maps, core_ids=list(range(n)))
    return np.concatenate([np.asarray(r["out"]) for r in res.results], axis=0)


# version 14
# speedup vs baseline: 1.0440x; 1.0327x over previous
import contextlib
import numpy as np
import ml_dtypes
import concourse.bass as bass
import concourse.mybir as mybir
from concourse.bass_utils import run_bass_kernel_spmd

F32 = mybir.dt.float32
BF16 = mybir.dt.bfloat16
AF = mybir.ActivationFunctionType
ALU = mybir.AluOpType
AX = mybir.AxisListType

ENGINES = ("sync", "vector", "scalar", "tensor", "gpsimd")

FULL_CFG = dict(B=4, S=2048, E=1024, I=2816, NCORES=8)


class Tok:
    __slots__ = ("ch", "val", "eng", "dma")

    def __init__(self, ch, val, eng, dma):
        self.ch, self.val, self.eng, self.dma = ch, val, eng, dma


class Prog:

    def __init__(self, nc):
        self.nc = nc
        self.streams = {e: [] for e in ENGINES}
        self.chan = {}
        self.water = {}

    def channel(self, name, sem):
        self.chan[name] = [sem, 0]

    CHAINED = ("vector", "scalar", "gpsimd")

    def emit(self, eng, fn, waits=(), inc=None, inc_k=1, sw=True):
        wl = []
        if sw and eng in self.CHAINED and inc is not None:
            cur = self.chan[inc][1]
            key = (eng, inc)
            if cur > 0 and self.water.get(key, -1) < cur:
                self.water[key] = cur
                wl.append((inc, cur))
        for t in waits:
            if t is None:
                continue
            if (not t.dma) and t.eng == eng:
                continue
            key = (eng, t.ch)
            if self.water.get(key, -1) >= t.val:
                continue
            self.water[key] = t.val
            wl.append((t.ch, t.val))
        tok = None
        if inc is not None:
            c = self.chan[inc]
            c[1] += inc_k
            tok = Tok(inc, c[1], eng, inc_k >= 16)
        self.streams[eng].append((wl, fn, inc, inc_k))
        return tok

    def wait_only(self, eng, waits):
        self.emit(eng, None, waits=waits)

    def run(self, engobj, eng):
        for wl, fn, inc, inc_k in self.streams[eng]:
            for ch, val in wl:
                engobj.wait_ge(self.chan[ch][0], val)
            if fn is None:
                continue
            ins = fn(engobj)
            if inc is not None:
                ins.then_inc(self.chan[inc][0], inc_k)


def bcast16(ap2d, nblk, blk=16):
    return bass.AP(ap2d.tensor, ap2d.offset, [list(ap2d.ap[0]), [1, nblk], [0, blk]])


def build(cfg, debug=False, interleave_w=True):
    B, S, E, I = cfg["B"], cfg["S"], cfg["E"], cfg["I"]
    NE, NS, NI = E // 128, S // 128, I // 128
    NSG = S // 512
    NHF = E // 512
    SCALE = float(E) ** -0.5
    E6 = 6 * E
    assert S % 512 == 0 and E % 512 == 0 and I % 128 == 0

    nc = bass.Bass("TRN2", target_bir_lowering=False, detect_race_conditions=False)

    x_in = nc.dram_tensor("x_in", [B, S, E], F32, kind="ExternalInput")
    wq_in = nc.dram_tensor("wq_in", [E, E], F32, kind="ExternalInput")
    wk_in = nc.dram_tensor("wk_in", [E, E], F32, kind="ExternalInput")
    wv_in = nc.dram_tensor("wv_in", [E, E], F32, kind="ExternalInput")
    wg_in = nc.dram_tensor("wg_in", [I, E], F32, kind="ExternalInput")
    wu_in = nc.dram_tensor("wu_in", [I, E], F32, kind="ExternalInput")
    wd_in = nc.dram_tensor("wd_in", [E, I], F32, kind="ExternalInput")
    c_in = nc.dram_tensor("c_in", [B, E], F32, kind="ExternalInput")
    wmod_in = nc.dram_tensor("wmod_in", [E, E6], F32, kind="ExternalInput")
    out_d = nc.dram_tensor("out", [B, S, E], F32, kind="ExternalOutput")

    _k = dict(kind="ExternalOutput") if debug else {}
    xmid_d = nc.dram_tensor("xmid_d", [B, S, E], F32, **_k)
    modd = nc.dram_tensor("modd", [B, E6], F32, **_k)
    qkT_d = nc.dram_tensor("qkT_d", [2 * NE * NE * 128, 128], BF16, **_k)
    wgT_d = nc.dram_tensor("wgT_d", [E, I], BF16, **_k)
    wuT_d = nc.dram_tensor("wuT_d", [E, I], BF16, **_k)
    wdT_d = nc.dram_tensor("wdT_d", [I, E], BF16, **_k)

    identb_d = nc.inline_tensor(np.eye(128, dtype=ml_dtypes.bfloat16), name="identb_c")
    identf_d = nc.inline_tensor(np.eye(128, dtype=np.float32), name="identf_c")
    _c = np.float32(np.sqrt(1e29))
    _ka = (-_c * (np.arange(128)[:, None] > np.arange(128)[None, :])
           ).astype(ml_dtypes.bfloat16)
    _kb = (_c * (np.arange(128)[:, None] <= np.arange(128)[None, :])
           ).astype(ml_dtypes.bfloat16)
    cmA_d = nc.inline_tensor(_ka, name="cmA_c")
    cmB_d = nc.inline_tensor(_kb, name="cmB_c")
    epsc_d = nc.inline_tensor(np.full((128, 1), 1e-5, dtype=np.float32), name="epsc_c")
    scc_d = nc.inline_tensor(np.full((128, 1), SCALE, dtype=np.float32), name="scc_c")
    nbias_d = nc.inline_tensor(np.full((128, 1), -12.0, dtype=np.float32),
                               name="nbias_c")

    SB_BASE = (nc.sbuf_base + 63) // 64 * 64
    SB_LIMIT = nc.sbuf_base + nc.sbuf_bytes_remaining
    off = [0]

    def sbt(name, shape, dtype, align=64):
        per = int(np.prod(shape[1:])) * mybir.dt.size(dtype)
        off[0] = (off[0] + align - 1) // align * align
        h = nc.alloc_sbuf_tensor_at(name, shape, dtype, offset=SB_BASE + off[0])
        off[0] += per
        return h

    identb = sbt("identb", [128, 128], BF16)
    identf = sbt("identf", [128, 128], F32)
    cmA = sbt("cmA", [128, 128], BF16)
    cmB = sbt("cmB", [128, 128], BF16)
    epsc = sbt("epsc", [128, 1], F32)
    scc = sbt("scc", [128, 1], F32)
    nbias = sbt("nbias", [128, 1], F32)
    xs0 = sbt("xs0", [128, E], F32)
    xs1 = sbt("xs1", [128, E], F32)
    ot0 = sbt("ot0", [128, E], F32)
    ot1 = sbt("ot1", [128, E], F32)
    oa = sbt("oa", [128, E], F32)
    xsq = sbt("xsq", [128, E], F32)
    gab = sbt("gab", [128, E], F32)
    caP = sbt("caP", [128, NE], F32)
    saP = sbt("saP", [128, NE], F32)
    stat = sbt("stat", [128, 12], F32)
    mx1 = sbt("mx1", [128, 4], F32)
    qax = (sbt("qax0", [128, 512], F32), sbt("qax1", [128, 512], F32))
    qbb = (sbt("qbb0", [128, 512], BF16), sbt("qbb1", [128, 512], BF16))
    msg = (sbt("msg0", [128, 512], BF16), sbt("msg1", [128, 512], BF16))
    m1 = sbt("m1", [128, 512], BF16)
    m2 = sbt("m2", [128, 512], BF16)
    m3 = sbt("m3", [128, 512], BF16)
    m4 = sbt("m4", [128, 512], BF16)
    m5 = sbt("m5", [128, 512], BF16)
    m6 = sbt("m6", [128, 512], BF16)
    qbm = (sbt("qbm0", [128, 32], F32), sbt("qbm1", [128, 32], F32))
    qr2 = (sbt("qr20", [128, 32], F32), sbt("qr21", [128, 32], F32))
    qs5 = (sbt("qs50", [128, 32], F32), sbt("qs51", [128, 32], F32))

    pers_end = off[0]
    xr0 = sbt("xr0", [128, E], F32)
    xr1 = sbt("xr1", [128, E], F32)
    xhb = sbt("xhb", [128, E], BF16)
    qT = sbt("qT", [128, NE * S], BF16)
    kT = sbt("kT", [128, NE * S], BF16)
    vsb = sbt("vsb", [128, NS * E], BF16)
    wvT = sbt("wvT", [128, NE * E], BF16)
    Msb = sbt("Msb", [128, NE * E], BF16)
    ssb = sbt("ssb", [128, S], F32)
    psb = sbt("psb", [128, S], BF16)
    ptT = sbt("ptT", [128, 8 * 128], BF16)
    wst0 = sbt("wst0", [128, 512], F32)
    wst1 = sbt("wst1", [128, 512], F32)
    wqo = sbt("wqo", [128, 512], BF16)
    wto0 = sbt("wto0", [128, NE * 128], BF16)
    orw = sbt("orw", [128, E], F32)
    wtoA = nc.alloc_sbuf_tensor_at("wtoA", [128, 512], F32,
                                   offset=nc.lookup_mloc(ssb).addr)
    wtoB = nc.alloc_sbuf_tensor_at("wtoB", [128, 512], F32,
                                   offset=nc.lookup_mloc(ptT).addr)
    a_end = off[0]
    print("A region end:", a_end, "limit:", SB_LIMIT - SB_BASE)
    assert SB_BASE + a_end <= SB_LIMIT, f"SBUF over budget (A): {a_end}"

    off[0] = pers_end
    wgT = sbt("wgT_s", [128, NE * I], BF16)
    wuT = sbt("wuT_s", [128, NE * I], BF16)
    wdT = sbt("wdT_s", [128, NI * E], BF16)
    h2f = sbt("h2f", [128, E], F32)
    h2q = sbt("h2q", [128, E], BF16)
    hqT = sbt("hqT", [128, NI * 128], BF16)
    h2qT = sbt("h2qT", [128, E], BF16)
    hqsb = sbt("hqsb", [128, I], BF16)
    sgf = sbt("sgf", [128, 512], F32)
    hidf = sbt("hidf", [128, 512], F32)
    cbb = sbt("cbb", [128, E], F32)
    sbb = sbt("sbb", [128, E], F32)
    b_end = off[0]
    print("B region end:", b_end)
    assert SB_BASE + b_end <= SB_LIMIT, f"SBUF over budget (B): {b_end}"

    psS = nc.alloc_psum_tensor("psS", [128, 2048], F32)
    psO = nc.alloc_psum_tensor("psO", [128, 1024], F32)
    psT0 = nc.alloc_psum_tensor("psT0", [128, 1024], BF16)
    psT1 = nc.alloc_psum_tensor("psT1", [128, 1024], BF16)

    es = contextlib.ExitStack()
    P = Prog(nc)
    for chn in ("DS", "DG", "T", "V", "S", "G"):
        P.channel(chn, es.enter_context(nc.semaphore("ch_" + chn)))

    def dma(eng, dst, src, waits=()):
        ch = "DS" if eng == "sync" else "DG"
        return P.emit(eng, lambda E_, d=dst, s=src: E_.dma_start(d, s),
                      waits=waits, inc=ch, inc_k=16)

    def op(eng, fn, waits=(), sw=True):
        ch = {"vector": "V", "scalar": "S", "tensor": "T", "gpsimd": "G"}[eng]
        return P.emit(eng, fn, waits=waits, inc=ch, sw=sw)

    st = {}

    QP = dict(i=0, sc={}, sg={}, g={}, v={})

    def quant(xin, qout, W, tin, tax=()):
        nb = W // 16
        pi = QP["i"] & 1
        QP["i"] += 1
        ax, bbt, sg = qax[pi], qbb[pi], msg[pi]
        bm, r2, s5 = qbm[pi], qr2[pi], qs5[pi]
        scfree = QP["sc"].get(pi)
        sgfree = QP["sg"].get(pi)
        gfree = QP["g"].get(pi)
        tsa = op("scalar", lambda E_: E_.activation(
            ax[:, 0:W], xin, AF.Abs), waits=(tin, scfree) + tuple(tax))
        tsg = op("scalar", lambda E_: E_.activation(
            sg[:, 0:W], xin, AF.Sign), waits=(tin, sgfree) + tuple(tax), sw=False)
        tbm = op("vector", lambda E_: E_.tensor_reduce(
            bm[:, 0:nb].rearrange("p (n o) -> p n o", o=1),
            ax[:, 0:W].rearrange("p (n b) -> p n b", b=16), AX.X, ALU.max),
            waits=(tsa, gfree), sw=False)
        op("vector", lambda E_: E_.tensor_scalar(
            s5[:, 0:nb], bm[:, 0:nb], 6e-12, float(np.float32(1.0 / 12.0)),
            ALU.max, ALU.mult), waits=(tbm,), sw=True)
        tr2 = op("vector", lambda E_: E_.reciprocal(r2[:, 0:nb], s5[:, 0:nb]),
                 sw=True)
        tb = op("vector", lambda E_: E_.tensor_tensor(
            qbb[pi][:, 0:W].rearrange("p (n b) -> p n b", b=16),
            ax[:, 0:W].rearrange("p (n b) -> p n b", b=16),
            bcast16(r2[:, 0:nb], nb), ALU.mult), waits=(tr2, tsa), sw=True)
        QP["sc"][pi] = tb
        vprev = QP["v"].get(1 - pi)
        bbW = bbt[:, 0:W]
        op("vector", lambda E_: E_.tensor_scalar(m1[:, 0:W], bbW, 0.5, None, ALU.is_gt),
           waits=(tb, vprev), sw=True)
        op("vector", lambda E_: E_.tensor_scalar(m2[:, 0:W], bbW, 1.5, None, ALU.is_gt), sw=False)
        op("vector", lambda E_: E_.tensor_scalar(m4[:, 0:W], bbW, 2.5, None, ALU.is_gt), sw=False)
        op("vector", lambda E_: E_.tensor_scalar(m5[:, 0:W], bbW, 3.5, None, ALU.is_gt), sw=False)
        op("vector", lambda E_: E_.tensor_tensor(m3[:, 0:W], m1[:, 0:W], m2[:, 0:W], ALU.add), sw=False)
        op("vector", lambda E_: E_.tensor_tensor(m6[:, 0:W], m4[:, 0:W], m5[:, 0:W], ALU.add), sw=False)
        op("vector", lambda E_: E_.tensor_scalar(m1[:, 0:W], bbW, 5.0, None, ALU.is_gt), sw=False)
        op("vector", lambda E_: E_.tensor_scalar(m2[:, 0:W], bbW, 7.0, None, ALU.is_gt), sw=False)
        op("vector", lambda E_: E_.tensor_scalar(m4[:, 0:W], bbW, 10.0, None, ALU.is_gt), sw=False)
        op("vector", lambda E_: E_.tensor_tensor(m5[:, 0:W], m1[:, 0:W], m2[:, 0:W], ALU.add), sw=False)
        op("vector", lambda E_: E_.tensor_tensor(m2[:, 0:W], m3[:, 0:W], m6[:, 0:W], ALU.add), sw=False)
        op("vector", lambda E_: E_.scalar_tensor_tensor(
            m1[:, 0:W], m5[:, 0:W], 2.0, m2[:, 0:W], ALU.mult, ALU.add), sw=True)
        tq2 = op("vector", lambda E_: E_.scalar_tensor_tensor(
            m3[:, 0:W], m4[:, 0:W], 4.0, m1[:, 0:W], ALU.mult, ALU.add), sw=False)
        tsgm = op("vector", lambda E_: E_.tensor_tensor(
            m4[:, 0:W], m3[:, 0:W], sg[:, 0:W], ALU.mult), waits=(tq2, tsg), sw=False)
        QP["sg"][pi] = tsgm
        tq = op("vector", lambda E_: E_.tensor_tensor(
            qout.rearrange("p (n b) -> p n b", b=16),
            m4[:, 0:W].rearrange("p (n b) -> p n b", b=16),
            bcast16(s5[:, 0:nb], nb), ALU.mult), waits=(tsgm,), sw=True)
        QP["g"][pi] = tq
        QP["v"][pi] = tq
        st["q_prev"] = tq
        return tq, tsg

    tid = dma("sync", identb[:], identb_d[:])
    tidf = dma("sync", identf[:], identf_d[:])
    tmA = dma("sync", cmA[:], cmA_d[:])
    tmB = dma("sync", cmB[:], cmB_d[:])
    teps = dma("sync", epsc[:], epsc_d[:])
    tscc = dma("sync", scc[:], scc_d[:])
    tnb = dma("sync", nbias[:], nbias_d[:])
    tc = dma("sync", xs0[0:B, 0:E], c_in[:, :])

    tcs = op("scalar", lambda E_: E_.activation(xs1[0:B, 0:E], xs0[0:B, 0:E],
                                                AF.Silu), waits=(tc,))
    csT = xhb
    tr_last = None
    for c_ in range(NE):
        tr_last = op("tensor", lambda E_, c_=c_: E_.transpose(
            psO[:, c_ * B:(c_ + 1) * B], xs1[0:B, c_ * 128:(c_ + 1) * 128],
            identf[0:B, 0:B]), waits=(tcs, tidf))
    tev_cs = op("vector", lambda E_: E_.tensor_copy(
        csT[:, 0:NE * B], psO[:, 0:NE * B]), waits=(tr_last,))

    wmst = (wst0, wst1, wtoA, wtoB)
    wmbf = (m1, m2, m5, m6)
    NT6 = E6 // 512
    bufdep = {}
    castdep = {}
    mm_last = None
    tmod_done = None
    for nt in range(NT6):
        mm_slot = nt % 2
        evac_prev = st.get("modev%d" % mm_slot)
        for e in range(NE):
            bi = (nt * NE + e) % 4
            tld = dma("gpsimd" if bi % 2 == 0 else "sync", wmst[bi][:, 0:512],
                      wmod_in[e * 128:(e + 1) * 128, nt * 512:(nt + 1) * 512],
                      waits=(bufdep.get(bi),))
            tcast = op("vector", lambda E_, bi=bi: E_.tensor_copy(
                wmbf[bi][:, 0:512], wmst[bi][:, 0:512]), waits=(tld,))
            castdep[bi] = tcast
            mm_last = op("tensor", lambda E_, e=e, bi=bi, ms=mm_slot: E_.matmul(
                psO[0:B, ms * 512:(ms + 1) * 512], csT[:, e * B:(e + 1) * B],
                wmbf[bi][:, 0:512], start=(e == 0), stop=(e == NE - 1)),
                waits=(tcast, tev_cs, evac_prev))
            bufdep[bi] = mm_last
        lo = nt * 512
        need_clamp = (2 * E <= lo < 3 * E) or (5 * E <= lo < 6 * E)
        tev2 = op("vector", lambda E_, ms=mm_slot: E_.tensor_copy(
            ot0[0:B, 0:512], psO[0:B, ms * 512:(ms + 1) * 512]), waits=(mm_last,))
        if need_clamp:
            op("vector", lambda E_: E_.tensor_scalar(
                ot1[0:B, 0:512], ot0[0:B, 0:512], 0.25, None, ALU.max))
            op("vector", lambda E_: E_.tensor_scalar(
                oa[0:B, 0:512], ot0[0:B, 0:512], -0.25, None, ALU.min))
            op("vector", lambda E_: E_.tensor_scalar(
                xsq[0:B, 0:512], ot0[0:B, 0:512], 0.0, None, ALU.is_ge))
            op("vector", lambda E_: E_.tensor_tensor(
                xs1[0:B, 0:512], ot1[0:B, 0:512], oa[0:B, 0:512], ALU.subtract))
            op("vector", lambda E_: E_.tensor_tensor(
                ot1[0:B, 0:512], xs1[0:B, 0:512], xsq[0:B, 0:512], ALU.mult))
            tev2 = op("vector", lambda E_: E_.tensor_tensor(
                ot0[0:B, 0:512], oa[0:B, 0:512], ot1[0:B, 0:512], ALU.add))
        st["modev%d" % mm_slot] = tev2
        tmod_done = dma("sync", modd[:, nt * 512:(nt + 1) * 512],
                        ot0[0:B, 0:512], waits=(tev2,))
    st["p0cast"] = castdep.get(2)
    st["p0castB"] = castdep.get(3)

    half = (NE + 1) // 2
    wqstg, wkstg = qT, kT
    tstg = None
    for ep in range(NE):
        tstg = dma("gpsimd", wqstg[:, ep * E:(ep + 1) * E],
                   wq_in[ep * 128:(ep + 1) * 128, :])
        tstg = dma("gpsimd", wkstg[:, ep * E:(ep + 1) * E],
                   wk_in[ep * 128:(ep + 1) * 128, :])
    tstg_s = None
    mev = {}
    tM_done = None
    tM_mm = None
    for ec in range(NE):
        for dh in range(NHF):
            slot = ((ec * NHF + dh) % 4) * 512
            mm = None
            for ep in range(NE):
                mm = op("tensor", lambda E_, ec=ec, dh=dh, ep=ep, slot=slot: E_.matmul(
                    psS[:, slot:slot + 512],
                    wqstg[:, ep * E + ec * 128: ep * E + ec * 128 + 128],
                    wkstg[:, ep * E + dh * 512: ep * E + dh * 512 + 512],
                    start=(ep == 0), stop=(ep == NE - 1)),
                    waits=(tstg, tstg_s, mev.get(slot)))
            tM_mm = mm
            tM_done = op("scalar", lambda E_, ec=ec, dh=dh, slot=slot: E_.activation(
                Msb[:, ec * E + dh * 512: ec * E + dh * 512 + 512],
                psS[:, slot:slot + 512], AF.Identity), waits=(mm,), sw=False)
            mev[slot] = tM_done

    bufdep2 = {0: mm_last, 1: mm_last}
    prev_tr = (tid,)
    twv_done = None
    qkv_tr_prev = mm_last
    for n in range(NE):
        bi = n % 2
        stg = wst0 if bi == 0 else wst1
        tca = []
        tprev = None
        for pc in range(E // 512):
            cbf = m3 if pc % 2 == 0 else m4
            tldp = dma("gpsimd", stg[:, 0:512],
                       wv_in[n * 128:(n + 1) * 128, pc * 512:(pc + 1) * 512],
                       waits=(bufdep2.get(bi), tprev, mm_last))
            tcp = op("vector", lambda E_, cbf=cbf, stg=stg: E_.tensor_copy(
                cbf[:, 0:512], stg[:, 0:512]),
                waits=(tldp, mm_last, qkv_tr_prev))
            tprev = tcp
            tca.append(tcp)
        bufdep2[bi] = tca[-1]
        trk = []
        for e in range(NE):
            pc = (e * 128) // 512
            cbf = m3 if pc % 2 == 0 else m4
            ecol = (e * 128) % 512
            bank = psT0 if e < half else psT1
            slot = e % half
            tt = op("tensor", lambda E_, bank=bank, slot=slot, cbf=cbf, ecol=ecol: E_.transpose(
                bank[:, slot * 128:(slot + 1) * 128],
                cbf[:, ecol:ecol + 128], identb[:]),
                waits=(tca[pc], tid, tM_mm) + prev_tr)
            trk.append(tt)
        qkv_tr_prev = trk[-1]

        def _evv(E_, n=n, banki=0):
            bank = psT0 if banki == 0 else psT1
            e0 = banki * half
            cnt = half if banki == 0 else NE - half
            base = wvT[:]
            dst = bass.AP(base.tensor, base.offset + e0 * E + n * 128,
                          [list(base.ap[0]), [E, cnt], [1, 128]])
            src = bank[:, 0:cnt * 128].rearrange("p (c w) -> p c w", w=128)
            return E_.tensor_copy(dst, src)
        ev0 = op("vector", lambda E_, n=n: _evv(E_, n, 0), waits=(trk[half - 1],))
        ev1 = op("vector", lambda E_, n=n: _evv(E_, n, 1), waits=(trk[-1],))
        twv_done = ev1
        prev_tr = (ev0, ev1)
    st["psT_ev"] = prev_tr[-1]
    st["wqo_tr"] = (twv_done, qkv_tr_prev)

    wchunks = []
    for w_d, wT_d, R, C in ((wg_in, wgT_d, I, E), (wu_in, wuT_d, I, E)):
        for r in range(R // 128):
            for cc in range(C // 512):
                wchunks.append((w_d, wT_d, r, cc * 512, 512, R))
    c_list = []
    c0 = 0
    while c0 < I:
        w_ = min(512, I - c0)
        c_list.append((c0, w_))
        c0 += w_
    for r in range(E // 128):
        for (c0, w_) in c_list:
            wchunks.append((wd_in, wdT_d, r, c0, w_, E))

    wqs = dict(free0=(mm_last, bufdep2.get(0)), free1=(mm_last, bufdep2.get(1)),
               stfree={}, bufi=0)

    def emit_wchunk(idx):
        if idx >= len(wchunks):
            return
        w_d, wT_d, r, c0, W, R = wchunks[idx]
        bi = wqs["bufi"]
        wqs["bufi"] ^= 1
        stg = wst0 if bi == 0 else wst1
        fr = wqs["free%d" % bi]
        tld = dma("gpsimd" if bi == 0 else "sync",
                  stg[:, 0:W], w_d[r * 128:(r + 1) * 128, c0:c0 + W],
                  waits=fr)
        tq, tsg = quant(stg[:, 0:W], wqo[:, 0:W], W, tld,
                        tax=st.get("wqo_tr", ()))
        wqs["free%d" % bi] = (tq, tsg)
        ntile = W // 128
        trk = []
        for e in range(ntile):
            bank = psT0 if e < 4 else psT1
            slot = e % 4
            tt = op("tensor", lambda E_, bank=bank, slot=slot, e=e: E_.transpose(
                bank[:, slot * 128:(slot + 1) * 128],
                wqo[:, e * 128:(e + 1) * 128], identb[:]),
                waits=(tq, st.get("psT_ev")))
            trk.append(tt)
        st["wqo_tr"] = (trk[-1],)
        wto = wto0
        stfree = wqs["stfree"].get(0)
        n0 = min(4, ntile)
        evs = [op("scalar", lambda E_, n0=n0, wto=wto: E_.activation(
            wto[:, 0:n0 * 128], psT0[:, 0:n0 * 128], AF.Identity),
            waits=(trk[n0 - 1], stfree), sw=False)]
        if ntile > 4:
            evs.append(op("scalar", lambda E_, ntile=ntile, wto=wto: E_.activation(
                wto[:, 4 * 128:ntile * 128], psT1[:, 0:(ntile - 4) * 128],
                AF.Identity), waits=(trk[-1], stfree), sw=False))
        st["psT_ev"] = evs[-1]
        dst = bass.AP(wT_d[:].tensor, c0 * R + r * 128,
                      [[R, 128], [128 * R, ntile], [1, 128]])
        tst = dma("gpsimd", dst,
                  wto[:, 0:ntile * 128].rearrange("p (c w) -> p c w", w=128),
                  waits=tuple(evs))
        wqs["stfree"][0] = tst

    WCH = [0]

    def next_wchunks(k):
        if not interleave_w:
            return
        for _ in range(k):
            if WCH[0] < len(wchunks):
                emit_wchunk(WCH[0])
                WCH[0] += 1

    ast = dict(x={}, xr={}, omst={}, tom=None, tga=None, sc_evac=None,
               texp=None, tmx=None, o_evac=None, ptr_last=None, t3=None,
               t5=None, xhb_tr=None, pv_last=None, trl=None,
               qslot={}, vslot={},
               qe={"q": None, "k": None, "v": None}, ptev={},
               ptT_mm={0: st["p0castB"], 1: st["p0castB"]},
               trlp={}, trmp={}, cap=None)
    xm_last = None
    for s in range(B):
        t_car = dma("sync", oa[0:1, 0:E], modd[s:s + 1, E:2 * E],
                    waits=(tmod_done, ast["tga"]))
        t_sar = dma("sync", orw[0:1, 0:E], modd[s:s + 1, 0:E],
                    waits=(tmod_done, ast["tom"]))
        t_gab = dma("sync", gab[:],
                    bass.AP(modd[:].tensor, s * E6 + 2 * E, [[0, 128], [1, E]]),
                    waits=(tmod_done, ast["tga"]))
        trc = trs = None
        for c_ in range(NE):
            trc = op("tensor", lambda E_, c_=c_: E_.transpose(
                psO[:, c_:c_ + 1], oa[0:1, c_ * 128:(c_ + 1) * 128],
                identf[0:1, 0:1]), waits=(t_car, tidf, ast["o_evac"]))
            trs = op("tensor", lambda E_, c_=c_: E_.transpose(
                psO[:, 512 + c_:512 + c_ + 1], orw[0:1, c_ * 128:(c_ + 1) * 128],
                identf[0:1, 0:1]), waits=(t_sar,))
        t_cap = op("scalar", lambda E_: E_.activation(
            caP[:, 0:NE], psO[:, 0:NE], AF.Identity, bias=1.0), waits=(trc,))
        t_sap = op("scalar", lambda E_: E_.activation(
            saP[:, 0:NE], psO[:, 512:512 + NE], AF.Identity), waits=(trs,))
        ast["o_evac"] = t_sap
        ast["cap"] = (t_cap, t_sap)

        xst = (xs0, xs1, xr0, xr1)
        for g in range(NSG):
            h_last = None
            for j in range(4):
                ti = g * 4 + j
                bi = ti % 4
                t_ld = dma("sync", xst[bi][:], x_in[s, ti * 128:(ti + 1) * 128, :],
                           waits=(ast["x"].get(bi), tmod_done,
                                  ast["xr"].get(bi - 2) if bi >= 2 else None,
                                  tcs if s == 0 and g == 0 else None))
                op("vector", lambda E_, bi=bi: E_.tensor_reduce(
                    stat[:, 0:1], xst[bi][:], AX.X, ALU.add),
                    waits=(t_ld, ast["t5"]))
                t2 = op("scalar", lambda E_, bi=bi: E_.activation(
                    xsq[:], xst[bi][:], AF.Square, accum_out=stat[:, 1:2]),
                    waits=(t_ld, ast["t3"], tmod_done, ast["tom"]), sw=False)
                ast["t3"] = t2
                op("vector", lambda E_: E_.tensor_scalar(
                    stat[:, 2:3], stat[:, 0:1], 1.0 / E, None, ALU.mult))
                op("vector", lambda E_: E_.tensor_scalar(
                    stat[:, 3:4], stat[:, 1:2], 1.0 / E, None, ALU.mult),
                    waits=(t2,))
                op("vector", lambda E_: E_.tensor_tensor(
                    stat[:, 4:5], stat[:, 2:3], stat[:, 2:3], ALU.mult))
                t4 = op("vector", lambda E_: E_.tensor_tensor(
                    stat[:, 5:6], stat[:, 3:4], stat[:, 4:5], ALU.subtract))
                t5 = op("scalar", lambda E_: E_.activation(
                    stat[:, 6:7], stat[:, 5:6], AF.Sqrt, bias=epsc[:, 0:1]),
                    waits=(t4, teps))
                ast["t5"] = t5
                t6 = op("vector", lambda E_: E_.reciprocal(
                    stat[:, 7:8], stat[:, 6:7]), waits=(t5,))
                t7 = op("vector", lambda E_: E_.scalar_tensor_tensor(
                    stat[:, 8:9], stat[:, 2:3], -1.0, stat[:, 7:8],
                    ALU.mult, ALU.mult))
                t8 = op("scalar", lambda E_, bi=bi: E_.activation(
                    xhb[:], xst[bi][:], AF.Identity, bias=stat[:, 8:9],
                    scale=stat[:, 7:8]),
                    waits=(t7, tmod_done, ast["xhb_tr"],
                           tev_cs if s == 0 and g == 0 else None), sw=False)
                ast["x"][bi] = t8
                trk = []
                for e in range(NE):
                    bank = psT0 if e < half else psT1
                    slot = e % half
                    tt = op("tensor", lambda E_, bank=bank, slot=slot, e=e: E_.transpose(
                        bank[:, slot * 128:(slot + 1) * 128],
                        xhb[:, e * 128:(e + 1) * 128], identb[:]),
                        waits=(t8, tid, st.get("psT_ev")))
                    trk.append(tt)
                ast["xhb_tr"] = trk[-1]
                for e in range(NE):
                    bank = psT0 if e < half else psT1
                    slot = e % half
                    h_last = op("scalar", lambda E_, bank=bank, slot=slot, e=e, j=j, g=g: E_.activation(
                        kT[:, e * S + g * 512 + j * 128: e * S + g * 512 + (j + 1) * 128],
                        bank[:, slot * 128:(slot + 1) * 128],
                        AF.Identity, bias=saP[:, e:e + 1], scale=caP[:, e:e + 1]),
                        waits=(trk[e], ast["texp"], tM_mm) + ast["cap"],
                        sw=(e == 0 and j == 0))
                st["psT_ev"] = h_last
            for n in range(NE):
                base = (n % 2) * 512
                mm = None
                for e in range(NE):
                    mm = op("tensor", lambda E_, n=n, e=e, base=base, g=g: E_.matmul(
                        psS[:, base:base + 512],
                        Msb[:, e * E + n * 128: e * E + n * 128 + 128],
                        kT[:, e * S + g * 512: e * S + (g + 1) * 512],
                        start=(e == 0), stop=(e == NE - 1)),
                        waits=(h_last, tM_done, ast["qslot"].get(base)))
                ast["qe"]["q"] = op(
                    "scalar", lambda E_, n=n, g=g, base=base: E_.activation(
                        qT[:, n * S + g * 512: n * S + (g + 1) * 512],
                        psS[:, base:base + 512], AF.Identity),
                    waits=(mm, ast["pv_last"], tM_mm), sw=False)
                ast["qslot"][base] = ast["qe"]["q"]
            for j in range(4):
                ti = g * 4 + j
                for hf in range(NHF):
                    mm = None
                    for e in range(NE):
                        mm = op("tensor", lambda E_, j=j, hf=hf, e=e, g=g: E_.matmul(
                            psS[:, 1024 + hf * 512: 1024 + (hf + 1) * 512],
                            kT[:, e * S + g * 512 + j * 128: e * S + g * 512 + (j + 1) * 128],
                            wvT[:, e * E + hf * 512: e * E + (hf + 1) * 512],
                            start=(e == 0), stop=(e == NE - 1)),
                            waits=(h_last, twv_done, ast["vslot"].get(hf)))
                    ast["qe"]["v"] = op(
                        "scalar", lambda E_, ti=ti, hf=hf: E_.activation(
                            vsb[:, ti * E + hf * 512: ti * E + (hf + 1) * 512],
                            psS[:, 1024 + hf * 512: 1024 + (hf + 1) * 512],
                            AF.Identity),
                        waits=(mm, ast["pv_last"]), sw=False)
                    ast["vslot"][hf] = ast["qe"]["v"]
            next_wchunks(1)
        ast["qe"]["k"] = h_last

        ssbs = (ssb, ssb)

        def a_stage1(i):
            W = (i + 1) * 128
            NK4 = (W + 511) // 512
            dk4 = (i * 128) // 512
            doff = (i * 128) % 512
            sc_mm = None
            for e in range(NE):
                for k4 in range(NK4):
                    w_ = min(512, W - k4 * 512)
                    sc_mm = op("tensor", lambda E_, e=e, k4=k4, w_=w_, i=i: E_.matmul(
                        psS[:, k4 * 512:k4 * 512 + w_],
                        qT[:, e * S + i * 128:e * S + (i + 1) * 128],
                        kT[:, e * S + k4 * 512:e * S + k4 * 512 + w_],
                        start=(e == 0),
                        stop=(e == NE - 1 and k4 != dk4)),
                        waits=(ast["qe"]["q"], ast["qe"]["k"], ast["sc_evac"]))
            sc_mm = op("tensor", lambda E_, dk4=dk4, doff=doff: E_.matmul(
                psS[:, dk4 * 512 + doff: dk4 * 512 + doff + 128],
                cmA[:], cmB[:], start=False, stop=True),
                waits=(tmA, tmB))
            ev = None
            for k4 in range(NK4):
                w_ = min(512, W - k4 * 512)
                ev = op("scalar", lambda E_, k4=k4, w_=w_, i=i: E_.activation(
                    ssbs[i % 2][:, k4 * 512:k4 * 512 + w_],
                    psS[:, k4 * 512:k4 * 512 + w_],
                    AF.Identity, scale=scc[:, 0:1]),
                    waits=(sc_mm, tscc, st["p0cast"]), sw=False)
            ast["sc_evac"] = ev

        def a_stage2_front(i):
            pi2 = i % 2
            texp = op("scalar", lambda E_, W=(i + 1) * 128, i=i, pi2=pi2: E_.activation(
                psb[:, 0:W], ssbs[i % 2][:, 0:W], AF.Exp, bias=nbias[:, 0:1],
                accum_out=mx1[:, pi2 * 2:pi2 * 2 + 1]),
                waits=(tnb, ast["ptr_last"], ast["trlp"].get(pi2)), sw=False)
            ast["texp"] = texp
            trl = op("vector", lambda E_, pi2=pi2: E_.reciprocal(
                mx1[:, pi2 * 2 + 1:pi2 * 2 + 2], mx1[:, pi2 * 2:pi2 * 2 + 1]),
                waits=(texp, ast["trmp"].get(pi2)))
            ast["trl"] = trl
            ast["trlp"][pi2] = trl

        def a_stage2(i, s):
            W = (i + 1) * 128
            bi = i % 2
            xrb = xr0 if bi == 0 else xr1
            t_ldr = dma("sync", xrb[:], x_in[s, i * 128:(i + 1) * 128, :],
                        waits=(ast["xr"].get(bi), ast["x"].get(bi + 2),
                               t_car, t_sar, t_cap if bi == 0 else t_sap))
            texp = ast["texp"]
            trl = ast["trl"]
            pv_mm = None
            NKT = i + 1
            for g4 in range(0, NKT, 4):
                cnt = min(4, NKT - g4)
                pgrp = (g4 // 4) % 2
                bank = psT0 if pgrp == 0 else psT1
                trk = []
                for kk in range(cnt):
                    kt = g4 + kk
                    tt = op("tensor", lambda E_, bank=bank, kk=kk, kt=kt: E_.transpose(
                        bank[:, kk * 128:(kk + 1) * 128],
                        psb[:, kt * 128:(kt + 1) * 128], identb[:]),
                        waits=(texp, st.get("psT_ev"), ast["ptev"].get(pgrp)))
                    trk.append(tt)
                ast["ptr_last"] = trk[-1]
                tevp = op("scalar", lambda E_, bank=bank, cnt=cnt, pgrp=pgrp: E_.activation(
                    ptT[:, pgrp * 512: pgrp * 512 + cnt * 128], bank[:, 0:cnt * 128],
                    AF.Identity), waits=(trk[-1], ast["ptT_mm"].get(pgrp)), sw=False)
                st["psT_ev"] = tevp
                ast["ptev"][pgrp] = tevp
                for kk in range(cnt):
                    kt = g4 + kk
                    for hf in range(NHF):
                        pv_mm = op("tensor", lambda E_, kt=kt, kk=kk, hf=hf, pgrp=pgrp, i=i: E_.matmul(
                            psO[:, hf * 512:(hf + 1) * 512],
                            ptT[:, pgrp * 512 + kk * 128: pgrp * 512 + (kk + 1) * 128],
                            vsb[:, kt * E + hf * 512: kt * E + (hf + 1) * 512],
                            start=(kt == 0), stop=(kt == i)),
                            waits=(tevp, ast["qe"]["v"], ast["o_evac"]))
                ast["ptT_mm"][pgrp] = pv_mm
            ast["pv_last"] = pv_mm
            oev = None
            for hf in range(NHF):
                oev = op("scalar", lambda E_, hf=hf: E_.activation(
                    oa[:, hf * 512:(hf + 1) * 512], psO[:, hf * 512:(hf + 1) * 512],
                    AF.Identity), waits=(pv_mm, ast["tga"]), sw=False)
            ast["o_evac"] = oev
            tga = op("gpsimd", lambda E_: E_.tensor_tensor(
                xsq[:], oa[:], gab[:], ALU.mult), waits=(oev, t_gab))
            ast["tga"] = tga
            pi2 = i % 2
            _m = mx1[:]
            rsc = bass.AP(_m.tensor, _m.offset + pi2 * 2 + 1,
                          [list(_m.ap[0]), [0, E]])
            trm = op("gpsimd", lambda E_, rsc=rsc: E_.tensor_tensor(
                orw[:], xsq[:], rsc, ALU.mult), waits=(trl,))
            ast["trmp"][pi2] = trm
            obuf = ot0 if bi == 0 else ot1
            tom = op("gpsimd", lambda E_, obuf=obuf, xrb=xrb: E_.tensor_tensor(
                obuf[:], orw[:], xrb[:], ALU.add),
                waits=(trm, t_ldr, ast["omst"].get(bi)))
            ast["tom"] = tom
            ast["xr"][bi] = tom
            xm = dma("sync", xmid_d[s, i * 128:(i + 1) * 128, :], obuf[:],
                     waits=(tom,))
            ast["omst"][bi] = xm
            next_wchunks(2)
            return xm

        a_stage1(0)
        for i in range(NS):
            a_stage2_front(i)
            if i + 1 < NS:
                a_stage1(i + 1)
            xm_last = a_stage2(i, s)

    while WCH[0] < len(wchunks):
        emit_wchunk(WCH[0])
        WCH[0] += 1

    barrier = tuple(x for x in (xm_last, wqs["stfree"].get(0),
                                wqs["stfree"].get(1)) if x is not None)
    t_wB = None
    for e in range(NE):
        t_wB = dma("gpsimd", wgT[:, e * I:(e + 1) * I],
                   wgT_d[e * 128:(e + 1) * 128, :], waits=barrier)
    for e in range(NE):
        t_wB = dma("gpsimd", wuT[:, e * I:(e + 1) * I],
                   wuT_d[e * 128:(e + 1) * 128, :], waits=barrier)
    for ic in range(NI):
        t_wB = dma("gpsimd", wdT[:, ic * E:(ic + 1) * E],
                   wdT_d[ic * 128:(ic + 1) * 128, :], waits=barrier)

    bst = dict(x={}, ost={}, tout=None, tgb=barrier[0], t3=None, t5=None,
               h2q_tr=(), hq_tr=(), gate_mm=None, mmd=None, thd=None,
               tsg=None, t9=None, sl={0: None, 1: None})
    itiles = c_list

    def b_back(s, i, bi, xb_, mmd, t_gbb):
        tgb = op("vector", lambda E_: E_.tensor_tensor(
            oa[:], psO[:, 0:E], gab[:], ALU.mult),
            waits=(mmd, t_gbb, bst["tout"]), sw=False)
        bst["tgb"] = tgb
        obuf = ot0 if bi == 0 else ot1
        tout = op("gpsimd", lambda E_, obuf=obuf, xb_=xb_: E_.tensor_tensor(
            obuf[:], oa[:], xb_[:], ALU.add),
            waits=(tgb, bst["ost"].get(bi)))
        bst["tout"] = tout
        bst["x"][bi] = tout
        tst = dma("gpsimd", out_d[s, i * 128:(i + 1) * 128, :], obuf[:],
                  waits=(tout,))
        bst["ost"][bi] = tst

    pend = None
    for s in range(B):
        if pend is not None:
            b_back(*pend)
            pend = None
        t_cbr = dma("sync", h2f[:],
                    bass.AP(modd[:].tensor, s * E6 + 4 * E, [[0, 128], [1, E]]),
                    waits=barrier + (bst["t9"], bst["tsg"]))
        t_cb1 = op("vector", lambda E_: E_.tensor_scalar(
            cbb[:], h2f[:], 1.0, None, ALU.add), waits=(t_cbr, bst["t9"]))
        t_sbb = dma("sync", sbb[:],
                    bass.AP(modd[:].tensor, s * E6 + 3 * E, [[0, 128], [1, E]]),
                    waits=barrier + (bst["t9"],))
        t_gbb = dma("sync", gab[:],
                    bass.AP(modd[:].tensor, s * E6 + 5 * E, [[0, 128], [1, E]]),
                    waits=barrier + (ast["tga"], bst["tgb"]))
        for i in range(NS):
            bi = i % 2
            xb_ = xs0 if bi == 0 else xs1
            t_ld = dma("sync", xb_[:], xmid_d[s, i * 128:(i + 1) * 128, :],
                       waits=(bst["x"].get(bi), xm_last))
            op("vector", lambda E_, xb_=xb_: E_.tensor_reduce(
                stat[:, 0:1], xb_[:], AX.X, ALU.add), waits=(t_ld, bst["t5"]))
            t2 = op("scalar", lambda E_, xb_=xb_: E_.activation(
                xsq[:], xb_[:], AF.Square, accum_out=stat[:, 1:2]),
                waits=(t_ld, bst["t3"], ast["tga"], bst["t9"]), sw=False)
            bst["t3"] = t2
            op("vector", lambda E_: E_.tensor_scalar(
                stat[:, 2:3], stat[:, 0:1], 1.0 / E, None, ALU.mult))
            op("vector", lambda E_: E_.tensor_scalar(
                stat[:, 3:4], stat[:, 1:2], 1.0 / E, None, ALU.mult),
                waits=(t2,))
            op("vector", lambda E_: E_.tensor_tensor(
                stat[:, 4:5], stat[:, 2:3], stat[:, 2:3], ALU.mult))
            t4 = op("vector", lambda E_: E_.tensor_tensor(
                stat[:, 5:6], stat[:, 3:4], stat[:, 4:5], ALU.subtract))
            t5 = op("scalar", lambda E_: E_.activation(
                stat[:, 6:7], stat[:, 5:6], AF.Sqrt, bias=epsc[:, 0:1]),
                waits=(t4, teps))
            bst["t5"] = t5
            t6 = op("vector", lambda E_: E_.reciprocal(stat[:, 7:8], stat[:, 6:7]),
                    waits=(t5,))
            t7 = op("vector", lambda E_: E_.scalar_tensor_tensor(
                stat[:, 8:9], stat[:, 2:3], -1.0, stat[:, 7:8], ALU.mult, ALU.mult))
            t8 = op("scalar", lambda E_, xb_=xb_: E_.activation(
                h2f[:], xb_[:], AF.Identity, bias=stat[:, 8:9], scale=stat[:, 7:8]),
                waits=(t7, t_cb1, bst["tsg"]), sw=False)
            t9a = op("gpsimd", lambda E_: E_.tensor_tensor(
                xsq[:], h2f[:], cbb[:], ALU.mult), waits=(t8, t_cb1))
            t9 = op("gpsimd", lambda E_: E_.tensor_tensor(
                h2f[:], xsq[:], sbb[:], ALU.add), waits=(t_sbb,))
            bst["t9"] = t9
            tqs = None
            for hc in range(E // 512):
                tqs, tsgh2 = quant(h2f[:, hc * 512:(hc + 1) * 512],
                                   h2q[:, hc * 512:(hc + 1) * 512], 512, t9,
                                   tax=bst["h2q_tr"])
                bst["tsg"] = tsgh2
            tq = tqs
            trk = []
            for e in range(NE):
                bank = psT0 if e < half else psT1
                slot = e % half
                tt = op("tensor", lambda E_, bank=bank, slot=slot, e=e: E_.transpose(
                    bank[:, slot * 128:(slot + 1) * 128],
                    h2q[:, e * 128:(e + 1) * 128], identb[:]),
                    waits=(tq, st.get("psT_ev"), tid))
                trk.append(tt)
            bst["h2q_tr"] = (trk[-1],)
            tev0 = op("scalar", lambda E_: E_.activation(
                h2qT[:, 0:half * 128], psT0[:, 0:half * 128], AF.Identity),
                waits=(trk[half - 1], bst["gate_mm"]), sw=False)
            tev1 = op("scalar", lambda E_: E_.activation(
                h2qT[:, half * 128:NE * 128], psT1[:, 0:(NE - half) * 128],
                AF.Identity), waits=(trk[-1], bst["gate_mm"]), sw=False)
            st["psT_ev"] = tev1
            th2qT = tev1
            sltoks = []
            for it, (c0, w_) in enumerate(itiles):
                sl = it % 2
                gslot = 0 if sl == 0 else 1024
                uslot = 512 if sl == 0 else 1536
                mmg = None
                for e in range(NE):
                    mmg = op("tensor", lambda E_, e=e, c0=c0, w_=w_, gslot=gslot: E_.matmul(
                        psS[:, gslot:gslot + w_], h2qT[:, e * 128:(e + 1) * 128],
                        wgT[:, e * I + c0:e * I + c0 + w_],
                        start=(e == 0), stop=(e == NE - 1)),
                        waits=(th2qT, t_wB, bst["sl"][sl]))
                for e in range(NE):
                    mmg = op("tensor", lambda E_, e=e, c0=c0, w_=w_, uslot=uslot: E_.matmul(
                        psS[:, uslot:uslot + w_], h2qT[:, e * 128:(e + 1) * 128],
                        wuT[:, e * I + c0:e * I + c0 + w_],
                        start=(e == 0), stop=(e == NE - 1)))
                bst["gate_mm"] = mmg
                tsl = op("scalar", lambda E_, gslot=gslot, w_=w_: E_.activation(
                    sgf[:, 0:w_], psS[:, gslot:gslot + w_], AF.Silu),
                    waits=(mmg, bst["thd"]), sw=False)
                thd = op("vector", lambda E_, uslot=uslot, w_=w_: E_.tensor_tensor(
                    hidf[:, 0:w_], sgf[:, 0:w_], psS[:, uslot:uslot + w_], ALU.mult),
                    waits=(tsl, bst["sl"][sl], bst["tsg"]), sw=False)
                bst["thd"] = thd
                hq_dst = bass.AP(hqsb[:].tensor, hqsb[:].offset + c0,
                                 [list(hqsb[:].ap[0]), [1, w_]])
                tqh, tsg_h = quant(hidf[:, 0:w_], hq_dst, w_, thd,
                                   tax=bst["hq_tr"])
                bst["sl"][sl] = tqh
                bst["tsg"] = tsg_h
                sltoks.append(tqh)
            trk2 = None
            for icb in range(0, NI, 4):
                cnt = min(4, NI - icb)
                bank = psT0 if (icb // 4) % 2 == 0 else psT1
                grp = []
                for kk in range(cnt):
                    trk2 = op("tensor", lambda E_, bank=bank, kk=kk, icb=icb: E_.transpose(
                        bank[:, kk * 128:(kk + 1) * 128],
                        hqsb[:, (icb + kk) * 128:(icb + kk + 1) * 128], identb[:]),
                        waits=(sltoks[icb // 4], st.get("psT_ev")))
                    grp.append(trk2)
                tevh = op("scalar", lambda E_, bank=bank, cnt=cnt, icb=icb: E_.activation(
                    hqT[:, icb * 128:(icb + cnt) * 128], bank[:, 0:cnt * 128],
                    AF.Identity), waits=(grp[-1], bst["mmd"]), sw=False)
                st["psT_ev"] = tevh
            bst["hq_tr"] = (trk2,)
            thqT = st["psT_ev"]
            if pend is not None:
                b_back(*pend)
            mmd = None
            for ic in range(NI):
                for hf in range(NHF):
                    mmd = op("tensor", lambda E_, ic=ic, hf=hf: E_.matmul(
                        psO[:, hf * 512:(hf + 1) * 512],
                        hqT[:, ic * 128:(ic + 1) * 128],
                        wdT[:, ic * E + hf * 512: ic * E + (hf + 1) * 512],
                        start=(ic == 0), stop=(ic == NI - 1)),
                        waits=(thqT, t_wB, bst["tgb"]))
            bst["mmd"] = mmd
            pend = (s, i, bi, xb_, mmd, t_gbb)
    b_back(*pend)

    final_ds = Tok("DS", P.chan["DS"][1], "sync", True)
    final_dg = Tok("DG", P.chan["DG"][1], "gpsimd", True)
    P.wait_only("sync", (final_ds,))
    P.wait_only("gpsimd", (final_dg,))

    with nc.Block() as block:
        @block.sync
        def _(sync):
            P.run(sync, "sync")

        @block.vector
        def _(vector):
            P.run(vector, "vector")

        @block.scalar
        def _(scalar):
            P.run(scalar, "scalar")

        @block.tensor
        def _(tensor):
            P.run(tensor, "tensor")

        @block.gpsimd
        def _(gpsimd):
            P.run(gpsimd, "gpsimd")

    es.close()
    return nc


_NC_CACHE = {}


def kernel(L0_attn_x, L0_attn_WQ, L0_attn_WK, L0_attn_WV,
           L0_ffn_w_gate, L0_ffn_w_up, L0_ffn_w_down, L0_c, L0_wmod):
    cfg = FULL_CFG
    n = cfg["NCORES"]
    bpc = L0_attn_x.shape[0] // n
    if "full" not in _NC_CACHE:
        _NC_CACHE["full"] = build(cfg)
    nc = _NC_CACHE["full"]
    x = np.ascontiguousarray(L0_attn_x)
    c = np.ascontiguousarray(L0_c)
    shared = {
        "wq_in": np.ascontiguousarray(L0_attn_WQ),
        "wk_in": np.ascontiguousarray(L0_attn_WK),
        "wv_in": np.ascontiguousarray(L0_attn_WV),
        "wg_in": np.ascontiguousarray(L0_ffn_w_gate),
        "wu_in": np.ascontiguousarray(L0_ffn_w_up),
        "wd_in": np.ascontiguousarray(L0_ffn_w_down),
        "wmod_in": np.ascontiguousarray(L0_wmod),
    }
    in_maps = []
    for ci in range(n):
        m = dict(shared)
        m["x_in"] = np.ascontiguousarray(x[ci * bpc:(ci + 1) * bpc])
        m["c_in"] = np.ascontiguousarray(c[ci * bpc:(ci + 1) * bpc])
        in_maps.append(m)
    res = run_bass_kernel_spmd(nc, in_maps, core_ids=list(range(n)))
    return np.concatenate([np.asarray(r["out"]) for r in res.results], axis=0)


# version 15 (final)
# speedup vs baseline: 1.0453x; 1.0012x over previous
import contextlib
import numpy as np
import ml_dtypes
import concourse.bass as bass
import concourse.mybir as mybir
from concourse.bass_utils import run_bass_kernel_spmd

F32 = mybir.dt.float32
BF16 = mybir.dt.bfloat16
AF = mybir.ActivationFunctionType
ALU = mybir.AluOpType
AX = mybir.AxisListType

ENGINES = ("sync", "vector", "scalar", "tensor", "gpsimd")

FULL_CFG = dict(B=4, S=2048, E=1024, I=2816, NCORES=8)


class Tok:
    __slots__ = ("ch", "val", "eng", "dma")

    def __init__(self, ch, val, eng, dma):
        self.ch, self.val, self.eng, self.dma = ch, val, eng, dma


class Prog:

    def __init__(self, nc):
        self.nc = nc
        self.streams = {e: [] for e in ENGINES}
        self.chan = {}
        self.water = {}

    def channel(self, name, sem):
        self.chan[name] = [sem, 0]

    CHAINED = ("vector", "scalar", "gpsimd")

    def emit(self, eng, fn, waits=(), inc=None, inc_k=1, sw=True):
        wl = []
        if sw and eng in self.CHAINED and inc is not None:
            cur = self.chan[inc][1]
            key = (eng, inc)
            if cur > 0 and self.water.get(key, -1) < cur:
                self.water[key] = cur
                wl.append((inc, cur))
        for t in waits:
            if t is None:
                continue
            if (not t.dma) and t.eng == eng:
                continue
            key = (eng, t.ch)
            if self.water.get(key, -1) >= t.val:
                continue
            self.water[key] = t.val
            wl.append((t.ch, t.val))
        tok = None
        if inc is not None:
            c = self.chan[inc]
            c[1] += inc_k
            tok = Tok(inc, c[1], eng, inc_k >= 16)
        self.streams[eng].append((wl, fn, inc, inc_k))
        return tok

    def wait_only(self, eng, waits):
        self.emit(eng, None, waits=waits)

    def run(self, engobj, eng):
        for wl, fn, inc, inc_k in self.streams[eng]:
            for ch, val in wl:
                engobj.wait_ge(self.chan[ch][0], val)
            if fn is None:
                continue
            ins = fn(engobj)
            if inc is not None:
                ins.then_inc(self.chan[inc][0], inc_k)


def bcast16(ap2d, nblk, blk=16):
    return bass.AP(ap2d.tensor, ap2d.offset, [list(ap2d.ap[0]), [1, nblk], [0, blk]])


def build(cfg, debug=False, interleave_w=True):
    B, S, E, I = cfg["B"], cfg["S"], cfg["E"], cfg["I"]
    NE, NS, NI = E // 128, S // 128, I // 128
    NSG = S // 512
    NHF = E // 512
    SCALE = float(E) ** -0.5
    E6 = 6 * E
    assert S % 512 == 0 and E % 512 == 0 and I % 128 == 0

    nc = bass.Bass("TRN2", target_bir_lowering=False, detect_race_conditions=False)

    x_in = nc.dram_tensor("x_in", [B, S, E], F32, kind="ExternalInput")
    wq_in = nc.dram_tensor("wq_in", [E, E], F32, kind="ExternalInput")
    wk_in = nc.dram_tensor("wk_in", [E, E], F32, kind="ExternalInput")
    wv_in = nc.dram_tensor("wv_in", [E, E], F32, kind="ExternalInput")
    wg_in = nc.dram_tensor("wg_in", [I, E], F32, kind="ExternalInput")
    wu_in = nc.dram_tensor("wu_in", [I, E], F32, kind="ExternalInput")
    wd_in = nc.dram_tensor("wd_in", [E, I], F32, kind="ExternalInput")
    c_in = nc.dram_tensor("c_in", [B, E], F32, kind="ExternalInput")
    wmod_in = nc.dram_tensor("wmod_in", [E, E6], F32, kind="ExternalInput")
    out_d = nc.dram_tensor("out", [B, S, E], F32, kind="ExternalOutput")

    _k = dict(kind="ExternalOutput") if debug else {}
    xmid_d = nc.dram_tensor("xmid_d", [B, S, E], F32, **_k)
    modd = nc.dram_tensor("modd", [B, E6], F32, **_k)
    qkT_d = nc.dram_tensor("qkT_d", [2 * NE * NE * 128, 128], BF16, **_k)
    wgT_d = nc.dram_tensor("wgT_d", [E, I], BF16, **_k)
    wuT_d = nc.dram_tensor("wuT_d", [E, I], BF16, **_k)
    wdT_d = nc.dram_tensor("wdT_d", [I, E], BF16, **_k)

    identb_d = nc.inline_tensor(np.eye(128, dtype=ml_dtypes.bfloat16), name="identb_c")
    identf_d = nc.inline_tensor(np.eye(128, dtype=np.float32), name="identf_c")
    _c = np.float32(np.sqrt(1e29))
    _ka = (-_c * (np.arange(128)[:, None] > np.arange(128)[None, :])
           ).astype(ml_dtypes.bfloat16)
    _kb = (_c * (np.arange(128)[:, None] <= np.arange(128)[None, :])
           ).astype(ml_dtypes.bfloat16)
    cmA_d = nc.inline_tensor(_ka, name="cmA_c")
    cmB_d = nc.inline_tensor(_kb, name="cmB_c")
    epsc_d = nc.inline_tensor(np.full((128, 1), 1e-5, dtype=np.float32), name="epsc_c")
    scc_d = nc.inline_tensor(np.full((128, 1), SCALE, dtype=np.float32), name="scc_c")
    nbias_d = nc.inline_tensor(np.full((128, 1), -12.0, dtype=np.float32),
                               name="nbias_c")

    SB_BASE = (nc.sbuf_base + 63) // 64 * 64
    SB_LIMIT = nc.sbuf_base + nc.sbuf_bytes_remaining
    off = [0]

    def sbt(name, shape, dtype, align=64):
        per = int(np.prod(shape[1:])) * mybir.dt.size(dtype)
        off[0] = (off[0] + align - 1) // align * align
        h = nc.alloc_sbuf_tensor_at(name, shape, dtype, offset=SB_BASE + off[0])
        off[0] += per
        return h

    identb = sbt("identb", [128, 128], BF16)
    identf = sbt("identf", [128, 128], F32)
    cmA = sbt("cmA", [128, 128], BF16)
    cmB = sbt("cmB", [128, 128], BF16)
    epsc = sbt("epsc", [128, 1], F32)
    scc = sbt("scc", [128, 1], F32)
    nbias = sbt("nbias", [128, 1], F32)
    xs0 = sbt("xs0", [128, E], F32)
    xs1 = sbt("xs1", [128, E], F32)
    ot0 = sbt("ot0", [128, E], F32)
    ot1 = sbt("ot1", [128, E], F32)
    oa = sbt("oa", [128, E], F32)
    xsq = sbt("xsq", [128, E], F32)
    gab = sbt("gab", [128, E], F32)
    caP = sbt("caP", [128, NE], F32)
    saP = sbt("saP", [128, NE], F32)
    stat = sbt("stat", [128, 12], F32)
    mx1 = sbt("mx1", [128, 4], F32)
    qax = (sbt("qax0", [128, 512], F32), sbt("qax1", [128, 512], F32))
    qbb = (sbt("qbb0", [128, 512], BF16), sbt("qbb1", [128, 512], BF16))
    msg = (sbt("msg0", [128, 512], BF16), sbt("msg1", [128, 512], BF16))
    m1 = sbt("m1", [128, 512], BF16)
    m2 = sbt("m2", [128, 512], BF16)
    m3 = sbt("m3", [128, 512], BF16)
    m4 = sbt("m4", [128, 512], BF16)
    m5 = sbt("m5", [128, 512], BF16)
    m6 = sbt("m6", [128, 512], BF16)
    qbm = (sbt("qbm0", [128, 32], F32), sbt("qbm1", [128, 32], F32))
    qr2 = (sbt("qr20", [128, 32], F32), sbt("qr21", [128, 32], F32))
    qs5 = (sbt("qs50", [128, 32], F32), sbt("qs51", [128, 32], F32))

    pers_end = off[0]
    xr0 = sbt("xr0", [128, E], F32)
    xr1 = sbt("xr1", [128, E], F32)
    xhb = sbt("xhb", [128, E], BF16)
    qT = sbt("qT", [128, NE * S], BF16)
    kT = sbt("kT", [128, NE * S], BF16)
    vsb = sbt("vsb", [128, NS * E], BF16)
    wvT = sbt("wvT", [128, NE * E], BF16)
    Msb = sbt("Msb", [128, NE * E], BF16)
    ssb = sbt("ssb", [128, S], F32)
    psb = sbt("psb", [128, S], BF16)
    ptT = sbt("ptT", [128, 8 * 128], BF16)
    wst0 = sbt("wst0", [128, 512], F32)
    wst1 = sbt("wst1", [128, 512], F32)
    wqo = sbt("wqo", [128, 512], BF16)
    wto0 = sbt("wto0", [128, NE * 128], BF16)
    orw = sbt("orw", [128, E], F32)
    wtoA = nc.alloc_sbuf_tensor_at("wtoA", [128, 512], F32,
                                   offset=nc.lookup_mloc(ssb).addr)
    wtoB = nc.alloc_sbuf_tensor_at("wtoB", [128, 512], F32,
                                   offset=nc.lookup_mloc(ptT).addr)
    a_end = off[0]
    print("A region end:", a_end, "limit:", SB_LIMIT - SB_BASE)
    assert SB_BASE + a_end <= SB_LIMIT, f"SBUF over budget (A): {a_end}"

    off[0] = pers_end
    wgT = sbt("wgT_s", [128, NE * I], BF16)
    wuT = sbt("wuT_s", [128, NE * I], BF16)
    wdT = sbt("wdT_s", [128, NI * E], BF16)
    h2f = sbt("h2f", [128, E], F32)
    h2q = sbt("h2q", [128, E], BF16)
    hqT = sbt("hqT", [128, NI * 128], BF16)
    h2qT = sbt("h2qT", [128, E], BF16)
    hqsb = sbt("hqsb", [128, I], BF16)
    sgf = sbt("sgf", [128, 512], F32)
    hidf = sbt("hidf", [128, 512], F32)
    cbb = sbt("cbb", [128, E], F32)
    sbb = sbt("sbb", [128, E], F32)
    b_end = off[0]
    print("B region end:", b_end)
    assert SB_BASE + b_end <= SB_LIMIT, f"SBUF over budget (B): {b_end}"

    psS = nc.alloc_psum_tensor("psS", [128, 2048], F32)
    psO = nc.alloc_psum_tensor("psO", [128, 1024], F32)
    psT0 = nc.alloc_psum_tensor("psT0", [128, 1024], BF16)
    psT1 = nc.alloc_psum_tensor("psT1", [128, 1024], BF16)

    es = contextlib.ExitStack()
    P = Prog(nc)
    for chn in ("DS", "DG", "T", "V", "S", "G"):
        P.channel(chn, es.enter_context(nc.semaphore("ch_" + chn)))

    def dma(eng, dst, src, waits=()):
        ch = "DS" if eng == "sync" else "DG"
        return P.emit(eng, lambda E_, d=dst, s=src: E_.dma_start(d, s),
                      waits=waits, inc=ch, inc_k=16)

    def op(eng, fn, waits=(), sw=True):
        ch = {"vector": "V", "scalar": "S", "tensor": "T", "gpsimd": "G"}[eng]
        return P.emit(eng, fn, waits=waits, inc=ch, sw=sw)

    st = {}

    QP = dict(i=0, sc={}, sg={}, g={}, v={})

    def quant(xin, qout, W, tin, tax=()):
        nb = W // 16
        pi = QP["i"] & 1
        QP["i"] += 1
        ax, bbt, sg = qax[pi], qbb[pi], msg[pi]
        bm, r2, s5 = qbm[pi], qr2[pi], qs5[pi]
        scfree = QP["sc"].get(pi)
        sgfree = QP["sg"].get(pi)
        gfree = QP["g"].get(pi)
        tsa = op("scalar", lambda E_: E_.activation(
            ax[:, 0:W], xin, AF.Abs), waits=(tin, scfree) + tuple(tax))
        tsg = op("scalar", lambda E_: E_.activation(
            sg[:, 0:W], xin, AF.Sign), waits=(tin, sgfree) + tuple(tax), sw=False)
        tbm = op("vector", lambda E_: E_.tensor_reduce(
            bm[:, 0:nb].rearrange("p (n o) -> p n o", o=1),
            ax[:, 0:W].rearrange("p (n b) -> p n b", b=16), AX.X, ALU.max),
            waits=(tsa, gfree), sw=False)
        op("vector", lambda E_: E_.tensor_scalar(
            s5[:, 0:nb], bm[:, 0:nb], 6e-12, 1.0 / 12.0,
            ALU.max, ALU.mult), waits=(tbm,), sw=True)
        tr2 = op("vector", lambda E_: E_.reciprocal(r2[:, 0:nb], s5[:, 0:nb]),
                 sw=True)
        tb = op("vector", lambda E_: E_.tensor_tensor(
            qbb[pi][:, 0:W].rearrange("p (n b) -> p n b", b=16),
            ax[:, 0:W].rearrange("p (n b) -> p n b", b=16),
            bcast16(r2[:, 0:nb], nb), ALU.mult), waits=(tr2, tsa), sw=True)
        QP["sc"][pi] = tb
        vprev = QP["v"].get(1 - pi)
        bbW = bbt[:, 0:W]
        op("vector", lambda E_: E_.tensor_scalar(m1[:, 0:W], bbW, 0.5, None, ALU.is_gt),
           waits=(tb, vprev), sw=True)
        op("vector", lambda E_: E_.tensor_scalar(m2[:, 0:W], bbW, 1.5, None, ALU.is_gt), sw=False)
        op("vector", lambda E_: E_.tensor_scalar(m4[:, 0:W], bbW, 2.5, None, ALU.is_gt), sw=False)
        op("vector", lambda E_: E_.tensor_scalar(m5[:, 0:W], bbW, 3.5, None, ALU.is_gt), sw=False)
        op("vector", lambda E_: E_.tensor_tensor(m3[:, 0:W], m1[:, 0:W], m2[:, 0:W], ALU.add), sw=False)
        op("vector", lambda E_: E_.tensor_tensor(m6[:, 0:W], m4[:, 0:W], m5[:, 0:W], ALU.add), sw=False)
        op("vector", lambda E_: E_.tensor_scalar(m1[:, 0:W], bbW, 5.0, None, ALU.is_gt), sw=False)
        op("vector", lambda E_: E_.tensor_scalar(m2[:, 0:W], bbW, 7.0, None, ALU.is_gt), sw=False)
        op("vector", lambda E_: E_.tensor_scalar(m4[:, 0:W], bbW, 10.0, None, ALU.is_gt), sw=False)
        op("vector", lambda E_: E_.tensor_tensor(m5[:, 0:W], m1[:, 0:W], m2[:, 0:W], ALU.add), sw=False)
        op("vector", lambda E_: E_.tensor_tensor(m2[:, 0:W], m3[:, 0:W], m6[:, 0:W], ALU.add), sw=False)
        op("vector", lambda E_: E_.scalar_tensor_tensor(
            m1[:, 0:W], m5[:, 0:W], 2.0, m2[:, 0:W], ALU.mult, ALU.add), sw=True)
        tq2 = op("vector", lambda E_: E_.scalar_tensor_tensor(
            m3[:, 0:W], m4[:, 0:W], 4.0, m1[:, 0:W], ALU.mult, ALU.add), sw=False)
        tsgm = op("vector", lambda E_: E_.tensor_tensor(
            m4[:, 0:W], m3[:, 0:W], sg[:, 0:W], ALU.mult), waits=(tq2, tsg), sw=False)
        QP["sg"][pi] = tsgm
        tq = op("vector", lambda E_: E_.tensor_tensor(
            qout.rearrange("p (n b) -> p n b", b=16),
            m4[:, 0:W].rearrange("p (n b) -> p n b", b=16),
            bcast16(s5[:, 0:nb], nb), ALU.mult), waits=(tsgm,), sw=True)
        QP["g"][pi] = tq
        QP["v"][pi] = tq
        st["q_prev"] = tq
        return tq, tsg

    tid = dma("sync", identb[:], identb_d[:])
    tidf = dma("sync", identf[:], identf_d[:])
    tmA = dma("sync", cmA[:], cmA_d[:])
    tmB = dma("sync", cmB[:], cmB_d[:])
    teps = dma("sync", epsc[:], epsc_d[:])
    tscc = dma("sync", scc[:], scc_d[:])
    tnb = dma("sync", nbias[:], nbias_d[:])
    tc = dma("sync", xs0[0:B, 0:E], c_in[:, :])

    tcs = op("scalar", lambda E_: E_.activation(xs1[0:B, 0:E], xs0[0:B, 0:E],
                                                AF.Silu), waits=(tc,))
    csT = xhb
    tr_last = None
    for c_ in range(NE):
        tr_last = op("tensor", lambda E_, c_=c_: E_.transpose(
            psO[:, c_ * B:(c_ + 1) * B], xs1[0:B, c_ * 128:(c_ + 1) * 128],
            identf[0:B, 0:B]), waits=(tcs, tidf))
    tev_cs = op("vector", lambda E_: E_.tensor_copy(
        csT[:, 0:NE * B], psO[:, 0:NE * B]), waits=(tr_last,))

    wmst = (wst0, wst1, wtoA, wtoB)
    wmbf = (m1, m2, m5, m6)
    NT6 = E6 // 512
    bufdep = {}
    castdep = {}
    mm_last = None
    tmod_done = None
    for nt in range(NT6):
        mm_slot = nt % 2
        evac_prev = st.get("modev%d" % mm_slot)
        for e in range(NE):
            bi = (nt * NE + e) % 4
            tld = dma("gpsimd" if bi % 2 == 0 else "sync", wmst[bi][:, 0:512],
                      wmod_in[e * 128:(e + 1) * 128, nt * 512:(nt + 1) * 512],
                      waits=(bufdep.get(bi),))
            tcast = op("vector", lambda E_, bi=bi: E_.tensor_copy(
                wmbf[bi][:, 0:512], wmst[bi][:, 0:512]), waits=(tld,))
            castdep[bi] = tcast
            mm_last = op("tensor", lambda E_, e=e, bi=bi, ms=mm_slot: E_.matmul(
                psO[0:B, ms * 512:(ms + 1) * 512], csT[:, e * B:(e + 1) * B],
                wmbf[bi][:, 0:512], start=(e == 0), stop=(e == NE - 1)),
                waits=(tcast, tev_cs, evac_prev))
            bufdep[bi] = mm_last
        lo = nt * 512
        need_clamp = (2 * E <= lo < 3 * E) or (5 * E <= lo < 6 * E)
        tev2 = op("vector", lambda E_, ms=mm_slot: E_.tensor_copy(
            ot0[0:B, 0:512], psO[0:B, ms * 512:(ms + 1) * 512]), waits=(mm_last,))
        if need_clamp:
            op("vector", lambda E_: E_.tensor_scalar(
                ot1[0:B, 0:512], ot0[0:B, 0:512], 0.25, None, ALU.max))
            op("vector", lambda E_: E_.tensor_scalar(
                oa[0:B, 0:512], ot0[0:B, 0:512], -0.25, None, ALU.min))
            op("vector", lambda E_: E_.tensor_scalar(
                xsq[0:B, 0:512], ot0[0:B, 0:512], 0.0, None, ALU.is_ge))
            op("vector", lambda E_: E_.tensor_tensor(
                xs1[0:B, 0:512], ot1[0:B, 0:512], oa[0:B, 0:512], ALU.subtract))
            op("vector", lambda E_: E_.tensor_tensor(
                ot1[0:B, 0:512], xs1[0:B, 0:512], xsq[0:B, 0:512], ALU.mult))
            tev2 = op("vector", lambda E_: E_.tensor_tensor(
                ot0[0:B, 0:512], oa[0:B, 0:512], ot1[0:B, 0:512], ALU.add))
        st["modev%d" % mm_slot] = tev2
        tmod_done = dma("sync", modd[:, nt * 512:(nt + 1) * 512],
                        ot0[0:B, 0:512], waits=(tev2,))
    st["p0cast"] = castdep.get(2)
    st["p0castB"] = castdep.get(3)

    half = (NE + 1) // 2
    wqstg, wkstg = qT, kT
    tstg = None
    for ep in range(NE):
        tstg = dma("gpsimd", wqstg[:, ep * E:(ep + 1) * E],
                   wq_in[ep * 128:(ep + 1) * 128, :])
        tstg = dma("gpsimd", wkstg[:, ep * E:(ep + 1) * E],
                   wk_in[ep * 128:(ep + 1) * 128, :])
    tstg_s = None
    mev = {}
    tM_done = None
    tM_mm = None
    for ec in range(NE):
        for dh in range(NHF):
            slot = ((ec * NHF + dh) % 4) * 512
            mm = None
            for ep in range(NE):
                mm = op("tensor", lambda E_, ec=ec, dh=dh, ep=ep, slot=slot: E_.matmul(
                    psS[:, slot:slot + 512],
                    wqstg[:, ep * E + ec * 128: ep * E + ec * 128 + 128],
                    wkstg[:, ep * E + dh * 512: ep * E + dh * 512 + 512],
                    start=(ep == 0), stop=(ep == NE - 1)),
                    waits=(tstg, tstg_s, mev.get(slot)))
            tM_mm = mm
            tM_done = op("scalar", lambda E_, ec=ec, dh=dh, slot=slot: E_.activation(
                Msb[:, ec * E + dh * 512: ec * E + dh * 512 + 512],
                psS[:, slot:slot + 512], AF.Identity), waits=(mm,), sw=False)
            mev[slot] = tM_done

    bufdep2 = {0: mm_last, 1: mm_last}
    prev_tr = (tid,)
    twv_done = None
    qkv_tr_prev = mm_last
    for n in range(NE):
        bi = n % 2
        stg = wst0 if bi == 0 else wst1
        tca = []
        tprev = None
        for pc in range(E // 512):
            cbf = m3 if pc % 2 == 0 else m4
            tldp = dma("gpsimd", stg[:, 0:512],
                       wv_in[n * 128:(n + 1) * 128, pc * 512:(pc + 1) * 512],
                       waits=(bufdep2.get(bi), tprev, mm_last))
            tcp = op("vector", lambda E_, cbf=cbf, stg=stg: E_.tensor_copy(
                cbf[:, 0:512], stg[:, 0:512]),
                waits=(tldp, mm_last, qkv_tr_prev))
            tprev = tcp
            tca.append(tcp)
        bufdep2[bi] = tca[-1]
        trk = []
        for e in range(NE):
            pc = (e * 128) // 512
            cbf = m3 if pc % 2 == 0 else m4
            ecol = (e * 128) % 512
            bank = psT0 if e < half else psT1
            slot = e % half
            tt = op("tensor", lambda E_, bank=bank, slot=slot, cbf=cbf, ecol=ecol: E_.transpose(
                bank[:, slot * 128:(slot + 1) * 128],
                cbf[:, ecol:ecol + 128], identb[:]),
                waits=(tca[pc], tid, tM_mm) + prev_tr)
            trk.append(tt)
        qkv_tr_prev = trk[-1]

        def _evv(E_, n=n, banki=0):
            bank = psT0 if banki == 0 else psT1
            e0 = banki * half
            cnt = half if banki == 0 else NE - half
            base = wvT[:]
            dst = bass.AP(base.tensor, base.offset + e0 * E + n * 128,
                          [list(base.ap[0]), [E, cnt], [1, 128]])
            src = bank[:, 0:cnt * 128].rearrange("p (c w) -> p c w", w=128)
            return E_.tensor_copy(dst, src)
        ev0 = op("vector", lambda E_, n=n: _evv(E_, n, 0), waits=(trk[half - 1],))
        ev1 = op("vector", lambda E_, n=n: _evv(E_, n, 1), waits=(trk[-1],))
        twv_done = ev1
        prev_tr = (ev0, ev1)
    st["psT_ev"] = prev_tr[-1]
    st["wqo_tr"] = (twv_done, qkv_tr_prev)

    wchunks = []
    for w_d, wT_d, R, C in ((wg_in, wgT_d, I, E), (wu_in, wuT_d, I, E)):
        for r in range(R // 128):
            for cc in range(C // 512):
                wchunks.append((w_d, wT_d, r, cc * 512, 512, R))
    c_list = []
    c0 = 0
    while c0 < I:
        w_ = min(512, I - c0)
        c_list.append((c0, w_))
        c0 += w_
    for r in range(E // 128):
        for (c0, w_) in c_list:
            wchunks.append((wd_in, wdT_d, r, c0, w_, E))

    wqs = dict(free0=(mm_last, bufdep2.get(0)), free1=(mm_last, bufdep2.get(1)),
               stfree={}, bufi=0)

    def emit_wchunk(idx):
        if idx >= len(wchunks):
            return
        w_d, wT_d, r, c0, W, R = wchunks[idx]
        bi = wqs["bufi"]
        wqs["bufi"] ^= 1
        stg = wst0 if bi == 0 else wst1
        fr = wqs["free%d" % bi]
        tld = dma("gpsimd" if bi == 0 else "sync",
                  stg[:, 0:W], w_d[r * 128:(r + 1) * 128, c0:c0 + W],
                  waits=fr)
        tq, tsg = quant(stg[:, 0:W], wqo[:, 0:W], W, tld,
                        tax=st.get("wqo_tr", ()))
        wqs["free%d" % bi] = (tq, tsg)
        ntile = W // 128
        trk = []
        for e in range(ntile):
            bank = psT0 if e < 4 else psT1
            slot = e % 4
            tt = op("tensor", lambda E_, bank=bank, slot=slot, e=e: E_.transpose(
                bank[:, slot * 128:(slot + 1) * 128],
                wqo[:, e * 128:(e + 1) * 128], identb[:]),
                waits=(tq, st.get("psT_ev")))
            trk.append(tt)
        st["wqo_tr"] = (trk[-1],)
        wto = wto0
        stfree = wqs["stfree"].get(0)
        n0 = min(4, ntile)
        evs = [op("scalar", lambda E_, n0=n0, wto=wto: E_.activation(
            wto[:, 0:n0 * 128], psT0[:, 0:n0 * 128], AF.Identity),
            waits=(trk[n0 - 1], stfree), sw=False)]
        if ntile > 4:
            evs.append(op("scalar", lambda E_, ntile=ntile, wto=wto: E_.activation(
                wto[:, 4 * 128:ntile * 128], psT1[:, 0:(ntile - 4) * 128],
                AF.Identity), waits=(trk[-1], stfree), sw=False))
        st["psT_ev"] = evs[-1]
        dst = bass.AP(wT_d[:].tensor, c0 * R + r * 128,
                      [[R, 128], [128 * R, ntile], [1, 128]])
        tst = dma("gpsimd", dst,
                  wto[:, 0:ntile * 128].rearrange("p (c w) -> p c w", w=128),
                  waits=tuple(evs))
        wqs["stfree"][0] = tst

    WCH = [0]

    def next_wchunks(k):
        if not interleave_w:
            return
        for _ in range(k):
            if WCH[0] < len(wchunks):
                emit_wchunk(WCH[0])
                WCH[0] += 1

    ast = dict(x={}, xr={}, omst={}, tom=None, tga=None, sc_evac=None,
               texp=None, tmx=None, o_evac=None, ptr_last=None, t3=None,
               t5=None, xhb_tr=None, pv_last=None, trl=None,
               qslot={}, vslot={},
               qe={"q": None, "k": None, "v": None}, ptev={},
               ptT_mm={0: st["p0castB"], 1: st["p0castB"]},
               trlp={}, trmp={}, cap=None)
    xm_last = None
    for s in range(B):
        t_car = dma("sync", oa[0:1, 0:E], modd[s:s + 1, E:2 * E],
                    waits=(tmod_done, ast["tga"]))
        t_sar = dma("sync", orw[0:1, 0:E], modd[s:s + 1, 0:E],
                    waits=(tmod_done, ast["tom"]))
        t_gab = dma("sync", gab[:],
                    bass.AP(modd[:].tensor, s * E6 + 2 * E, [[0, 128], [1, E]]),
                    waits=(tmod_done, ast["tga"]))
        trc = trs = None
        for c_ in range(NE):
            trc = op("tensor", lambda E_, c_=c_: E_.transpose(
                psO[:, c_:c_ + 1], oa[0:1, c_ * 128:(c_ + 1) * 128],
                identf[0:1, 0:1]), waits=(t_car, tidf, ast["o_evac"]))
            trs = op("tensor", lambda E_, c_=c_: E_.transpose(
                psO[:, 512 + c_:512 + c_ + 1], orw[0:1, c_ * 128:(c_ + 1) * 128],
                identf[0:1, 0:1]), waits=(t_sar,))
        t_cap = op("scalar", lambda E_: E_.activation(
            caP[:, 0:NE], psO[:, 0:NE], AF.Identity, bias=1.0), waits=(trc,))
        t_sap = op("scalar", lambda E_: E_.activation(
            saP[:, 0:NE], psO[:, 512:512 + NE], AF.Identity), waits=(trs,))
        ast["o_evac"] = t_sap
        ast["cap"] = (t_cap, t_sap)

        xst = (xs0, xs1, xr0, xr1)
        for g in range(NSG):
            h_last = None
            for j in range(4):
                ti = g * 4 + j
                bi = ti % 4
                t_ld = dma("sync", xst[bi][:], x_in[s, ti * 128:(ti + 1) * 128, :],
                           waits=(ast["x"].get(bi), tmod_done,
                                  ast["xr"].get(bi - 2) if bi >= 2 else None,
                                  tcs if s == 0 and g == 0 else None))
                op("vector", lambda E_, bi=bi: E_.tensor_reduce(
                    stat[:, 0:1], xst[bi][:], AX.X, ALU.add),
                    waits=(t_ld, ast["t5"]))
                t2 = op("scalar", lambda E_, bi=bi: E_.activation(
                    xsq[:], xst[bi][:], AF.Square, accum_out=stat[:, 1:2]),
                    waits=(t_ld, ast["t3"], tmod_done, ast["tom"]), sw=False)
                ast["t3"] = t2
                op("vector", lambda E_: E_.tensor_scalar(
                    stat[:, 2:3], stat[:, 0:1], 1.0 / E, None, ALU.mult))
                op("vector", lambda E_: E_.tensor_scalar(
                    stat[:, 3:4], stat[:, 1:2], 1.0 / E, None, ALU.mult),
                    waits=(t2,))
                op("vector", lambda E_: E_.tensor_tensor(
                    stat[:, 4:5], stat[:, 2:3], stat[:, 2:3], ALU.mult))
                t4 = op("vector", lambda E_: E_.tensor_tensor(
                    stat[:, 5:6], stat[:, 3:4], stat[:, 4:5], ALU.subtract))
                t5 = op("scalar", lambda E_: E_.activation(
                    stat[:, 6:7], stat[:, 5:6], AF.Sqrt, bias=epsc[:, 0:1]),
                    waits=(t4, teps))
                ast["t5"] = t5
                t6 = op("vector", lambda E_: E_.reciprocal(
                    stat[:, 7:8], stat[:, 6:7]), waits=(t5,))
                t7 = op("vector", lambda E_: E_.scalar_tensor_tensor(
                    stat[:, 8:9], stat[:, 2:3], -1.0, stat[:, 7:8],
                    ALU.mult, ALU.mult))
                t8 = op("scalar", lambda E_, bi=bi: E_.activation(
                    xhb[:], xst[bi][:], AF.Identity, bias=stat[:, 8:9],
                    scale=stat[:, 7:8]),
                    waits=(t7, tmod_done, ast["xhb_tr"],
                           tev_cs if s == 0 and g == 0 else None), sw=False)
                ast["x"][bi] = t8
                trk = []
                for e in range(NE):
                    bank = psT0 if e < half else psT1
                    slot = e % half
                    tt = op("tensor", lambda E_, bank=bank, slot=slot, e=e: E_.transpose(
                        bank[:, slot * 128:(slot + 1) * 128],
                        xhb[:, e * 128:(e + 1) * 128], identb[:]),
                        waits=(t8, tid, st.get("psT_ev")))
                    trk.append(tt)
                ast["xhb_tr"] = trk[-1]
                for e in range(NE):
                    bank = psT0 if e < half else psT1
                    slot = e % half
                    h_last = op("scalar", lambda E_, bank=bank, slot=slot, e=e, j=j, g=g: E_.activation(
                        kT[:, e * S + g * 512 + j * 128: e * S + g * 512 + (j + 1) * 128],
                        bank[:, slot * 128:(slot + 1) * 128],
                        AF.Identity, bias=saP[:, e:e + 1], scale=caP[:, e:e + 1]),
                        waits=(trk[e], ast["texp"], tM_mm) + ast["cap"],
                        sw=(e == 0 and j == 0))
                st["psT_ev"] = h_last
            for n in range(NE):
                base = (n % 2) * 512
                mm = None
                for e in range(NE):
                    mm = op("tensor", lambda E_, n=n, e=e, base=base, g=g: E_.matmul(
                        psS[:, base:base + 512],
                        Msb[:, e * E + n * 128: e * E + n * 128 + 128],
                        kT[:, e * S + g * 512: e * S + (g + 1) * 512],
                        start=(e == 0), stop=(e == NE - 1)),
                        waits=(h_last, tM_done, ast["qslot"].get(base)))
                ast["qe"]["q"] = op(
                    "scalar", lambda E_, n=n, g=g, base=base: E_.activation(
                        qT[:, n * S + g * 512: n * S + (g + 1) * 512],
                        psS[:, base:base + 512], AF.Identity),
                    waits=(mm, ast["pv_last"], tM_mm), sw=False)
                ast["qslot"][base] = ast["qe"]["q"]
            for j in range(4):
                ti = g * 4 + j
                for hf in range(NHF):
                    mm = None
                    for e in range(NE):
                        mm = op("tensor", lambda E_, j=j, hf=hf, e=e, g=g: E_.matmul(
                            psS[:, 1024 + hf * 512: 1024 + (hf + 1) * 512],
                            kT[:, e * S + g * 512 + j * 128: e * S + g * 512 + (j + 1) * 128],
                            wvT[:, e * E + hf * 512: e * E + (hf + 1) * 512],
                            start=(e == 0), stop=(e == NE - 1)),
                            waits=(h_last, twv_done, ast["vslot"].get(hf)))
                    ast["qe"]["v"] = op(
                        "scalar", lambda E_, ti=ti, hf=hf: E_.activation(
                            vsb[:, ti * E + hf * 512: ti * E + (hf + 1) * 512],
                            psS[:, 1024 + hf * 512: 1024 + (hf + 1) * 512],
                            AF.Identity),
                        waits=(mm, ast["pv_last"]), sw=False)
                    ast["vslot"][hf] = ast["qe"]["v"]
            next_wchunks(1)
        ast["qe"]["k"] = h_last

        ssbs = (ssb, ssb)

        def a_stage1(i):
            W = (i + 1) * 128
            NK4 = (W + 511) // 512
            dk4 = (i * 128) // 512
            doff = (i * 128) % 512
            sc_mm = None
            for e in range(NE):
                for k4 in range(NK4):
                    w_ = min(512, W - k4 * 512)
                    sc_mm = op("tensor", lambda E_, e=e, k4=k4, w_=w_, i=i: E_.matmul(
                        psS[:, k4 * 512:k4 * 512 + w_],
                        qT[:, e * S + i * 128:e * S + (i + 1) * 128],
                        kT[:, e * S + k4 * 512:e * S + k4 * 512 + w_],
                        start=(e == 0),
                        stop=(e == NE - 1 and k4 != dk4)),
                        waits=(ast["qe"]["q"], ast["qe"]["k"], ast["sc_evac"]))
            sc_mm = op("tensor", lambda E_, dk4=dk4, doff=doff: E_.matmul(
                psS[:, dk4 * 512 + doff: dk4 * 512 + doff + 128],
                cmA[:], cmB[:], start=False, stop=True),
                waits=(tmA, tmB))
            ev = None
            for k4 in range(NK4):
                w_ = min(512, W - k4 * 512)
                ev = op("scalar", lambda E_, k4=k4, w_=w_, i=i: E_.activation(
                    ssbs[i % 2][:, k4 * 512:k4 * 512 + w_],
                    psS[:, k4 * 512:k4 * 512 + w_],
                    AF.Identity, scale=scc[:, 0:1]),
                    waits=(sc_mm, tscc, st["p0cast"]), sw=False)
            ast["sc_evac"] = ev

        def a_stage2_front(i):
            pi2 = i % 2
            texp = op("scalar", lambda E_, W=(i + 1) * 128, i=i, pi2=pi2: E_.activation(
                psb[:, 0:W], ssbs[i % 2][:, 0:W], AF.Exp, bias=nbias[:, 0:1],
                accum_out=mx1[:, pi2 * 2:pi2 * 2 + 1]),
                waits=(tnb, ast["ptr_last"], ast["trlp"].get(pi2)), sw=False)
            ast["texp"] = texp
            trl = op("vector", lambda E_, pi2=pi2: E_.reciprocal(
                mx1[:, pi2 * 2 + 1:pi2 * 2 + 2], mx1[:, pi2 * 2:pi2 * 2 + 1]),
                waits=(texp, ast["trmp"].get(pi2)))
            ast["trl"] = trl
            ast["trlp"][pi2] = trl

        def a_stage2(i, s):
            W = (i + 1) * 128
            bi = i % 2
            xrb = xr0 if bi == 0 else xr1
            t_ldr = dma("sync", xrb[:], x_in[s, i * 128:(i + 1) * 128, :],
                        waits=(ast["xr"].get(bi), ast["x"].get(bi + 2),
                               t_car, t_sar, t_cap if bi == 0 else t_sap))
            texp = ast["texp"]
            trl = ast["trl"]
            pv_mm = None
            NKT = i + 1
            for g4 in range(0, NKT, 4):
                cnt = min(4, NKT - g4)
                pgrp = (g4 // 4) % 2
                bank = psT0 if pgrp == 0 else psT1
                trk = []
                for kk in range(cnt):
                    kt = g4 + kk
                    tt = op("tensor", lambda E_, bank=bank, kk=kk, kt=kt: E_.transpose(
                        bank[:, kk * 128:(kk + 1) * 128],
                        psb[:, kt * 128:(kt + 1) * 128], identb[:]),
                        waits=(texp, st.get("psT_ev"), ast["ptev"].get(pgrp)))
                    trk.append(tt)
                ast["ptr_last"] = trk[-1]
                tevp = op("scalar", lambda E_, bank=bank, cnt=cnt, pgrp=pgrp: E_.activation(
                    ptT[:, pgrp * 512: pgrp * 512 + cnt * 128], bank[:, 0:cnt * 128],
                    AF.Identity), waits=(trk[-1], ast["ptT_mm"].get(pgrp)), sw=False)
                st["psT_ev"] = tevp
                ast["ptev"][pgrp] = tevp
                for kk in range(cnt):
                    kt = g4 + kk
                    for hf in range(NHF):
                        pv_mm = op("tensor", lambda E_, kt=kt, kk=kk, hf=hf, pgrp=pgrp, i=i: E_.matmul(
                            psO[:, hf * 512:(hf + 1) * 512],
                            ptT[:, pgrp * 512 + kk * 128: pgrp * 512 + (kk + 1) * 128],
                            vsb[:, kt * E + hf * 512: kt * E + (hf + 1) * 512],
                            start=(kt == 0), stop=(kt == i)),
                            waits=(tevp, ast["qe"]["v"], ast["o_evac"]))
                ast["ptT_mm"][pgrp] = pv_mm
            ast["pv_last"] = pv_mm
            oev = None
            for hf in range(NHF):
                oev = op("scalar", lambda E_, hf=hf: E_.activation(
                    oa[:, hf * 512:(hf + 1) * 512], psO[:, hf * 512:(hf + 1) * 512],
                    AF.Identity), waits=(pv_mm, ast["tga"]), sw=False)
            ast["o_evac"] = oev
            tga = op("gpsimd", lambda E_: E_.tensor_tensor(
                xsq[:], oa[:], gab[:], ALU.mult), waits=(oev, t_gab))
            ast["tga"] = tga
            pi2 = i % 2
            _m = mx1[:]
            rsc = bass.AP(_m.tensor, _m.offset + pi2 * 2 + 1,
                          [list(_m.ap[0]), [0, E]])
            trm = op("gpsimd", lambda E_, rsc=rsc: E_.tensor_tensor(
                orw[:], xsq[:], rsc, ALU.mult), waits=(trl,))
            ast["trmp"][pi2] = trm
            obuf = ot0 if bi == 0 else ot1
            tom = op("gpsimd", lambda E_, obuf=obuf, xrb=xrb: E_.tensor_tensor(
                obuf[:], orw[:], xrb[:], ALU.add),
                waits=(trm, t_ldr, ast["omst"].get(bi)))
            ast["tom"] = tom
            ast["xr"][bi] = tom
            xm = dma("sync", xmid_d[s, i * 128:(i + 1) * 128, :], obuf[:],
                     waits=(tom,))
            ast["omst"][bi] = xm
            next_wchunks(2)
            return xm

        a_stage1(0)
        for i in range(NS):
            a_stage2_front(i)
            if i + 1 < NS:
                a_stage1(i + 1)
            xm_last = a_stage2(i, s)

    while WCH[0] < len(wchunks):
        emit_wchunk(WCH[0])
        WCH[0] += 1

    barrier = tuple(x for x in (xm_last, wqs["stfree"].get(0),
                                wqs["stfree"].get(1)) if x is not None)
    t_wB = None
    for e in range(NE):
        t_wB = dma("gpsimd", wgT[:, e * I:(e + 1) * I],
                   wgT_d[e * 128:(e + 1) * 128, :], waits=barrier)
    for e in range(NE):
        t_wB = dma("gpsimd", wuT[:, e * I:(e + 1) * I],
                   wuT_d[e * 128:(e + 1) * 128, :], waits=barrier)
    for ic in range(NI):
        t_wB = dma("gpsimd", wdT[:, ic * E:(ic + 1) * E],
                   wdT_d[ic * 128:(ic + 1) * 128, :], waits=barrier)

    bst = dict(x={}, ost={}, tout=None, tgb=barrier[0], t3=None, t5=None,
               h2q_tr=(), hq_tr=(), gate_mm=None, mmd=None, thd=None,
               tsg=None, t9=None, sl={0: None, 1: None})
    itiles = c_list

    def b_back(s, i, bi, xb_, mmd, t_gbb):
        tgb = op("vector", lambda E_: E_.tensor_tensor(
            oa[:], psO[:, 0:E], gab[:], ALU.mult),
            waits=(mmd, t_gbb, bst["tout"]), sw=False)
        bst["tgb"] = tgb
        obuf = ot0 if bi == 0 else ot1
        tout = op("gpsimd", lambda E_, obuf=obuf, xb_=xb_: E_.tensor_tensor(
            obuf[:], oa[:], xb_[:], ALU.add),
            waits=(tgb, bst["ost"].get(bi)))
        bst["tout"] = tout
        bst["x"][bi] = tout
        tst = dma("gpsimd", out_d[s, i * 128:(i + 1) * 128, :], obuf[:],
                  waits=(tout,))
        bst["ost"][bi] = tst

    pend = None
    for s in range(B):
        if pend is not None:
            b_back(*pend)
            pend = None
        t_cbr = dma("sync", h2f[:],
                    bass.AP(modd[:].tensor, s * E6 + 4 * E, [[0, 128], [1, E]]),
                    waits=barrier + (bst["t9"], bst["tsg"]))
        t_cb1 = op("vector", lambda E_: E_.tensor_scalar(
            cbb[:], h2f[:], 1.0, None, ALU.add), waits=(t_cbr, bst["t9"]))
        t_sbb = dma("sync", sbb[:],
                    bass.AP(modd[:].tensor, s * E6 + 3 * E, [[0, 128], [1, E]]),
                    waits=barrier + (bst["t9"],))
        t_gbb = dma("sync", gab[:],
                    bass.AP(modd[:].tensor, s * E6 + 5 * E, [[0, 128], [1, E]]),
                    waits=barrier + (ast["tga"], bst["tgb"]))
        for i in range(NS):
            bi = i % 2
            xb_ = xs0 if bi == 0 else xs1
            t_ld = dma("sync", xb_[:], xmid_d[s, i * 128:(i + 1) * 128, :],
                       waits=(bst["x"].get(bi), xm_last))
            op("vector", lambda E_, xb_=xb_: E_.tensor_reduce(
                stat[:, 0:1], xb_[:], AX.X, ALU.add), waits=(t_ld, bst["t5"]))
            t2 = op("scalar", lambda E_, xb_=xb_: E_.activation(
                xsq[:], xb_[:], AF.Square, accum_out=stat[:, 1:2]),
                waits=(t_ld, bst["t3"], ast["tga"], bst["t9"]), sw=False)
            bst["t3"] = t2
            op("vector", lambda E_: E_.tensor_scalar(
                stat[:, 2:3], stat[:, 0:1], 1.0 / E, None, ALU.mult))
            op("vector", lambda E_: E_.tensor_scalar(
                stat[:, 3:4], stat[:, 1:2], 1.0 / E, None, ALU.mult),
                waits=(t2,))
            op("vector", lambda E_: E_.tensor_tensor(
                stat[:, 4:5], stat[:, 2:3], stat[:, 2:3], ALU.mult))
            t4 = op("vector", lambda E_: E_.tensor_tensor(
                stat[:, 5:6], stat[:, 3:4], stat[:, 4:5], ALU.subtract))
            t5 = op("scalar", lambda E_: E_.activation(
                stat[:, 6:7], stat[:, 5:6], AF.Sqrt, bias=epsc[:, 0:1]),
                waits=(t4, teps))
            bst["t5"] = t5
            t6 = op("vector", lambda E_: E_.reciprocal(stat[:, 7:8], stat[:, 6:7]),
                    waits=(t5,))
            t7 = op("vector", lambda E_: E_.scalar_tensor_tensor(
                stat[:, 8:9], stat[:, 2:3], -1.0, stat[:, 7:8], ALU.mult, ALU.mult))
            t8 = op("scalar", lambda E_, xb_=xb_: E_.activation(
                h2f[:], xb_[:], AF.Identity, bias=stat[:, 8:9], scale=stat[:, 7:8]),
                waits=(t7, t_cb1, bst["tsg"]), sw=False)
            t9a = op("gpsimd", lambda E_: E_.tensor_tensor(
                xsq[:], h2f[:], cbb[:], ALU.mult), waits=(t8, t_cb1))
            t9 = op("gpsimd", lambda E_: E_.tensor_tensor(
                h2f[:], xsq[:], sbb[:], ALU.add), waits=(t_sbb,))
            bst["t9"] = t9
            tqs = None
            for hc in range(E // 512):
                tqs, tsgh2 = quant(h2f[:, hc * 512:(hc + 1) * 512],
                                   h2q[:, hc * 512:(hc + 1) * 512], 512, t9,
                                   tax=bst["h2q_tr"])
                bst["tsg"] = tsgh2
            tq = tqs
            trk = []
            for e in range(NE):
                bank = psT0 if e < half else psT1
                slot = e % half
                tt = op("tensor", lambda E_, bank=bank, slot=slot, e=e: E_.transpose(
                    bank[:, slot * 128:(slot + 1) * 128],
                    h2q[:, e * 128:(e + 1) * 128], identb[:]),
                    waits=(tq, st.get("psT_ev"), tid))
                trk.append(tt)
            bst["h2q_tr"] = (trk[-1],)
            tev0 = op("scalar", lambda E_: E_.activation(
                h2qT[:, 0:half * 128], psT0[:, 0:half * 128], AF.Identity),
                waits=(trk[half - 1], bst["gate_mm"]), sw=False)
            tev1 = op("scalar", lambda E_: E_.activation(
                h2qT[:, half * 128:NE * 128], psT1[:, 0:(NE - half) * 128],
                AF.Identity), waits=(trk[-1], bst["gate_mm"]), sw=False)
            st["psT_ev"] = tev1
            th2qT = tev1
            sltoks = []
            for it, (c0, w_) in enumerate(itiles):
                sl = it % 2
                gslot = 0 if sl == 0 else 1024
                uslot = 512 if sl == 0 else 1536
                mmg = None
                for e in range(NE):
                    mmg = op("tensor", lambda E_, e=e, c0=c0, w_=w_, gslot=gslot: E_.matmul(
                        psS[:, gslot:gslot + w_], h2qT[:, e * 128:(e + 1) * 128],
                        wgT[:, e * I + c0:e * I + c0 + w_],
                        start=(e == 0), stop=(e == NE - 1)),
                        waits=(th2qT, t_wB, bst["sl"][sl]))
                for e in range(NE):
                    mmg = op("tensor", lambda E_, e=e, c0=c0, w_=w_, uslot=uslot: E_.matmul(
                        psS[:, uslot:uslot + w_], h2qT[:, e * 128:(e + 1) * 128],
                        wuT[:, e * I + c0:e * I + c0 + w_],
                        start=(e == 0), stop=(e == NE - 1)))
                bst["gate_mm"] = mmg
                tsl = op("scalar", lambda E_, gslot=gslot, w_=w_: E_.activation(
                    sgf[:, 0:w_], psS[:, gslot:gslot + w_], AF.Silu),
                    waits=(mmg, bst["thd"]), sw=False)
                thd = op("vector", lambda E_, uslot=uslot, w_=w_: E_.tensor_tensor(
                    hidf[:, 0:w_], sgf[:, 0:w_], psS[:, uslot:uslot + w_], ALU.mult),
                    waits=(tsl, bst["sl"][sl], bst["tsg"]), sw=False)
                bst["thd"] = thd
                hq_dst = bass.AP(hqsb[:].tensor, hqsb[:].offset + c0,
                                 [list(hqsb[:].ap[0]), [1, w_]])
                tqh, tsg_h = quant(hidf[:, 0:w_], hq_dst, w_, thd,
                                   tax=bst["hq_tr"])
                bst["sl"][sl] = tqh
                bst["tsg"] = tsg_h
                sltoks.append(tqh)
            trk2 = None
            for icb in range(0, NI, 4):
                cnt = min(4, NI - icb)
                bank = psT0 if (icb // 4) % 2 == 0 else psT1
                grp = []
                for kk in range(cnt):
                    trk2 = op("tensor", lambda E_, bank=bank, kk=kk, icb=icb: E_.transpose(
                        bank[:, kk * 128:(kk + 1) * 128],
                        hqsb[:, (icb + kk) * 128:(icb + kk + 1) * 128], identb[:]),
                        waits=(sltoks[icb // 4], st.get("psT_ev")))
                    grp.append(trk2)
                tevh = op("scalar", lambda E_, bank=bank, cnt=cnt, icb=icb: E_.activation(
                    hqT[:, icb * 128:(icb + cnt) * 128], bank[:, 0:cnt * 128],
                    AF.Identity), waits=(grp[-1], bst["mmd"]), sw=False)
                st["psT_ev"] = tevh
            bst["hq_tr"] = (trk2,)
            thqT = st["psT_ev"]
            if pend is not None:
                b_back(*pend)
            mmd = None
            for ic in range(NI):
                for hf in range(NHF):
                    mmd = op("tensor", lambda E_, ic=ic, hf=hf: E_.matmul(
                        psO[:, hf * 512:(hf + 1) * 512],
                        hqT[:, ic * 128:(ic + 1) * 128],
                        wdT[:, ic * E + hf * 512: ic * E + (hf + 1) * 512],
                        start=(ic == 0), stop=(ic == NI - 1)),
                        waits=(thqT, t_wB, bst["tgb"]))
            bst["mmd"] = mmd
            pend = (s, i, bi, xb_, mmd, t_gbb)
    b_back(*pend)

    final_ds = Tok("DS", P.chan["DS"][1], "sync", True)
    final_dg = Tok("DG", P.chan["DG"][1], "gpsimd", True)
    P.wait_only("sync", (final_ds,))
    P.wait_only("gpsimd", (final_dg,))

    with nc.Block() as block:
        @block.sync
        def _(sync):
            P.run(sync, "sync")

        @block.vector
        def _(vector):
            P.run(vector, "vector")

        @block.scalar
        def _(scalar):
            P.run(scalar, "scalar")

        @block.tensor
        def _(tensor):
            P.run(tensor, "tensor")

        @block.gpsimd
        def _(gpsimd):
            P.run(gpsimd, "gpsimd")

    es.close()
    return nc


_NC_CACHE = {}


def kernel(L0_attn_x, L0_attn_WQ, L0_attn_WK, L0_attn_WV,
           L0_ffn_w_gate, L0_ffn_w_up, L0_ffn_w_down, L0_c, L0_wmod):
    cfg = FULL_CFG
    n = cfg["NCORES"]
    bpc = L0_attn_x.shape[0] // n
    if "full" not in _NC_CACHE:
        _NC_CACHE["full"] = build(cfg)
    nc = _NC_CACHE["full"]
    x = np.ascontiguousarray(L0_attn_x)
    c = np.ascontiguousarray(L0_c)
    shared = {
        "wq_in": np.ascontiguousarray(L0_attn_WQ),
        "wk_in": np.ascontiguousarray(L0_attn_WK),
        "wv_in": np.ascontiguousarray(L0_attn_WV),
        "wg_in": np.ascontiguousarray(L0_ffn_w_gate),
        "wu_in": np.ascontiguousarray(L0_ffn_w_up),
        "wd_in": np.ascontiguousarray(L0_ffn_w_down),
        "wmod_in": np.ascontiguousarray(L0_wmod),
    }
    in_maps = []
    for ci in range(n):
        m = dict(shared)
        m["x_in"] = np.ascontiguousarray(x[ci * bpc:(ci + 1) * bpc])
        m["c_in"] = np.ascontiguousarray(c[ci * bpc:(ci + 1) * bpc])
        in_maps.append(m)
    res = run_bass_kernel_spmd(nc, in_maps, core_ids=list(range(n)))
    return np.concatenate([np.asarray(r["out"]) for r in res.results], axis=0)


# version 16
# speedup vs baseline: 1.0476x; 1.0022x over previous
import contextlib
import numpy as np
import ml_dtypes
import concourse.bass as bass
import concourse.mybir as mybir
from concourse.bass_utils import run_bass_kernel_spmd

F32 = mybir.dt.float32
BF16 = mybir.dt.bfloat16
AF = mybir.ActivationFunctionType
ALU = mybir.AluOpType
AX = mybir.AxisListType

ENGINES = ("sync", "vector", "scalar", "tensor", "gpsimd")

FULL_CFG = dict(B=4, S=2048, E=1024, I=2816, NCORES=8)


class Tok:
    __slots__ = ("ch", "val", "eng", "dma")

    def __init__(self, ch, val, eng, dma):
        self.ch, self.val, self.eng, self.dma = ch, val, eng, dma


class Prog:

    def __init__(self, nc):
        self.nc = nc
        self.streams = {e: [] for e in ENGINES}
        self.chan = {}
        self.water = {}

    def channel(self, name, sem):
        self.chan[name] = [sem, 0]

    CHAINED = ("vector", "scalar", "gpsimd")

    def emit(self, eng, fn, waits=(), inc=None, inc_k=1, sw=True):
        wl = []
        if sw and eng in self.CHAINED and inc is not None:
            cur = self.chan[inc][1]
            key = (eng, inc)
            if cur > 0 and self.water.get(key, -1) < cur:
                self.water[key] = cur
                wl.append((inc, cur))
        for t in waits:
            if t is None:
                continue
            if (not t.dma) and t.eng == eng:
                continue
            key = (eng, t.ch)
            if self.water.get(key, -1) >= t.val:
                continue
            self.water[key] = t.val
            wl.append((t.ch, t.val))
        tok = None
        if inc is not None:
            c = self.chan[inc]
            c[1] += inc_k
            tok = Tok(inc, c[1], eng, inc_k >= 16)
        self.streams[eng].append((wl, fn, inc, inc_k))
        return tok

    def wait_only(self, eng, waits):
        self.emit(eng, None, waits=waits)

    def run(self, engobj, eng):
        for wl, fn, inc, inc_k in self.streams[eng]:
            for ch, val in wl:
                engobj.wait_ge(self.chan[ch][0], val)
            if fn is None:
                continue
            ins = fn(engobj)
            if inc is not None:
                ins.then_inc(self.chan[inc][0], inc_k)


def bcast16(ap2d, nblk, blk=16):
    return bass.AP(ap2d.tensor, ap2d.offset, [list(ap2d.ap[0]), [1, nblk], [0, blk]])


def build(cfg, debug=False, interleave_w=True):
    B, S, E, I = cfg["B"], cfg["S"], cfg["E"], cfg["I"]
    NE, NS, NI = E // 128, S // 128, I // 128
    NSG = S // 512
    NHF = E // 512
    SCALE = float(E) ** -0.5
    E6 = 6 * E
    assert S % 512 == 0 and E % 512 == 0 and I % 128 == 0

    nc = bass.Bass("TRN2", target_bir_lowering=False, detect_race_conditions=False)

    x_in = nc.dram_tensor("x_in", [B, S, E], F32, kind="ExternalInput")
    wq_in = nc.dram_tensor("wq_in", [E, E], F32, kind="ExternalInput")
    wk_in = nc.dram_tensor("wk_in", [E, E], F32, kind="ExternalInput")
    wv_in = nc.dram_tensor("wv_in", [E, E], F32, kind="ExternalInput")
    wg_in = nc.dram_tensor("wg_in", [I, E], F32, kind="ExternalInput")
    wu_in = nc.dram_tensor("wu_in", [I, E], F32, kind="ExternalInput")
    wd_in = nc.dram_tensor("wd_in", [E, I], F32, kind="ExternalInput")
    c_in = nc.dram_tensor("c_in", [B, E], F32, kind="ExternalInput")
    wmod_in = nc.dram_tensor("wmod_in", [E, E6], F32, kind="ExternalInput")
    out_d = nc.dram_tensor("out", [B, S, E], F32, kind="ExternalOutput")

    _k = dict(kind="ExternalOutput") if debug else {}
    xmid_d = nc.dram_tensor("xmid_d", [B, S, E], F32, **_k)
    modd = nc.dram_tensor("modd", [B, E6], F32, **_k)
    qkT_d = nc.dram_tensor("qkT_d", [2 * NE * NE * 128, 128], BF16, **_k)
    wgT_d = nc.dram_tensor("wgT_d", [E, I], BF16, **_k)
    wuT_d = nc.dram_tensor("wuT_d", [E, I], BF16, **_k)
    wdT_d = nc.dram_tensor("wdT_d", [I, E], BF16, **_k)

    identb_d = nc.inline_tensor(np.eye(128, dtype=ml_dtypes.bfloat16), name="identb_c")
    identf_d = nc.inline_tensor(np.eye(128, dtype=np.float32), name="identf_c")
    _c = np.float32(np.sqrt(1e29))
    _ka = (-_c * (np.arange(128)[:, None] > np.arange(128)[None, :])
           ).astype(ml_dtypes.bfloat16)
    _kb = (_c * (np.arange(128)[:, None] <= np.arange(128)[None, :])
           ).astype(ml_dtypes.bfloat16)
    cmA_d = nc.inline_tensor(_ka, name="cmA_c")
    cmB_d = nc.inline_tensor(_kb, name="cmB_c")
    epsc_d = nc.inline_tensor(np.full((128, 1), 1e-5, dtype=np.float32), name="epsc_c")
    scc_d = nc.inline_tensor(np.full((128, 1), SCALE, dtype=np.float32), name="scc_c")
    nbias_d = nc.inline_tensor(np.full((128, 1), -12.0, dtype=np.float32),
                               name="nbias_c")

    SB_BASE = (nc.sbuf_base + 63) // 64 * 64
    SB_LIMIT = nc.sbuf_base + nc.sbuf_bytes_remaining
    off = [0]

    def sbt(name, shape, dtype, align=64):
        per = int(np.prod(shape[1:])) * mybir.dt.size(dtype)
        off[0] = (off[0] + align - 1) // align * align
        h = nc.alloc_sbuf_tensor_at(name, shape, dtype, offset=SB_BASE + off[0])
        off[0] += per
        return h

    identb = sbt("identb", [128, 128], BF16)
    identf = sbt("identf", [128, 128], F32)
    cmA = sbt("cmA", [128, 128], BF16)
    cmB = sbt("cmB", [128, 128], BF16)
    epsc = sbt("epsc", [128, 1], F32)
    scc = sbt("scc", [128, 1], F32)
    nbias = sbt("nbias", [128, 1], F32)
    xs0 = sbt("xs0", [128, E], F32)
    xs1 = sbt("xs1", [128, E], F32)
    ot0 = sbt("ot0", [128, E], F32)
    ot1 = sbt("ot1", [128, E], F32)
    oa = sbt("oa", [128, E], F32)
    xsq = sbt("xsq", [128, E], F32)
    gab = sbt("gab", [128, E], F32)
    caP = sbt("caP", [128, NE], F32)
    saP = sbt("saP", [128, NE], F32)
    stat = sbt("stat", [128, 12], F32)
    mx1 = sbt("mx1", [128, 4], F32)
    qax = (sbt("qax0", [128, 512], F32), sbt("qax1", [128, 512], F32))
    qbb = (sbt("qbb0", [128, 512], BF16), sbt("qbb1", [128, 512], BF16))
    msg = (sbt("msg0", [128, 512], BF16), sbt("msg1", [128, 512], BF16))
    m1 = sbt("m1", [128, 512], BF16)
    m2 = sbt("m2", [128, 512], BF16)
    m3 = sbt("m3", [128, 512], BF16)
    m4 = sbt("m4", [128, 512], BF16)
    m5 = sbt("m5", [128, 512], BF16)
    m6 = sbt("m6", [128, 512], BF16)
    qbm = (sbt("qbm0", [128, 32], F32), sbt("qbm1", [128, 32], F32))
    qr2 = (sbt("qr20", [128, 32], F32), sbt("qr21", [128, 32], F32))
    qs5 = (sbt("qs50", [128, 32], F32), sbt("qs51", [128, 32], F32))

    pers_end = off[0]
    xr0 = sbt("xr0", [128, E], F32)
    xr1 = sbt("xr1", [128, E], F32)
    xhb = sbt("xhb", [128, E], BF16)
    qT = sbt("qT", [128, NE * S], BF16)
    kT = sbt("kT", [128, NE * S], BF16)
    vsb = sbt("vsb", [128, NS * E], BF16)
    wvT = sbt("wvT", [128, NE * E], BF16)
    Msb = sbt("Msb", [128, NE * E], BF16)
    ssb = sbt("ssb", [128, S], F32)
    psb = sbt("psb", [128, S], BF16)
    ptT = sbt("ptT", [128, 8 * 128], BF16)
    wst0 = sbt("wst0", [128, 512], F32)
    wst1 = sbt("wst1", [128, 512], F32)
    wqo = sbt("wqo", [128, 512], BF16)
    wto0 = sbt("wto0", [128, NE * 128], BF16)
    orw = sbt("orw", [128, E], F32)
    wtoA = nc.alloc_sbuf_tensor_at("wtoA", [128, 512], F32,
                                   offset=nc.lookup_mloc(ssb).addr)
    wtoB = nc.alloc_sbuf_tensor_at("wtoB", [128, 512], F32,
                                   offset=nc.lookup_mloc(ptT).addr)
    a_end = off[0]
    print("A region end:", a_end, "limit:", SB_LIMIT - SB_BASE)
    assert SB_BASE + a_end <= SB_LIMIT, f"SBUF over budget (A): {a_end}"

    off[0] = pers_end
    wgT = sbt("wgT_s", [128, NE * I], BF16)
    wuT = sbt("wuT_s", [128, NE * I], BF16)
    wdT = sbt("wdT_s", [128, NI * E], BF16)
    h2f = sbt("h2f", [128, E], F32)
    h2q = sbt("h2q", [128, E], BF16)
    hqT = sbt("hqT", [128, NI * 128], BF16)
    h2qT = sbt("h2qT", [128, E], BF16)
    hqsb = sbt("hqsb", [128, I], BF16)
    sgf = sbt("sgf", [128, 512], F32)
    hidf = sbt("hidf", [128, 512], F32)
    cbb = sbt("cbb", [128, E], F32)
    sbb = sbt("sbb", [128, E], F32)
    b_end = off[0]
    print("B region end:", b_end)
    assert SB_BASE + b_end <= SB_LIMIT, f"SBUF over budget (B): {b_end}"

    psS = nc.alloc_psum_tensor("psS", [128, 2048], F32)
    psO = nc.alloc_psum_tensor("psO", [128, 1024], F32)
    psT0 = nc.alloc_psum_tensor("psT0", [128, 1024], BF16)
    psT1 = nc.alloc_psum_tensor("psT1", [128, 1024], BF16)

    es = contextlib.ExitStack()
    P = Prog(nc)
    for chn in ("DS", "DG", "T", "V", "S", "G"):
        P.channel(chn, es.enter_context(nc.semaphore("ch_" + chn)))

    def dma(eng, dst, src, waits=()):
        ch = "DS" if eng == "sync" else "DG"
        return P.emit(eng, lambda E_, d=dst, s=src: E_.dma_start(d, s),
                      waits=waits, inc=ch, inc_k=16)

    def op(eng, fn, waits=(), sw=True):
        ch = {"vector": "V", "scalar": "S", "tensor": "T", "gpsimd": "G"}[eng]
        return P.emit(eng, fn, waits=waits, inc=ch, sw=sw)

    st = {}

    QP = dict(i=0, sc={}, sg={}, g={}, v={})

    def quant(xin, qout, W, tin, tax=()):
        nb = W // 16
        pi = QP["i"] & 1
        QP["i"] += 1
        ax, bbt, sg = qax[pi], qbb[pi], msg[pi]
        bm, r2, s5 = qbm[pi], qr2[pi], qs5[pi]
        scfree = QP["sc"].get(pi)
        sgfree = QP["sg"].get(pi)
        gfree = QP["g"].get(pi)
        tsa = op("scalar", lambda E_: E_.activation(
            ax[:, 0:W], xin, AF.Abs), waits=(tin, scfree) + tuple(tax))
        tsg = op("scalar", lambda E_: E_.activation(
            sg[:, 0:W], xin, AF.Sign), waits=(tin, sgfree) + tuple(tax), sw=False)
        tbm = op("vector", lambda E_: E_.tensor_reduce(
            bm[:, 0:nb].rearrange("p (n o) -> p n o", o=1),
            ax[:, 0:W].rearrange("p (n b) -> p n b", b=16), AX.X, ALU.max),
            waits=(tsa, gfree), sw=False)
        op("vector", lambda E_: E_.tensor_scalar(
            s5[:, 0:nb], bm[:, 0:nb], 6e-12, 1.0 / 12.0,
            ALU.max, ALU.mult), waits=(tbm,), sw=True)
        tr2 = op("vector", lambda E_: E_.reciprocal(r2[:, 0:nb], s5[:, 0:nb]),
                 sw=True)
        tb = op("vector", lambda E_: E_.tensor_tensor(
            qbb[pi][:, 0:W].rearrange("p (n b) -> p n b", b=16),
            ax[:, 0:W].rearrange("p (n b) -> p n b", b=16),
            bcast16(r2[:, 0:nb], nb), ALU.mult), waits=(tr2, tsa), sw=True)
        QP["sc"][pi] = tb
        vprev = QP["v"].get(1 - pi)
        bbW = bbt[:, 0:W]
        op("vector", lambda E_: E_.tensor_scalar(m1[:, 0:W], bbW, 0.5, None, ALU.is_gt),
           waits=(tb, vprev), sw=True)
        op("vector", lambda E_: E_.tensor_scalar(m2[:, 0:W], bbW, 1.5, None, ALU.is_gt), sw=False)
        op("vector", lambda E_: E_.tensor_scalar(m4[:, 0:W], bbW, 2.5, None, ALU.is_gt), sw=False)
        op("vector", lambda E_: E_.tensor_scalar(m5[:, 0:W], bbW, 3.5, None, ALU.is_gt), sw=False)
        op("vector", lambda E_: E_.tensor_tensor(m3[:, 0:W], m1[:, 0:W], m2[:, 0:W], ALU.add), sw=False)
        op("vector", lambda E_: E_.tensor_tensor(m6[:, 0:W], m4[:, 0:W], m5[:, 0:W], ALU.add), sw=False)
        op("vector", lambda E_: E_.tensor_scalar(m1[:, 0:W], bbW, 5.0, None, ALU.is_gt), sw=False)
        op("vector", lambda E_: E_.tensor_scalar(m2[:, 0:W], bbW, 7.0, None, ALU.is_gt), sw=False)
        op("vector", lambda E_: E_.tensor_scalar(m4[:, 0:W], bbW, 10.0, None, ALU.is_gt), sw=False)
        op("vector", lambda E_: E_.tensor_tensor(m5[:, 0:W], m1[:, 0:W], m2[:, 0:W], ALU.add), sw=False)
        op("vector", lambda E_: E_.tensor_tensor(m2[:, 0:W], m3[:, 0:W], m6[:, 0:W], ALU.add), sw=False)
        op("vector", lambda E_: E_.scalar_tensor_tensor(
            m1[:, 0:W], m5[:, 0:W], 2.0, m2[:, 0:W], ALU.mult, ALU.add), sw=True)
        tq2 = op("vector", lambda E_: E_.scalar_tensor_tensor(
            m3[:, 0:W], m4[:, 0:W], 4.0, m1[:, 0:W], ALU.mult, ALU.add), sw=False)
        tsgm = op("vector", lambda E_: E_.tensor_tensor(
            m4[:, 0:W], m3[:, 0:W], sg[:, 0:W], ALU.mult), waits=(tq2, tsg), sw=False)
        QP["sg"][pi] = tsgm
        tq = op("vector", lambda E_: E_.tensor_tensor(
            qout.rearrange("p (n b) -> p n b", b=16),
            m4[:, 0:W].rearrange("p (n b) -> p n b", b=16),
            bcast16(s5[:, 0:nb], nb), ALU.mult), waits=(tsgm,), sw=True)
        QP["g"][pi] = tq
        QP["v"][pi] = tq
        st["q_prev"] = tq
        return tq, tsg

    tid = dma("sync", identb[:], identb_d[:])
    tidf = dma("sync", identf[:], identf_d[:])
    tmA = dma("sync", cmA[:], cmA_d[:])
    tmB = dma("sync", cmB[:], cmB_d[:])
    teps = dma("sync", epsc[:], epsc_d[:])
    tscc = dma("sync", scc[:], scc_d[:])
    tnb = dma("sync", nbias[:], nbias_d[:])
    tc = dma("sync", xs0[0:B, 0:E], c_in[:, :])

    tcs = op("scalar", lambda E_: E_.activation(xs1[0:B, 0:E], xs0[0:B, 0:E],
                                                AF.Silu), waits=(tc,))
    csT = xhb
    tr_last = None
    for c_ in range(NE):
        tr_last = op("tensor", lambda E_, c_=c_: E_.transpose(
            psO[:, c_ * B:(c_ + 1) * B], xs1[0:B, c_ * 128:(c_ + 1) * 128],
            identf[0:B, 0:B]), waits=(tcs, tidf))
    tev_cs = op("vector", lambda E_: E_.tensor_copy(
        csT[:, 0:NE * B], psO[:, 0:NE * B]), waits=(tr_last,))

    wmst = (wst0, wst1, wtoA, wtoB)
    wmbf = (m1, m2, m5, m6)
    NT6 = E6 // 512
    bufdep = {}
    castdep = {}
    mm_last = None
    tmod_done = None
    for nt in range(NT6):
        mm_slot = nt % 2
        evac_prev = st.get("modev%d" % mm_slot)
        for e in range(NE):
            bi = (nt * NE + e) % 4
            tld = dma("gpsimd" if bi % 2 == 0 else "sync", wmst[bi][:, 0:512],
                      wmod_in[e * 128:(e + 1) * 128, nt * 512:(nt + 1) * 512],
                      waits=(bufdep.get(bi),))
            tcast = op("vector", lambda E_, bi=bi: E_.tensor_copy(
                wmbf[bi][:, 0:512], wmst[bi][:, 0:512]), waits=(tld,))
            castdep[bi] = tcast
            mm_last = op("tensor", lambda E_, e=e, bi=bi, ms=mm_slot: E_.matmul(
                psO[0:B, ms * 512:(ms + 1) * 512], csT[:, e * B:(e + 1) * B],
                wmbf[bi][:, 0:512], start=(e == 0), stop=(e == NE - 1)),
                waits=(tcast, tev_cs, evac_prev))
            bufdep[bi] = mm_last
        lo = nt * 512
        need_clamp = (2 * E <= lo < 3 * E) or (5 * E <= lo < 6 * E)
        tev2 = op("vector", lambda E_, ms=mm_slot: E_.tensor_copy(
            ot0[0:B, 0:512], psO[0:B, ms * 512:(ms + 1) * 512]), waits=(mm_last,))
        if need_clamp:
            op("vector", lambda E_: E_.tensor_scalar(
                ot1[0:B, 0:512], ot0[0:B, 0:512], 0.25, None, ALU.max))
            op("vector", lambda E_: E_.tensor_scalar(
                oa[0:B, 0:512], ot0[0:B, 0:512], -0.25, None, ALU.min))
            op("vector", lambda E_: E_.tensor_scalar(
                xsq[0:B, 0:512], ot0[0:B, 0:512], 0.0, None, ALU.is_ge))
            op("vector", lambda E_: E_.tensor_tensor(
                xs1[0:B, 0:512], ot1[0:B, 0:512], oa[0:B, 0:512], ALU.subtract))
            op("vector", lambda E_: E_.tensor_tensor(
                ot1[0:B, 0:512], xs1[0:B, 0:512], xsq[0:B, 0:512], ALU.mult))
            tev2 = op("vector", lambda E_: E_.tensor_tensor(
                ot0[0:B, 0:512], oa[0:B, 0:512], ot1[0:B, 0:512], ALU.add))
        st["modev%d" % mm_slot] = tev2
        tmod_done = dma("sync", modd[:, nt * 512:(nt + 1) * 512],
                        ot0[0:B, 0:512], waits=(tev2,))
    st["p0cast"] = castdep.get(2)
    st["p0castB"] = castdep.get(3)

    half = (NE + 1) // 2
    wqstg, wkstg = qT, kT
    tstg = None
    for ep in range(NE):
        tstg = dma("gpsimd", wqstg[:, ep * E:(ep + 1) * E],
                   wq_in[ep * 128:(ep + 1) * 128, :])
        tstg = dma("gpsimd", wkstg[:, ep * E:(ep + 1) * E],
                   wk_in[ep * 128:(ep + 1) * 128, :])
    tstg_s = None
    mev = {}
    tM_done = None
    tM_mm = None
    for ec in range(NE):
        for dh in range(NHF):
            slot = ((ec * NHF + dh) % 4) * 512
            mm = None
            for ep in range(NE):
                mm = op("tensor", lambda E_, ec=ec, dh=dh, ep=ep, slot=slot: E_.matmul(
                    psS[:, slot:slot + 512],
                    wqstg[:, ep * E + ec * 128: ep * E + ec * 128 + 128],
                    wkstg[:, ep * E + dh * 512: ep * E + dh * 512 + 512],
                    start=(ep == 0), stop=(ep == NE - 1)),
                    waits=(tstg, tstg_s, mev.get(slot)))
            tM_mm = mm
            tM_done = op("scalar", lambda E_, ec=ec, dh=dh, slot=slot: E_.activation(
                Msb[:, ec * E + dh * 512: ec * E + dh * 512 + 512],
                psS[:, slot:slot + 512], AF.Identity), waits=(mm,), sw=False)
            mev[slot] = tM_done

    bufdep2 = {0: mm_last, 1: mm_last}
    prev_tr = (tid,)
    twv_done = None
    qkv_tr_prev = mm_last
    for n in range(NE):
        bi = n % 2
        stg = wst0 if bi == 0 else wst1
        tca = []
        tprev = None
        for pc in range(E // 512):
            cbf = m3 if pc % 2 == 0 else m4
            tldp = dma("gpsimd", stg[:, 0:512],
                       wv_in[n * 128:(n + 1) * 128, pc * 512:(pc + 1) * 512],
                       waits=(bufdep2.get(bi), tprev, mm_last))
            tcp = op("vector", lambda E_, cbf=cbf, stg=stg: E_.tensor_copy(
                cbf[:, 0:512], stg[:, 0:512]),
                waits=(tldp, mm_last, qkv_tr_prev))
            tprev = tcp
            tca.append(tcp)
        bufdep2[bi] = tca[-1]
        trk = []
        for e in range(NE):
            pc = (e * 128) // 512
            cbf = m3 if pc % 2 == 0 else m4
            ecol = (e * 128) % 512
            bank = psT0 if e < half else psT1
            slot = e % half
            tt = op("tensor", lambda E_, bank=bank, slot=slot, cbf=cbf, ecol=ecol: E_.transpose(
                bank[:, slot * 128:(slot + 1) * 128],
                cbf[:, ecol:ecol + 128], identb[:]),
                waits=(tca[pc], tid, tM_mm) + prev_tr)
            trk.append(tt)
        qkv_tr_prev = trk[-1]

        def _evv(E_, n=n, banki=0):
            bank = psT0 if banki == 0 else psT1
            e0 = banki * half
            cnt = half if banki == 0 else NE - half
            base = wvT[:]
            dst = bass.AP(base.tensor, base.offset + e0 * E + n * 128,
                          [list(base.ap[0]), [E, cnt], [1, 128]])
            src = bank[:, 0:cnt * 128].rearrange("p (c w) -> p c w", w=128)
            return E_.tensor_copy(dst, src)
        ev0 = op("vector", lambda E_, n=n: _evv(E_, n, 0), waits=(trk[half - 1],))
        ev1 = op("vector", lambda E_, n=n: _evv(E_, n, 1), waits=(trk[-1],))
        twv_done = ev1
        prev_tr = (ev0, ev1)
    st["psT_ev"] = prev_tr[-1]
    st["wqo_tr"] = (twv_done, qkv_tr_prev)

    wchunks = []
    for w_d, wT_d, R, C in ((wg_in, wgT_d, I, E), (wu_in, wuT_d, I, E)):
        for r in range(R // 128):
            for cc in range(C // 512):
                wchunks.append((w_d, wT_d, r, cc * 512, 512, R))
    c_list = []
    c0 = 0
    while c0 < I:
        w_ = min(512, I - c0)
        c_list.append((c0, w_))
        c0 += w_
    for r in range(E // 128):
        for (c0, w_) in c_list:
            wchunks.append((wd_in, wdT_d, r, c0, w_, E))

    wqs = dict(free0=(mm_last, bufdep2.get(0)), free1=(mm_last, bufdep2.get(1)),
               stfree={}, bufi=0)

    def emit_wchunk(idx):
        if idx >= len(wchunks):
            return
        w_d, wT_d, r, c0, W, R = wchunks[idx]
        bi = wqs["bufi"]
        wqs["bufi"] ^= 1
        stg = wst0 if bi == 0 else wst1
        fr = wqs["free%d" % bi]
        tld = dma("gpsimd" if bi == 0 else "sync",
                  stg[:, 0:W], w_d[r * 128:(r + 1) * 128, c0:c0 + W],
                  waits=fr)
        tq, tsg = quant(stg[:, 0:W], wqo[:, 0:W], W, tld,
                        tax=st.get("wqo_tr", ()))
        wqs["free%d" % bi] = (tq, tsg)
        ntile = W // 128
        trk = []
        for e in range(ntile):
            bank = psT0 if e < 4 else psT1
            slot = e % 4
            tt = op("tensor", lambda E_, bank=bank, slot=slot, e=e: E_.transpose(
                bank[:, slot * 128:(slot + 1) * 128],
                wqo[:, e * 128:(e + 1) * 128], identb[:]),
                waits=(tq, st.get("psT_ev")))
            trk.append(tt)
        st["wqo_tr"] = (trk[-1],)
        wto = wto0
        stfree = wqs["stfree"].get(0)
        n0 = min(4, ntile)
        evs = [op("scalar", lambda E_, n0=n0, wto=wto: E_.activation(
            wto[:, 0:n0 * 128], psT0[:, 0:n0 * 128], AF.Identity),
            waits=(trk[n0 - 1], stfree), sw=False)]
        if ntile > 4:
            evs.append(op("scalar", lambda E_, ntile=ntile, wto=wto: E_.activation(
                wto[:, 4 * 128:ntile * 128], psT1[:, 0:(ntile - 4) * 128],
                AF.Identity), waits=(trk[-1], stfree), sw=False))
        st["psT_ev"] = evs[-1]
        dst = bass.AP(wT_d[:].tensor, c0 * R + r * 128,
                      [[R, 128], [128 * R, ntile], [1, 128]])
        tst = dma("gpsimd", dst,
                  wto[:, 0:ntile * 128].rearrange("p (c w) -> p c w", w=128),
                  waits=tuple(evs))
        wqs["stfree"][0] = tst

    WCH = [0]

    def next_wchunks(k):
        if not interleave_w:
            return
        for _ in range(k):
            if WCH[0] < len(wchunks):
                emit_wchunk(WCH[0])
                WCH[0] += 1

    ast = dict(x={}, xr={}, omst={}, tom=None, tga=None, sc_evac=None,
               texp=None, tmx=None, o_evac=None, ptr_last=None, t3=None,
               t5=None, xhb_tr=None, pv_last=None, trl=None,
               qslot={}, vslot={},
               qe={"q": None, "k": None, "v": None}, ptev={},
               ptT_mm={0: st["p0castB"], 1: st["p0castB"]},
               trlp={}, trmp={}, cap=None)
    xm_last = None
    for s in range(B):
        t_car = dma("sync", oa[0:1, 0:E], modd[s:s + 1, E:2 * E],
                    waits=(tmod_done, ast["tga"]))
        t_sar = dma("sync", orw[0:1, 0:E], modd[s:s + 1, 0:E],
                    waits=(tmod_done, ast["tom"]))
        t_gab = dma("sync", gab[:],
                    bass.AP(modd[:].tensor, s * E6 + 2 * E, [[0, 128], [1, E]]),
                    waits=(tmod_done, ast["tga"]))
        trc = trs = None
        for c_ in range(NE):
            trc = op("tensor", lambda E_, c_=c_: E_.transpose(
                psO[:, c_:c_ + 1], oa[0:1, c_ * 128:(c_ + 1) * 128],
                identf[0:1, 0:1]), waits=(t_car, tidf, ast["o_evac"]))
            trs = op("tensor", lambda E_, c_=c_: E_.transpose(
                psO[:, 512 + c_:512 + c_ + 1], orw[0:1, c_ * 128:(c_ + 1) * 128],
                identf[0:1, 0:1]), waits=(t_sar,))
        t_cap = op("scalar", lambda E_: E_.activation(
            caP[:, 0:NE], psO[:, 0:NE], AF.Identity, bias=1.0), waits=(trc,))
        t_sap = op("scalar", lambda E_: E_.activation(
            saP[:, 0:NE], psO[:, 512:512 + NE], AF.Identity), waits=(trs,))
        ast["o_evac"] = t_sap
        ast["cap"] = (t_cap, t_sap)

        xst = (xs0, xs1, xr0, xr1)
        for g in range(NSG):
            h_last = None
            for j in range(4):
                ti = g * 4 + j
                bi = ti % 4
                t_ld = dma("sync", xst[bi][:], x_in[s, ti * 128:(ti + 1) * 128, :],
                           waits=(ast["x"].get(bi), tmod_done,
                                  ast["xr"].get(bi - 2) if bi >= 2 else None,
                                  tcs if s == 0 and g == 0 else None))
                op("vector", lambda E_, bi=bi: E_.tensor_reduce(
                    stat[:, 0:1], xst[bi][:], AX.X, ALU.add),
                    waits=(t_ld, ast["t5"]))
                t2 = op("scalar", lambda E_, bi=bi: E_.activation(
                    xsq[:], xst[bi][:], AF.Square, accum_out=stat[:, 1:2]),
                    waits=(t_ld, ast["t3"], tmod_done, ast["tom"]), sw=False)
                ast["t3"] = t2
                op("vector", lambda E_: E_.tensor_scalar(
                    stat[:, 2:3], stat[:, 0:1], 1.0 / E, None, ALU.mult))
                op("vector", lambda E_: E_.tensor_scalar(
                    stat[:, 3:4], stat[:, 1:2], 1.0 / E, None, ALU.mult),
                    waits=(t2,))
                op("vector", lambda E_: E_.tensor_tensor(
                    stat[:, 4:5], stat[:, 2:3], stat[:, 2:3], ALU.mult))
                t4 = op("vector", lambda E_: E_.tensor_tensor(
                    stat[:, 5:6], stat[:, 3:4], stat[:, 4:5], ALU.subtract))
                t5 = op("scalar", lambda E_: E_.activation(
                    stat[:, 6:7], stat[:, 5:6], AF.Sqrt, bias=epsc[:, 0:1]),
                    waits=(t4, teps))
                ast["t5"] = t5
                t6 = op("vector", lambda E_: E_.reciprocal(
                    stat[:, 7:8], stat[:, 6:7]), waits=(t5,))
                t7 = op("vector", lambda E_: E_.scalar_tensor_tensor(
                    stat[:, 8:9], stat[:, 2:3], -1.0, stat[:, 7:8],
                    ALU.mult, ALU.mult))
                t8 = op("scalar", lambda E_, bi=bi: E_.activation(
                    xhb[:], xst[bi][:], AF.Identity, bias=stat[:, 8:9],
                    scale=stat[:, 7:8]),
                    waits=(t7, tmod_done, ast["xhb_tr"],
                           tev_cs if s == 0 and g == 0 else None), sw=False)
                ast["x"][bi] = t8
                trk = []
                for e in range(NE):
                    bank = psT0 if e < half else psT1
                    slot = e % half
                    tt = op("tensor", lambda E_, bank=bank, slot=slot, e=e: E_.transpose(
                        bank[:, slot * 128:(slot + 1) * 128],
                        xhb[:, e * 128:(e + 1) * 128], identb[:]),
                        waits=(t8, tid, st.get("psT_ev")))
                    trk.append(tt)
                ast["xhb_tr"] = trk[-1]
                for e in range(NE):
                    bank = psT0 if e < half else psT1
                    slot = e % half
                    h_last = op("scalar", lambda E_, bank=bank, slot=slot, e=e, j=j, g=g: E_.activation(
                        kT[:, e * S + g * 512 + j * 128: e * S + g * 512 + (j + 1) * 128],
                        bank[:, slot * 128:(slot + 1) * 128],
                        AF.Identity, bias=saP[:, e:e + 1], scale=caP[:, e:e + 1]),
                        waits=(trk[e], ast["texp"], tM_mm) + ast["cap"],
                        sw=(e == 0 and j == 0))
                st["psT_ev"] = h_last
            for n in range(NE):
                base = (n % 2) * 512
                mm = None
                for e in range(NE):
                    mm = op("tensor", lambda E_, n=n, e=e, base=base, g=g: E_.matmul(
                        psS[:, base:base + 512],
                        Msb[:, e * E + n * 128: e * E + n * 128 + 128],
                        kT[:, e * S + g * 512: e * S + (g + 1) * 512],
                        start=(e == 0), stop=(e == NE - 1)),
                        waits=(h_last, tM_done, ast["qslot"].get(base)))
                ast["qe"]["q"] = op(
                    "scalar", lambda E_, n=n, g=g, base=base: E_.activation(
                        qT[:, n * S + g * 512: n * S + (g + 1) * 512],
                        psS[:, base:base + 512], AF.Identity),
                    waits=(mm, ast["pv_last"], tM_mm), sw=False)
                ast["qslot"][base] = ast["qe"]["q"]
            for j in range(4):
                ti = g * 4 + j
                for hf in range(NHF):
                    mm = None
                    for e in range(NE):
                        mm = op("tensor", lambda E_, j=j, hf=hf, e=e, g=g: E_.matmul(
                            psS[:, 1024 + hf * 512: 1024 + (hf + 1) * 512],
                            kT[:, e * S + g * 512 + j * 128: e * S + g * 512 + (j + 1) * 128],
                            wvT[:, e * E + hf * 512: e * E + (hf + 1) * 512],
                            start=(e == 0), stop=(e == NE - 1)),
                            waits=(h_last, twv_done, ast["vslot"].get(hf)))
                    ast["qe"]["v"] = op(
                        "scalar", lambda E_, ti=ti, hf=hf: E_.activation(
                            vsb[:, ti * E + hf * 512: ti * E + (hf + 1) * 512],
                            psS[:, 1024 + hf * 512: 1024 + (hf + 1) * 512],
                            AF.Identity),
                        waits=(mm, ast["pv_last"]), sw=False)
                    ast["vslot"][hf] = ast["qe"]["v"]
            next_wchunks(4)
        ast["qe"]["k"] = h_last

        ssbs = (ssb, ssb)

        def a_stage1(i):
            W = (i + 1) * 128
            NK4 = (W + 511) // 512
            dk4 = (i * 128) // 512
            doff = (i * 128) % 512
            sc_mm = None
            for e in range(NE):
                for k4 in range(NK4):
                    w_ = min(512, W - k4 * 512)
                    sc_mm = op("tensor", lambda E_, e=e, k4=k4, w_=w_, i=i: E_.matmul(
                        psS[:, k4 * 512:k4 * 512 + w_],
                        qT[:, e * S + i * 128:e * S + (i + 1) * 128],
                        kT[:, e * S + k4 * 512:e * S + k4 * 512 + w_],
                        start=(e == 0),
                        stop=(e == NE - 1 and k4 != dk4)),
                        waits=(ast["qe"]["q"], ast["qe"]["k"], ast["sc_evac"]))
            sc_mm = op("tensor", lambda E_, dk4=dk4, doff=doff: E_.matmul(
                psS[:, dk4 * 512 + doff: dk4 * 512 + doff + 128],
                cmA[:], cmB[:], start=False, stop=True),
                waits=(tmA, tmB))
            ev = None
            for k4 in range(NK4):
                w_ = min(512, W - k4 * 512)
                ev = op("scalar", lambda E_, k4=k4, w_=w_, i=i: E_.activation(
                    ssbs[i % 2][:, k4 * 512:k4 * 512 + w_],
                    psS[:, k4 * 512:k4 * 512 + w_],
                    AF.Identity, scale=scc[:, 0:1]),
                    waits=(sc_mm, tscc, st["p0cast"]), sw=False)
            ast["sc_evac"] = ev

        def a_stage2_front(i):
            pi2 = i % 2
            texp = op("scalar", lambda E_, W=(i + 1) * 128, i=i, pi2=pi2: E_.activation(
                psb[:, 0:W], ssbs[i % 2][:, 0:W], AF.Exp, bias=nbias[:, 0:1],
                accum_out=mx1[:, pi2 * 2:pi2 * 2 + 1]),
                waits=(tnb, ast["ptr_last"], ast["trlp"].get(pi2)), sw=False)
            ast["texp"] = texp
            trl = op("vector", lambda E_, pi2=pi2: E_.reciprocal(
                mx1[:, pi2 * 2 + 1:pi2 * 2 + 2], mx1[:, pi2 * 2:pi2 * 2 + 1]),
                waits=(texp, ast["trmp"].get(pi2)))
            ast["trl"] = trl
            ast["trlp"][pi2] = trl

        def a_stage2(i, s):
            W = (i + 1) * 128
            bi = i % 2
            xrb = xr0 if bi == 0 else xr1
            t_ldr = dma("sync", xrb[:], x_in[s, i * 128:(i + 1) * 128, :],
                        waits=(ast["xr"].get(bi), ast["x"].get(bi + 2),
                               t_car, t_sar, t_cap if bi == 0 else t_sap))
            texp = ast["texp"]
            trl = ast["trl"]
            pv_mm = None
            NKT = i + 1
            for g4 in range(0, NKT, 4):
                cnt = min(4, NKT - g4)
                pgrp = (g4 // 4) % 2
                bank = psT0 if pgrp == 0 else psT1
                trk = []
                for kk in range(cnt):
                    kt = g4 + kk
                    tt = op("tensor", lambda E_, bank=bank, kk=kk, kt=kt: E_.transpose(
                        bank[:, kk * 128:(kk + 1) * 128],
                        psb[:, kt * 128:(kt + 1) * 128], identb[:]),
                        waits=(texp, st.get("psT_ev"), ast["ptev"].get(pgrp)))
                    trk.append(tt)
                ast["ptr_last"] = trk[-1]
                tevp = op("scalar", lambda E_, bank=bank, cnt=cnt, pgrp=pgrp: E_.activation(
                    ptT[:, pgrp * 512: pgrp * 512 + cnt * 128], bank[:, 0:cnt * 128],
                    AF.Identity), waits=(trk[-1], ast["ptT_mm"].get(pgrp)), sw=False)
                st["psT_ev"] = tevp
                ast["ptev"][pgrp] = tevp
                for kk in range(cnt):
                    kt = g4 + kk
                    for hf in range(NHF):
                        pv_mm = op("tensor", lambda E_, kt=kt, kk=kk, hf=hf, pgrp=pgrp, i=i: E_.matmul(
                            psO[:, hf * 512:(hf + 1) * 512],
                            ptT[:, pgrp * 512 + kk * 128: pgrp * 512 + (kk + 1) * 128],
                            vsb[:, kt * E + hf * 512: kt * E + (hf + 1) * 512],
                            start=(kt == 0), stop=(kt == i)),
                            waits=(tevp, ast["qe"]["v"], ast["o_evac"]))
                ast["ptT_mm"][pgrp] = pv_mm
            ast["pv_last"] = pv_mm
            oev = None
            for hf in range(NHF):
                oev = op("scalar", lambda E_, hf=hf: E_.activation(
                    oa[:, hf * 512:(hf + 1) * 512], psO[:, hf * 512:(hf + 1) * 512],
                    AF.Identity), waits=(pv_mm, ast["tga"]), sw=False)
            ast["o_evac"] = oev
            tga = op("gpsimd", lambda E_: E_.tensor_tensor(
                xsq[:], oa[:], gab[:], ALU.mult), waits=(oev, t_gab))
            ast["tga"] = tga
            pi2 = i % 2
            _m = mx1[:]
            rsc = bass.AP(_m.tensor, _m.offset + pi2 * 2 + 1,
                          [list(_m.ap[0]), [0, E]])
            trm = op("gpsimd", lambda E_, rsc=rsc: E_.tensor_tensor(
                orw[:], xsq[:], rsc, ALU.mult), waits=(trl,))
            ast["trmp"][pi2] = trm
            obuf = ot0 if bi == 0 else ot1
            tom = op("gpsimd", lambda E_, obuf=obuf, xrb=xrb: E_.tensor_tensor(
                obuf[:], orw[:], xrb[:], ALU.add),
                waits=(trm, t_ldr, ast["omst"].get(bi)))
            ast["tom"] = tom
            ast["xr"][bi] = tom
            xm = dma("sync", xmid_d[s, i * 128:(i + 1) * 128, :], obuf[:],
                     waits=(tom,))
            ast["omst"][bi] = xm
            next_wchunks(2)
            return xm

        a_stage1(0)
        for i in range(NS):
            a_stage2_front(i)
            if i + 1 < NS:
                a_stage1(i + 1)
            xm_last = a_stage2(i, s)

    while WCH[0] < len(wchunks):
        emit_wchunk(WCH[0])
        WCH[0] += 1

    barrier = tuple(x for x in (xm_last, wqs["stfree"].get(0),
                                wqs["stfree"].get(1)) if x is not None)
    t_wB = None
    for e in range(NE):
        t_wB = dma("gpsimd", wgT[:, e * I:(e + 1) * I],
                   wgT_d[e * 128:(e + 1) * 128, :], waits=barrier)
    for e in range(NE):
        t_wB = dma("gpsimd", wuT[:, e * I:(e + 1) * I],
                   wuT_d[e * 128:(e + 1) * 128, :], waits=barrier)
    for ic in range(NI):
        t_wB = dma("gpsimd", wdT[:, ic * E:(ic + 1) * E],
                   wdT_d[ic * 128:(ic + 1) * 128, :], waits=barrier)

    bst = dict(x={}, ost={}, tout=None, tgb=barrier[0], t3=None, t5=None,
               h2q_tr=(), hq_tr=(), gate_mm=None, mmd=None, thd=None,
               tsg=None, t9=None, sl={0: None, 1: None})
    itiles = c_list

    def b_back(s, i, bi, xb_, mmd, t_gbb):
        tgb = op("vector", lambda E_: E_.tensor_tensor(
            oa[:], psO[:, 0:E], gab[:], ALU.mult),
            waits=(mmd, t_gbb, bst["tout"]), sw=False)
        bst["tgb"] = tgb
        obuf = ot0 if bi == 0 else ot1
        tout = op("gpsimd", lambda E_, obuf=obuf, xb_=xb_: E_.tensor_tensor(
            obuf[:], oa[:], xb_[:], ALU.add),
            waits=(tgb, bst["ost"].get(bi)))
        bst["tout"] = tout
        bst["x"][bi] = tout
        tst = dma("gpsimd", out_d[s, i * 128:(i + 1) * 128, :], obuf[:],
                  waits=(tout,))
        bst["ost"][bi] = tst

    pend = None
    for s in range(B):
        if pend is not None:
            b_back(*pend)
            pend = None
        t_cbr = dma("sync", h2f[:],
                    bass.AP(modd[:].tensor, s * E6 + 4 * E, [[0, 128], [1, E]]),
                    waits=barrier + (bst["t9"], bst["tsg"]))
        t_cb1 = op("vector", lambda E_: E_.tensor_scalar(
            cbb[:], h2f[:], 1.0, None, ALU.add), waits=(t_cbr, bst["t9"]))
        t_sbb = dma("sync", sbb[:],
                    bass.AP(modd[:].tensor, s * E6 + 3 * E, [[0, 128], [1, E]]),
                    waits=barrier + (bst["t9"],))
        t_gbb = dma("sync", gab[:],
                    bass.AP(modd[:].tensor, s * E6 + 5 * E, [[0, 128], [1, E]]),
                    waits=barrier + (ast["tga"], bst["tgb"]))
        for i in range(NS):
            bi = i % 2
            xb_ = xs0 if bi == 0 else xs1
            t_ld = dma("sync", xb_[:], xmid_d[s, i * 128:(i + 1) * 128, :],
                       waits=(bst["x"].get(bi), xm_last))
            op("vector", lambda E_, xb_=xb_: E_.tensor_reduce(
                stat[:, 0:1], xb_[:], AX.X, ALU.add), waits=(t_ld, bst["t5"]))
            t2 = op("scalar", lambda E_, xb_=xb_: E_.activation(
                xsq[:], xb_[:], AF.Square, accum_out=stat[:, 1:2]),
                waits=(t_ld, bst["t3"], ast["tga"], bst["t9"]), sw=False)
            bst["t3"] = t2
            op("vector", lambda E_: E_.tensor_scalar(
                stat[:, 2:3], stat[:, 0:1], 1.0 / E, None, ALU.mult))
            op("vector", lambda E_: E_.tensor_scalar(
                stat[:, 3:4], stat[:, 1:2], 1.0 / E, None, ALU.mult),
                waits=(t2,))
            op("vector", lambda E_: E_.tensor_tensor(
                stat[:, 4:5], stat[:, 2:3], stat[:, 2:3], ALU.mult))
            t4 = op("vector", lambda E_: E_.tensor_tensor(
                stat[:, 5:6], stat[:, 3:4], stat[:, 4:5], ALU.subtract))
            t5 = op("scalar", lambda E_: E_.activation(
                stat[:, 6:7], stat[:, 5:6], AF.Sqrt, bias=epsc[:, 0:1]),
                waits=(t4, teps))
            bst["t5"] = t5
            t6 = op("vector", lambda E_: E_.reciprocal(stat[:, 7:8], stat[:, 6:7]),
                    waits=(t5,))
            t7 = op("vector", lambda E_: E_.scalar_tensor_tensor(
                stat[:, 8:9], stat[:, 2:3], -1.0, stat[:, 7:8], ALU.mult, ALU.mult))
            t8 = op("scalar", lambda E_, xb_=xb_: E_.activation(
                h2f[:], xb_[:], AF.Identity, bias=stat[:, 8:9], scale=stat[:, 7:8]),
                waits=(t7, t_cb1, bst["tsg"]), sw=False)
            t9a = op("gpsimd", lambda E_: E_.tensor_tensor(
                xsq[:], h2f[:], cbb[:], ALU.mult), waits=(t8, t_cb1))
            t9 = op("gpsimd", lambda E_: E_.tensor_tensor(
                h2f[:], xsq[:], sbb[:], ALU.add), waits=(t_sbb,))
            bst["t9"] = t9
            tqs = None
            for hc in range(E // 512):
                tqs, tsgh2 = quant(h2f[:, hc * 512:(hc + 1) * 512],
                                   h2q[:, hc * 512:(hc + 1) * 512], 512, t9,
                                   tax=bst["h2q_tr"])
                bst["tsg"] = tsgh2
            tq = tqs
            trk = []
            for e in range(NE):
                bank = psT0 if e < half else psT1
                slot = e % half
                tt = op("tensor", lambda E_, bank=bank, slot=slot, e=e: E_.transpose(
                    bank[:, slot * 128:(slot + 1) * 128],
                    h2q[:, e * 128:(e + 1) * 128], identb[:]),
                    waits=(tq, st.get("psT_ev"), tid))
                trk.append(tt)
            bst["h2q_tr"] = (trk[-1],)
            tev0 = op("scalar", lambda E_: E_.activation(
                h2qT[:, 0:half * 128], psT0[:, 0:half * 128], AF.Identity),
                waits=(trk[half - 1], bst["gate_mm"]), sw=False)
            tev1 = op("scalar", lambda E_: E_.activation(
                h2qT[:, half * 128:NE * 128], psT1[:, 0:(NE - half) * 128],
                AF.Identity), waits=(trk[-1], bst["gate_mm"]), sw=False)
            st["psT_ev"] = tev1
            th2qT = tev1
            sltoks = []
            for it, (c0, w_) in enumerate(itiles):
                sl = it % 2
                gslot = 0 if sl == 0 else 1024
                uslot = 512 if sl == 0 else 1536
                mmg = None
                for e in range(NE):
                    mmg = op("tensor", lambda E_, e=e, c0=c0, w_=w_, gslot=gslot: E_.matmul(
                        psS[:, gslot:gslot + w_], h2qT[:, e * 128:(e + 1) * 128],
                        wgT[:, e * I + c0:e * I + c0 + w_],
                        start=(e == 0), stop=(e == NE - 1)),
                        waits=(th2qT, t_wB, bst["sl"][sl]))
                for e in range(NE):
                    mmg = op("tensor", lambda E_, e=e, c0=c0, w_=w_, uslot=uslot: E_.matmul(
                        psS[:, uslot:uslot + w_], h2qT[:, e * 128:(e + 1) * 128],
                        wuT[:, e * I + c0:e * I + c0 + w_],
                        start=(e == 0), stop=(e == NE - 1)))
                bst["gate_mm"] = mmg
                tsl = op("scalar", lambda E_, gslot=gslot, w_=w_: E_.activation(
                    sgf[:, 0:w_], psS[:, gslot:gslot + w_], AF.Silu),
                    waits=(mmg, bst["thd"]), sw=False)
                thd = op("vector", lambda E_, uslot=uslot, w_=w_: E_.tensor_tensor(
                    hidf[:, 0:w_], sgf[:, 0:w_], psS[:, uslot:uslot + w_], ALU.mult),
                    waits=(tsl, bst["sl"][sl], bst["tsg"]), sw=False)
                bst["thd"] = thd
                hq_dst = bass.AP(hqsb[:].tensor, hqsb[:].offset + c0,
                                 [list(hqsb[:].ap[0]), [1, w_]])
                tqh, tsg_h = quant(hidf[:, 0:w_], hq_dst, w_, thd,
                                   tax=bst["hq_tr"])
                bst["sl"][sl] = tqh
                bst["tsg"] = tsg_h
                sltoks.append(tqh)
            trk2 = None
            for icb in range(0, NI, 4):
                cnt = min(4, NI - icb)
                bank = psT0 if (icb // 4) % 2 == 0 else psT1
                grp = []
                for kk in range(cnt):
                    trk2 = op("tensor", lambda E_, bank=bank, kk=kk, icb=icb: E_.transpose(
                        bank[:, kk * 128:(kk + 1) * 128],
                        hqsb[:, (icb + kk) * 128:(icb + kk + 1) * 128], identb[:]),
                        waits=(sltoks[icb // 4], st.get("psT_ev")))
                    grp.append(trk2)
                tevh = op("scalar", lambda E_, bank=bank, cnt=cnt, icb=icb: E_.activation(
                    hqT[:, icb * 128:(icb + cnt) * 128], bank[:, 0:cnt * 128],
                    AF.Identity), waits=(grp[-1], bst["mmd"]), sw=False)
                st["psT_ev"] = tevh
            bst["hq_tr"] = (trk2,)
            thqT = st["psT_ev"]
            if pend is not None:
                b_back(*pend)
            mmd = None
            for ic in range(NI):
                for hf in range(NHF):
                    mmd = op("tensor", lambda E_, ic=ic, hf=hf: E_.matmul(
                        psO[:, hf * 512:(hf + 1) * 512],
                        hqT[:, ic * 128:(ic + 1) * 128],
                        wdT[:, ic * E + hf * 512: ic * E + (hf + 1) * 512],
                        start=(ic == 0), stop=(ic == NI - 1)),
                        waits=(thqT, t_wB, bst["tgb"]))
            bst["mmd"] = mmd
            pend = (s, i, bi, xb_, mmd, t_gbb)
    b_back(*pend)

    final_ds = Tok("DS", P.chan["DS"][1], "sync", True)
    final_dg = Tok("DG", P.chan["DG"][1], "gpsimd", True)
    P.wait_only("sync", (final_ds,))
    P.wait_only("gpsimd", (final_dg,))

    with nc.Block() as block:
        @block.sync
        def _(sync):
            P.run(sync, "sync")

        @block.vector
        def _(vector):
            P.run(vector, "vector")

        @block.scalar
        def _(scalar):
            P.run(scalar, "scalar")

        @block.tensor
        def _(tensor):
            P.run(tensor, "tensor")

        @block.gpsimd
        def _(gpsimd):
            P.run(gpsimd, "gpsimd")

    es.close()
    return nc


_NC_CACHE = {}


def kernel(L0_attn_x, L0_attn_WQ, L0_attn_WK, L0_attn_WV,
           L0_ffn_w_gate, L0_ffn_w_up, L0_ffn_w_down, L0_c, L0_wmod):
    cfg = FULL_CFG
    n = cfg["NCORES"]
    bpc = L0_attn_x.shape[0] // n
    if "full" not in _NC_CACHE:
        _NC_CACHE["full"] = build(cfg)
    nc = _NC_CACHE["full"]
    x = np.ascontiguousarray(L0_attn_x)
    c = np.ascontiguousarray(L0_c)
    shared = {
        "wq_in": np.ascontiguousarray(L0_attn_WQ),
        "wk_in": np.ascontiguousarray(L0_attn_WK),
        "wv_in": np.ascontiguousarray(L0_attn_WV),
        "wg_in": np.ascontiguousarray(L0_ffn_w_gate),
        "wu_in": np.ascontiguousarray(L0_ffn_w_up),
        "wd_in": np.ascontiguousarray(L0_ffn_w_down),
        "wmod_in": np.ascontiguousarray(L0_wmod),
    }
    in_maps = []
    for ci in range(n):
        m = dict(shared)
        m["x_in"] = np.ascontiguousarray(x[ci * bpc:(ci + 1) * bpc])
        m["c_in"] = np.ascontiguousarray(c[ci * bpc:(ci + 1) * bpc])
        in_maps.append(m)
    res = run_bass_kernel_spmd(nc, in_maps, core_ids=list(range(n)))
    return np.concatenate([np.asarray(r["out"]) for r in res.results], axis=0)
